# version 16
# baseline (speedup 1.0000x reference)
import numpy as np
import ml_dtypes
import concourse.bass as bass
import concourse.mybir as mybir
from concourse.bass_utils import run_bass_kernel_spmd

F32 = mybir.dt.float32
BF16 = mybir.dt.bfloat16
AF = mybir.ActivationFunctionType
ALU = mybir.AluOpType
AX = mybir.AxisListType
BFDT = ml_dtypes.bfloat16

D = 1024
T = 2048
NT = T // 128
NC = D // 128
H = 16
DH = 64
NP = 8
SEQ = 8192
EPS = 1e-5
ISC = 0.125
FFI = 2816
NFI = FFI // 128
NEXP = 8

QMID = [0.25, 0.75, 1.25, 1.75, 2.5, 3.5, 5.0]
QINC = [0.5, 0.5, 0.5, 0.5, 1.0, 1.0, 2.0]



from contextlib import ExitStack


def _sb(st, nc, name, shape, dtype):
    return st.enter_context(nc.sbuf_tensor(name, shape, dtype))


def _ps(st, nc, name, shape, dtype):
    return st.enter_context(nc.psum_tensor(name, shape, dtype))



def _rope_tables():
    inv = (10000.0 ** (-(np.arange(0, 64, 2, dtype=np.float32)) / 64.0)).astype(np.float32)
    pos = np.arange(SEQ, dtype=np.float32) - 1.0
    ang = pos[None, :] * inv[:, None]
    cos32 = np.cos(ang).astype(np.float32)
    sin32 = np.sin(ang).astype(np.float32)
    cos32[:, 0] = 1.0
    sin32[:, 0] = 0.0
    cos128 = np.tile(cos32, (4, 1))
    sin128 = np.tile(sin32, (4, 1))
    sgn = np.ones((128, 1), np.float32)
    sgn[(np.arange(128) % 64) < 32] = -1.0
    return cos128, sin128 * sgn


def _shard_rows(x):
    out = []
    for c in range(8):
        b, r = c // 4, c % 4
        out.append(np.ascontiguousarray(x[b, r * T:(r + 1) * T, :]))
    return out



def _emit_ln_cols(e_vec, cols, base, c, sv):
    raise NotImplementedError



def emit_segA(nc, pre, io, st, ag1k_hook=None):
    xin = io["xin"]
    wqkv = io["wqkv"]
    gam = io["gam"]
    bet = io["bet"]
    cosd = io["cosd"]
    sind = io["sind"]
    idb = io["idb"]

    qT_o = io["qT_o"]
    kT_o = io["kT_o"]
    v_o = io["v_o"]
    xn_o = io["xn_o"]

    xa = _sb(st, nc, pre + "xa", [128, 2 * D], F32)
    xnt = _sb(st, nc, pre + "xnt", [128, 2 * D], F32)
    xsc = _sb(st, nc, pre + "xsc", [128, D], F32)
    xsc2 = _sb(st, nc, pre + "xsc2", [128, D], F32)
    xb2 = _sb(st, nc, pre + "xb2", [128, 2 * D], BF16)
    dump = _sb(st, nc, pre + "dump", [128, D], F32)
    dump2 = _sb(st, nc, pre + "dump2", [128, D], F32)
    cols = _sb(st, nc, pre + "cols", [128, 32], F32)
    onesr = _sb(st, nc, pre + "onesr", [1, 128], F32)
    epscol = _sb(st, nc, pre + "epscol", [128, 1], F32)
    gbc = _sb(st, nc, pre + "gbc", [128, D], F32)
    bbc = _sb(st, nc, pre + "bbc", [128, D], F32)
    cosT = _sb(st, nc, pre + "cosT", [128, T], F32)
    sinT = _sb(st, nc, pre + "sinT", [128, T], F32)
    identb = _sb(st, nc, pre + "identb", [128, 128], BF16)
    x2T = _sb(st, nc, pre + "x2T", [128, NC * T], BF16)
    wqk = _sb(st, nc, pre + "wqk", [128, NC * 2 * D], BF16)
    wv = _sb(st, nc, pre + "wv", [128, NC * D], BF16)
    rot = _sb(st, nc, pre + "rot", [128, 512], F32)
    rtmp = _sb(st, nc, pre + "rtmp", [128, 512], F32)
    rtmp2 = _sb(st, nc, pre + "rtmp2", [128, 512], F32)
    obuf = _sb(st, nc, pre + "obuf", [128, 2 * 512], BF16)
    vbuf = _sb(st, nc, pre + "vbuf", [128, 4 * 512], BF16)

    psG = _ps(st, nc, pre + "psG", [128, 512], F32)
    psT0 = _ps(st, nc, pre + "psT0", [128, 128], BF16)
    psT1 = _ps(st, nc, pre + "psT1", [128, 128], BF16)
    psQ0 = _ps(st, nc, pre + "psQ0", [128, 512], F32)
    psQ1 = _ps(st, nc, pre + "psQ1", [128, 512], F32)
    psV0 = _ps(st, nc, pre + "psV0", [128, 512], F32)
    psV1 = _ps(st, nc, pre + "psV1", [128, 512], F32)

    wl = nc.alloc_semaphore(pre + "wl")
    xl = [nc.alloc_semaphore(f"{pre}xl{i}") for i in range(2)]
    sxn = [nc.alloc_semaphore(f"{pre}sxn{i}") for i in range(2)]
    sro = [nc.alloc_semaphore(f"{pre}sro{i}") for i in range(2)]
    svo = [nc.alloc_semaphore(f"{pre}svo{i}") for i in range(4)]
    sa = nc.alloc_semaphore(pre + "sa")
    sv = nc.alloc_semaphore(pre + "sv")
    sq = nc.alloc_semaphore(pre + "sq")
    tp = nc.alloc_semaphore(pre + "tp")
    mm = nc.alloc_semaphore(pre + "mm")

    WL_ALL = 16 * (5 + 16)

    with nc.Block() as b:
        @b.gpsimd
        def _(e):
            e.dma_start(gbc[0:1, :], gam[:]).then_inc(wl, 16)
            e.dma_start(bbc[0:1, :], bet[:]).then_inc(wl, 16)
            e.dma_start(cosT[:], cosd[:]).then_inc(wl, 16)
            e.dma_start(sinT[:], sind[:]).then_inc(wl, 16)
            e.dma_start(identb[:], idb[:]).then_inc(wl, 16)
            for kc in range(NC):
                e.dma_start(wqk[:, kc * 2 * D:(kc + 1) * 2 * D],
                            wqkv[kc * 128:(kc + 1) * 128, 0:2 * D]).then_inc(wl, 16)
                e.dma_start(wv[:, kc * D:(kc + 1) * D],
                            wqkv[kc * 128:(kc + 1) * 128, 2 * D:3 * D]).then_inc(wl, 16)
        @b.vector
        def _(e):
            e.memset(epscol[:], EPS)
            e.memset(onesr[:], 1.0).then_inc(sv, 1)
        @b.tensor
        def _(e):
            e.wait_ge(wl, WL_ALL)
            e.wait_ge(sv, 1)
            for j in range(4):
                if j >= 1:
                    e.wait_ge(sv, 1 + j)
                src = (gbc, gbc, bbc, bbc)[j]
                half = j % 2
                e.matmul(psG[:], onesr[:], src[0:1, half * 512:(half + 1) * 512],
                         start=True, stop=True).then_inc(mm, 1)
        @b.vector
        def _(e):
            for j in range(4):
                e.wait_ge(mm, j + 1)
                dst = (gbc, gbc, bbc, bbc)[j]
                half = j % 2
                e.tensor_copy(dst[:, half * 512:(half + 1) * 512], psG[:]).then_inc(sv, 1)

    SVB = 5
    MMB = 4
    VPT = 22

    with nc.Block() as b:
        @b.sync
        def _(e):
            for tt in range(NT):
                if tt >= 2:
                    e.wait_ge(sv, SVB + (tt - 2) * VPT + 6)
                e.dma_start(xa[:, (tt % 2) * D:(tt % 2 + 1) * D],
                            xin[tt * 128:(tt + 1) * 128, :]).then_inc(xl[tt % 2], 16)
        @b.scalar
        def _(e):
            for tt in range(NT):
                par = tt % 2
                o = 16 * par
                xat = xa[:, par * D:(par + 1) * D]
                xn_ = xnt[:, par * D:(par + 1) * D]
                cl = lambda j: cols[:, o + j:o + j + 1]
                e.wait_ge(xl[tt % 2], 16 * (tt // 2 + 1))
                e.activation(dump[:], xat, AF.Copy, accum_out=cl(0)).then_inc(sa, 1)
                e.activation(dump2[:], xat, AF.Square, accum_out=cl(1)).then_inc(sa, 1)
                e.wait_ge(sv, SVB + tt * VPT + 3)
                e.activation(cl(5), cl(4), AF.Sqrt, bias=epscol[:]).then_inc(sq, 1)
                e.wait_ge(sv, SVB + tt * VPT + 6)
                e.activation(dump[:], xn_, AF.Copy, accum_out=cl(8)).then_inc(sa, 1)
                e.activation(dump2[:], xn_, AF.Square, accum_out=cl(9)).then_inc(sa, 1)
                e.wait_ge(sv, SVB + tt * VPT + 9)
                e.activation(cl(13), cl(12), AF.Sqrt, bias=epscol[:]).then_inc(sq, 1)
        @b.vector
        def _(e):
            c = 1.0 / D
            for tt in range(NT):
                par = tt % 2
                o = 16 * par
                xat = xa[:, par * D:(par + 1) * D]
                xn_ = xnt[:, par * D:(par + 1) * D]
                xb2t = xb2[:, par * D:(par + 1) * D]
                cl = lambda j: cols[:, o + j:o + j + 1]
                B0 = SVB + tt * VPT
                e.wait_ge(sa, tt * 4 + 2)
                e.tensor_scalar_mul(cl(2), cl(0), c).then_inc(sv, 1)
                e.wait_ge(sv, B0 + 1)
                e.tensor_mul(cl(3), cl(2), cl(2)).then_inc(sv, 1)
                e.wait_ge(sv, B0 + 2)
                e.scalar_tensor_tensor(cl(4), cl(1), c, cl(3),
                                       ALU.mult, ALU.subtract).then_inc(sv, 1)
                e.wait_ge(sq, tt * 2 + 1)
                e.reciprocal(cl(6), cl(5)).then_inc(sv, 1)
                e.wait_ge(sv, B0 + 4)
                e.scalar_tensor_tensor(cl(7), cl(2), -1.0, cl(6),
                                       ALU.mult, ALU.mult).then_inc(sv, 1)
                if tt >= 2:
                    e.wait_ge(sxn[tt % 2], 16 * (tt // 2))
                e.wait_ge(sv, B0 + 5)
                e.tensor_scalar(xn_, xat, cl(6), cl(7), ALU.mult,
                                ALU.add).then_inc(sv, 1)
                e.wait_ge(sa, tt * 4 + 4)
                e.tensor_scalar_mul(cl(10), cl(8), c).then_inc(sv, 1)
                e.wait_ge(sv, B0 + 7)
                e.tensor_mul(cl(11), cl(10), cl(10)).then_inc(sv, 1)
                e.wait_ge(sv, B0 + 8)
                e.scalar_tensor_tensor(cl(12), cl(9), c, cl(11),
                                       ALU.mult, ALU.subtract).then_inc(sv, 1)
                e.wait_ge(sq, tt * 2 + 2)
                e.reciprocal(cl(14), cl(13)).then_inc(sv, 1)
                e.wait_ge(sv, B0 + 10)
                e.scalar_tensor_tensor(cl(15), cl(10), -1.0, cl(14),
                                       ALU.mult, ALU.mult).then_inc(sv, 1)
                e.wait_ge(sv, B0 + 11)
                e.tensor_scalar(xsc[:], xn_, cl(14), cl(15), ALU.mult,
                                ALU.add).then_inc(sv, 1)
                e.wait_ge(sv, B0 + 12)
                e.tensor_mul(xsc2[:], xsc[:], gbc[:]).then_inc(sv, 1)
                e.wait_ge(sv, B0 + 13)
                e.tensor_add(xb2t, xsc2[:], bbc[:]).then_inc(sv, 1)
                for kc in range(NC):
                    e.wait_ge(tp, tt * NC + kc + 1)
                    ps = (psT0, psT1)[kc % 2]
                    e.tensor_copy(x2T[:, kc * T + tt * 128:kc * T + (tt + 1) * 128],
                                  ps[:]).then_inc(sv, 1)
        @b.gpsimd
        def _(e):
            for tt in range(NT):
                e.wait_ge(sv, SVB + tt * VPT + 6)
                e.dma_start(xn_o[tt * 128:(tt + 1) * 128, :],
                            xnt[:, (tt % 2) * D:(tt % 2 + 1) * D]).then_inc(sxn[tt % 2], 16)
        @b.tensor
        def _(e):
            for tt in range(NT):
                e.wait_ge(sv, SVB + tt * VPT + 14)
                for kc in range(NC):
                    if kc >= 2:
                        e.wait_ge(sv, SVB + tt * VPT + 14 + kc - 1)
                    ps = (psT0, psT1)[kc % 2]
                    e.transpose(ps[:], xb2[:, (tt % 2) * D + kc * 128:(tt % 2) * D + (kc + 1) * 128],
                                identb[:]).then_inc(tp, 1)

    SV1 = SVB + NT * VPT
    MM1 = MMB
    SO1 = NT * 16

    with nc.Block() as b:
        @b.tensor
        def _(e):
            e.wait_ge(sv, SV1)
            n = 0
            for ot in range(16):
                for tc in range(4):
                    ps = (psQ0, psQ1)[n % 2]
                    if n >= 2:
                        e.wait_ge(sv, SV1 + (n - 1) * 7)
                    for kc in range(NC):
                        m = e.matmul(ps[:], wqk[:, kc * 2 * D + ot * 128:kc * 2 * D + (ot + 1) * 128],
                                     x2T[:, kc * T + tc * 512:kc * T + (tc + 1) * 512],
                                     start=(kc == 0), stop=(kc == NC - 1))
                        if kc == NC - 1:
                            m.then_inc(mm, 1)
                    n += 1
        @b.vector
        def _(e):
            n = 0
            for ot in range(16):
                for tc in range(4):
                    par = n % 2
                    ps = (psQ0, psQ1)[par]
                    e.wait_ge(mm, MM1 + n + 1)
                    cs = cosT[:, tc * 512:(tc + 1) * 512]
                    sn = sinT[:, tc * 512:(tc + 1) * 512]
                    e.tensor_copy(rot[0:32, :], ps[32:64, :]).then_inc(sv, 1)
                    e.tensor_copy(rot[32:64, :], ps[0:32, :]).then_inc(sv, 1)
                    e.tensor_copy(rot[64:96, :], ps[96:128, :]).then_inc(sv, 1)
                    e.tensor_copy(rot[96:128, :], ps[64:96, :]).then_inc(sv, 1)
                    e.wait_ge(sv, SV1 + n * 7 + 4)
                    e.tensor_mul(rtmp[:], rot[:], sn).then_inc(sv, 1)
                    e.tensor_mul(rtmp2[:], ps[:], cs).then_inc(sv, 1)
                    if n >= 2:
                        e.wait_ge(sro[n % 2], 16 * (n // 2))
                    e.wait_ge(sv, SV1 + n * 7 + 6)
                    e.tensor_add(obuf[:, par * 512:(par + 1) * 512], rtmp2[:],
                                 rtmp[:]).then_inc(sv, 1)
                    n += 1
        @b.sync
        def _(e):
            n = 0
            for ot in range(16):
                for tc in range(4):
                    e.wait_ge(sv, SV1 + (n + 1) * 7)
                    dst = qT_o if ot < 8 else kT_o
                    orow = (ot % 8) * 128
                    e.dma_start(dst[orow:orow + 128, tc * 512:(tc + 1) * 512],
                                obuf[:, (n % 2) * 512:(n % 2 + 1) * 512]).then_inc(sro[n % 2], 16)
                    n += 1


    if ag1k_hook is not None:
        ag1k_hook(sro)

    SV2 = SV1 + 64 * 7
    MM2 = MM1 + 64
    SO2 = SO1 + 64 * 16

    with nc.Block() as b:
        @b.tensor
        def _(e):
            n = 0
            for tt in range(NT):
                for vc in range(2):
                    ps = (psV0, psV1)[n % 2]
                    if n >= 2:
                        e.wait_ge(sv, SV2 + n - 1)
                    for kc in range(NC):
                        m = e.matmul(ps[:], x2T[:, kc * T + tt * 128:kc * T + (tt + 1) * 128],
                                     wv[:, kc * D + vc * 512:kc * D + (vc + 1) * 512],
                                     start=(kc == 0), stop=(kc == NC - 1))
                        if kc == NC - 1:
                            m.then_inc(mm, 1)
                    n += 1
        @b.vector
        def _(e):
            n = 0
            for tt in range(NT):
                for vc in range(2):
                    e.wait_ge(mm, MM2 + n + 1)
                    if n >= 4:
                        e.wait_ge(svo[n % 4], 16 * (n // 4))
                    ps = (psV0, psV1)[n % 2]
                    e.tensor_copy(vbuf[:, (n % 4) * 512:(n % 4 + 1) * 512],
                                  ps[:]).then_inc(sv, 1)
                    n += 1
        @b.sync
        def _(e):
            n = 0
            for tt in range(NT):
                for vc in range(2):
                    e.wait_ge(sv, SV2 + n + 1)
                    e.dma_start(v_o[tt * 128:(tt + 1) * 128, vc * 512:(vc + 1) * 512],
                                vbuf[:, (n % 4) * 512:(n % 4 + 1) * 512]).then_inc(svo[n % 4], 16)
                    n += 1
            for i in range(2):
                e.wait_ge(sxn[i], 16 * 8)
                e.wait_ge(sro[i], 16 * 32)
            for i in range(4):
                e.wait_ge(svo[i], 16 * 8)
    return


def build_launchA():
    nc = bass.Bass("TRN2", target_bir_lowering=False)
    io = {}
    io["xin"] = nc.dram_tensor("xin", [T, D], F32, kind="ExternalInput").ap()
    io["wqkv"] = nc.dram_tensor("wqkv", [D, 3 * D], F32, kind="ExternalInput").ap()
    io["gam"] = nc.dram_tensor("gam", [1, D], F32, kind="ExternalInput").ap()
    io["bet"] = nc.dram_tensor("bet", [1, D], F32, kind="ExternalInput").ap()
    io["cosd"] = nc.dram_tensor("cosd", [128, T], F32, kind="ExternalInput").ap()
    io["sind"] = nc.dram_tensor("sind", [128, T], F32, kind="ExternalInput").ap()
    io["idb"] = nc.dram_tensor("idb", [128, 128], BF16, kind="ExternalInput").ap()
    io["qT_o"] = nc.dram_tensor("qT_o", [D, T], BF16, kind="ExternalOutput").ap()
    io["kT_o"] = nc.dram_tensor("kT_o", [D, T], BF16, kind="ExternalOutput").ap()
    io["v_o"] = nc.dram_tensor("v_o", [T, D], BF16, kind="ExternalOutput").ap()
    io["xn_o"] = nc.dram_tensor("xn_o", [T, D], F32, kind="ExternalOutput").ap()
    st = ExitStack()
    emit_segA(nc, "A_", io, st)
    st.close()
    return nc





def emit_segB(nc, pre, io, st, gathered=False, ag1k=None, ag1v=None):
    qT_i = io["qT_i"]
    kT_i = io["kT_i"]
    v_i = io["v_i"]
    wout = io["wout"]
    yat_o = io["yat_o"]

    qsb = _sb(st, nc, pre + "qsb", [128, NP * T], BF16)
    kpair = _sb(st, nc, pre + "kpair", [128, 3 * SEQ], BF16)
    vaug = _sb(st, nc, pre + "vaug", [128, 3 * 2 * (64 * 65 + 64)], BF16)
    pbuf = _sb(st, nc, pre + "pbuf", [128, 4 * 1024], BF16)
    absb = _sb(st, nc, pre + "absb", [128, NP * T], BF16)
    wosb = _sb(st, nc, pre + "wosb", [128, NC * D], BF16)
    zrow = _sb(st, nc, pre + "zrow", [1, 1024], F32)
    rz = _sb(st, nc, pre + "rz", [1, 1024], F32)
    rzb = _sb(st, nc, pre + "rzb", [64, 1024], F32)
    ones64 = _sb(st, nc, pre + "ones64", [1, 64], F32)
    ybuf = _sb(st, nc, pre + "ybuf", [128, 2 * 512], F32)

    psS0 = _ps(st, nc, pre + "psS0", [128, 1024], F32)
    psS1 = _ps(st, nc, pre + "psS1", [128, 1024], F32)
    psUA = _ps(st, nc, pre + "psUA", [128, 512], F32)
    psUB = _ps(st, nc, pre + "psUB", [128, 512], F32)
    psBC = _ps(st, nc, pre + "psBC", [64, 1024], F32)

    wl = nc.alloc_semaphore(pre + "wl")
    kl = [nc.alloc_semaphore(f"{pre}kl{i}") for i in range(3)]
    vl = [nc.alloc_semaphore(f"{pre}vl{i}") for i in range(3)]
    syo = [nc.alloc_semaphore(f"{pre}syo{i}") for i in range(2)]
    mmq = nc.alloc_semaphore(pre + "mmq")
    se = nc.alloc_semaphore(pre + "se")
    mmv = nc.alloc_semaphore(pre + "mmv")
    sz = nc.alloc_semaphore(pre + "sz")
    mb = nc.alloc_semaphore(pre + "mb")
    szm = nc.alloc_semaphore(pre + "szm")
    mo = nc.alloc_semaphore(pre + "mo")
    svy = nc.alloc_semaphore(pre + "svy")

    WL_ALL = 16 * 16

    with nc.Block() as b:
        @b.gpsimd
        def _(e):
            for pr in range(NP):
                e.dma_start(qsb[:, pr * T:(pr + 1) * T],
                            qT_i[pr * 128:(pr + 1) * 128, :]).then_inc(wl, 16)
            for kc in range(NC):
                e.dma_start(wosb[:, kc * D:(kc + 1) * D],
                            wout[kc * 128:(kc + 1) * 128, :]).then_inc(wl, 16)
        @b.vector
        def _(e):
            e.memset(ones64[:], 1.0).then_inc(szm, 1)

    VH = 64 * 65 + 64

    with nc.Block() as b:
        @b.gpsimd
        def _(e):
            for pr in range(NP):
                sl3 = pr % 3
                if pr == 0 and ag1k is not None:
                    e.wait_ge(ag1k[0], ag1k[1])
                if pr >= 3:
                    e.wait_ge(mmq, 256 * (pr - 2))
                if gathered:
                    for rk in range(4):
                        e.dma_start(kpair[:, sl3 * SEQ + rk * T:sl3 * SEQ + (rk + 1) * T],
                                    kT_i[rk * D + pr * 128:rk * D + (pr + 1) * 128, :]
                                    ).then_inc(kl[sl3], 16)
                else:
                    e.dma_start(kpair[:, sl3 * SEQ:(sl3 + 1) * SEQ],
                                kT_i[pr * 128:(pr + 1) * 128, :]).then_inc(kl[sl3], 16)
                if pr == 0 and ag1v is not None:
                    e.wait_ge(ag1v[0], ag1v[1])
                e.wait_ge(szm, 1 + 2 * (pr + 1))
                for hh in range(2):
                    h = 2 * pr + hh
                    dst = vaug[:, sl3 * 2 * VH + hh * VH: sl3 * 2 * VH + hh * VH + 64 * 65]
                    dst3 = dst.rearrange("p (kb c) -> p kb c", kb=64)[:, :, 0:64]
                    src = v_i.rearrange("(kb p) (h d) -> p kb h d", p=128, h=16)[:, :, h, :]
                    for q4 in range(4):
                        e.dma_start(dst3[:, q4 * 16:(q4 + 1) * 16, :],
                                    src[:, q4 * 16:(q4 + 1) * 16, :]).then_inc(vl[sl3], 16)
        @b.vector
        def _(e):
            for pr in range(NP):
                sl3 = pr % 3
                if pr >= 3:
                    e.wait_ge(mmv, 256 * (pr - 2))
                e.memset(vaug[:, sl3 * 2 * VH:sl3 * 2 * VH + VH], 1.0).then_inc(szm, 1)
                e.memset(vaug[:, sl3 * 2 * VH + VH:sl3 * 2 * VH + 2 * VH], 1.0).then_inc(szm, 1)
                for qc in range(4):
                    m = pr * 4 + qc
                    e.wait_ge(mmv, 64 * (m + 1))
                    e.wait_ge(sz, 7 * m)
                    e.tensor_copy(zrow[0:1, 0:512], psUA[64:65, :]).then_inc(sz, 1)
                    e.tensor_copy(zrow[0:1, 512:1024], psUB[64:65, :]).then_inc(sz, 1)
                    e.wait_ge(mb, 2 * m)
                    e.wait_ge(sz, 7 * m + 2)
                    e.reciprocal(rz[:], zrow[:]).then_inc(sz, 1)
                    e.wait_ge(mb, 2 * (m + 1))
                    e.tensor_copy(rzb[:, 0:512], psBC[:, 0:512]).then_inc(sz, 1)
                    e.tensor_copy(rzb[:, 512:1024], psBC[:, 512:1024]).then_inc(sz, 1)
                    dstA = absb[0:64, pr * T + qc * 512:pr * T + (qc + 1) * 512]
                    dstB = absb[64:128, pr * T + qc * 512:pr * T + (qc + 1) * 512]
                    e.wait_ge(sz, 7 * m + 5)
                    e.tensor_tensor(dstA, psUA[0:64, :], rzb[:, 0:512],
                                    ALU.mult).then_inc(sz, 1)
                    e.tensor_tensor(dstB, psUB[0:64, :], rzb[:, 512:1024],
                                    ALU.mult).then_inc(sz, 1)
        @b.tensor
        def _(e):
            for pr in range(NP):
                sl3 = pr % 3
                e.wait_ge(kl[sl3], (64 if gathered else 16) * (pr // 3 + 1))
                e.wait_ge(vl[sl3], 128 * (pr // 3 + 1))
                if pr == 0:
                    e.wait_ge(wl, WL_ALL)
                for qc in range(4):
                    m = pr * 4 + qc
                    qmv = qsb[:, pr * T + qc * 512:pr * T + (qc + 1) * 512]
                    for kb in range(64):
                        kp = kb % 2
                        psS = (psS0, psS1)[kp]
                        if kb >= 2:
                            e.wait_ge(se, 64 * m + kb - 1)
                        elif m >= 1:
                            e.wait_ge(se, 64 * m)
                        e.matmul(psS[:, 0:512],
                                 kpair[0:64, sl3 * SEQ + kb * 128:sl3 * SEQ + (kb + 1) * 128],
                                 qmv[0:64, :], start=True, stop=True,
                                 tile_position=(0, 0))
                        e.matmul(psS[:, 512:1024],
                                 kpair[64:128, sl3 * SEQ + kb * 128:sl3 * SEQ + (kb + 1) * 128],
                                 qmv[64:128, :], start=True, stop=True,
                                 tile_position=(64, 0)).then_inc(mmq, 1)
                        if kb >= 1:
                            pkb = kb - 1
                            pkp = pkb % 4
                            e.wait_ge(se, 64 * m + pkb + 1)
                            if pkb == 0:
                                e.wait_ge(sz, 7 * m)
                            e.matmul(psUA[:], vaug[:, sl3 * 2 * VH + pkb * 65:sl3 * 2 * VH + pkb * 65 + 128],
                                     pbuf[:, pkp * 1024:pkp * 1024 + 512],
                                     start=(pkb == 0), stop=False)
                            e.matmul(psUB[:], vaug[:, sl3 * 2 * VH + VH + pkb * 65:sl3 * 2 * VH + VH + pkb * 65 + 128],
                                     pbuf[:, pkp * 1024 + 512:pkp * 1024 + 1024],
                                     start=(pkb == 0), stop=False).then_inc(mmv, 1)
                    pkb, pkp = 63, 63 % 4
                    e.wait_ge(se, 64 * m + 64)
                    e.matmul(psUA[:], vaug[:, sl3 * 2 * VH + pkb * 65:sl3 * 2 * VH + pkb * 65 + 128],
                             pbuf[:, pkp * 1024:pkp * 1024 + 512],
                             start=False, stop=True)
                    e.matmul(psUB[:], vaug[:, sl3 * 2 * VH + VH + pkb * 65:sl3 * 2 * VH + VH + pkb * 65 + 128],
                             pbuf[:, pkp * 1024 + 512:pkp * 1024 + 1024],
                             start=False, stop=True).then_inc(mmv, 1)
                    e.wait_ge(sz, 7 * m + 3)
                    e.matmul(psBC[:, 0:512], ones64[:], rz[0:1, 0:512],
                             start=True, stop=True)
                    e.matmul(psBC[:, 512:1024], ones64[:], rz[0:1, 512:1024],
                             start=True, stop=True).then_inc(mb, 2)
        @b.scalar
        def _(e):
            for pr in range(NP):
                for qc in range(4):
                    m = pr * 4 + qc
                    for kb in range(64):
                        kp = kb % 2
                        kp4 = kb % 4
                        psS = (psS0, psS1)[kp]
                        e.wait_ge(mmq, 64 * m + kb + 1)
                        if kb >= 4:
                            e.wait_ge(mmv, 64 * m + kb - 3)
                        elif m >= 1:
                            e.wait_ge(mmv, 64 * m)
                        e.activation(pbuf[:, kp4 * 1024:(kp4 + 1) * 1024], psS[:],
                                     AF.Exp, scale=ISC).then_inc(se, 1)

    SZ_ALL = 7 * 32 + 1 + 16
    with nc.Block() as b:
        @b.tensor
        def _(e):
            e.wait_ge(sz, 7 * 32)
            for j in range(32):
                tt, half = j // 2, j % 2
                psO = (psS0, psS1)[j % 2]
                if j >= 2:
                    e.wait_ge(svy, j - 1)
                for kc in range(NC):
                    mmi = e.matmul(psO[:, 0:512],
                                   absb[:, kc * T + tt * 128:kc * T + (tt + 1) * 128],
                                   wosb[:, kc * D + half * 512:kc * D + (half + 1) * 512],
                                   start=(kc == 0), stop=(kc == NC - 1))
                    if kc == NC - 1:
                        mmi.then_inc(mo, 1)
        @b.vector
        def _(e):
            for j in range(32):
                e.wait_ge(mo, j + 1)
                if j >= 2:
                    e.wait_ge(syo[j % 2], 16 * (j // 2))
                psO = (psS0, psS1)[j % 2]
                e.tensor_copy(ybuf[:, (j % 2) * 512:(j % 2 + 1) * 512],
                              psO[:, 0:512]).then_inc(svy, 1)
        @b.sync
        def _(e):
            for j in range(32):
                tt, half = j // 2, j % 2
                e.wait_ge(svy, j + 1)
                e.dma_start(yat_o[tt * 128:(tt + 1) * 128, half * 512:(half + 1) * 512],
                            ybuf[:, (j % 2) * 512:(j % 2 + 1) * 512]).then_inc(syo[j % 2], 16)
            for i in range(2):
                e.wait_ge(syo[i], 16 * 16)
    return


def build_launchB():
    nc = bass.Bass("TRN2", target_bir_lowering=False)
    io = {}
    io["qT_i"] = nc.dram_tensor("qT_i", [D, T], BF16, kind="ExternalInput").ap()
    io["kT_i"] = nc.dram_tensor("kT_i", [D, SEQ], BF16, kind="ExternalInput").ap()
    io["v_i"] = nc.dram_tensor("v_i", [SEQ, D], BF16, kind="ExternalInput").ap()
    io["wout"] = nc.dram_tensor("wout", [D, D], F32, kind="ExternalInput").ap()
    io["yat_o"] = nc.dram_tensor("yat_o", [T, D], F32, kind="ExternalOutput").ap()
    st = ExitStack()
    emit_segB(nc, "B_", io, st)
    st.close()
    return nc


def _launchB(resA, inputs):
    qTs = [np.asarray(resA.results[c]["qT_o"]) for c in range(8)]
    kTs = [np.asarray(resA.results[c]["kT_o"]) for c in range(8)]
    vs = [np.asarray(resA.results[c]["v_o"]) for c in range(8)]
    ncB = build_launchB()
    in_maps = []
    for c in range(8):
        b = c // 4
        grp = range(b * 4, b * 4 + 4)
        kT_full = np.concatenate([kTs[g] for g in grp], axis=1)
        v_full = np.concatenate([vs[g] for g in grp], axis=0)
        in_maps.append({
            "qT_i": qTs[c],
            "kT_i": np.ascontiguousarray(kT_full),
            "v_i": np.ascontiguousarray(v_full),
            "wout": np.asarray(inputs["L0_attn_w_out"], np.float32),
        })
    return run_bass_kernel_spmd(ncB, in_maps, core_ids=list(range(8)))





def _emit_quant_scalar(e, sa, qa, qabs, qsg, W):
    e.activation(qabs[:, 0:W], qa[:, 0:W], AF.Abs).then_inc(sa, 1)
    e.activation(qsg[:, 0:W], qa[:, 0:W], AF.Sign).then_inc(sa, 1)


def _emit_quant_vector(e, vq, base, qa, qabs, qsg, qtmp, l2t, l3t, qmc, qsc, qrc,
                       qacc, qacc2, qout, W):
    BL = W // 16
    n = base

    def step(inst):
        nonlocal n
        inst.then_inc(vq, 1)
        n += 1

    def w(x=None):
        e.wait_ge(vq, n if x is None else x)

    v3 = lambda t, wdt, inner: t[:, 0:wdt].rearrange("p (b i) -> p b i", i=inner)
    a3 = v3(qabs, W, 16)
    step(e.tensor_tensor(v3(qtmp, W // 2, 8), a3[:, :, 0:16:2], a3[:, :, 1:16:2], ALU.max))
    w()
    t3 = v3(qtmp, W // 2, 8)
    step(e.tensor_tensor(v3(l2t, W // 4, 4), t3[:, :, 0:8:2], t3[:, :, 1:8:2], ALU.max))
    w()
    u3 = v3(l2t, W // 4, 4)
    step(e.tensor_tensor(v3(l3t, W // 8, 2), u3[:, :, 0:4:2], u3[:, :, 1:4:2], ALU.max))
    w()
    s3 = v3(l3t, W // 8, 2)
    step(e.tensor_tensor(qmc[:, 0:BL].rearrange("p (b i) -> p b i", i=1),
                         s3[:, :, 0:2:2], s3[:, :, 1:2:2], ALU.max))
    w()
    step(e.tensor_scalar(qsc[:, 0:BL], qmc[:, 0:BL], 1.0 / 6.0, 1e-12, ALU.mult, ALU.max))
    w()
    step(e.reciprocal(qrc[:, 0:BL], qsc[:, 0:BL]))
    w()
    rb = qrc[:, 0:BL].unsqueeze(2).broadcast_to((128, BL, 16))
    step(e.tensor_tensor(v3(qa, W, 16), a3, rb, ALU.mult))
    w()
    step(e.tensor_single_scalar(qacc[:, 0:W], qa[:, 0:W], QMID[0], ALU.is_gt))
    cur, oth = qacc, qacc2
    for k in (1, 2, 3):
        w()
        step(e.scalar_tensor_tensor(oth[:, 0:W], qa[:, 0:W], QMID[k], cur[:, 0:W],
                                    ALU.is_gt, ALU.add))
        cur, oth = oth, cur
    step(e.tensor_single_scalar(qtmp[:, 0:W], qa[:, 0:W], QMID[4], ALU.is_gt))
    w()
    step(e.scalar_tensor_tensor(qtmp[:, 0:W], qa[:, 0:W], QMID[5], qtmp[:, 0:W],
                                ALU.is_gt, ALU.add))
    step(e.tensor_single_scalar(qabs[:, 0:W], qa[:, 0:W], QMID[6], ALU.is_gt))
    w()
    step(e.scalar_tensor_tensor(oth[:, 0:W], cur[:, 0:W], 0.5, qtmp[:, 0:W],
                                ALU.mult, ALU.add))
    cur, oth = oth, cur
    w()
    step(e.scalar_tensor_tensor(oth[:, 0:W], qabs[:, 0:W], 2.0, cur[:, 0:W],
                                ALU.mult, ALU.add))
    cur, oth = oth, cur
    w()
    sb = qsc[:, 0:BL].unsqueeze(2).broadcast_to((128, BL, 16))
    step(e.tensor_tensor(v3(qtmp, W, 16), v3(qsg, W, 16), sb, ALU.mult))
    w()
    step(e.tensor_tensor(qout[:, 0:W], cur[:, 0:W], qtmp[:, 0:W], ALU.mult))
    return n - base


VQ = 18



def emit_segC0(nc, pre, io, st):
    NJ = 7
    W = FFI
    wtiles = io["wtiles"]
    idb = io["idb"]
    wq_o = io["wq_o"]

    qa = _sb(st, nc, pre + "qa", [128, W], F32)
    qabs = _sb(st, nc, pre + "qabs", [128, W], F32)
    qsg = _sb(st, nc, pre + "qsg", [128, W], F32)
    qtmp = _sb(st, nc, pre + "qtmp", [128, W], F32)
    l2t = _sb(st, nc, pre + "l2t", [128, W // 4], F32)
    l3t = _sb(st, nc, pre + "l3t", [128, W // 8], F32)
    qmc = _sb(st, nc, pre + "qmc", [128, W // 16], F32)
    qsc = _sb(st, nc, pre + "qsc", [128, W // 16], F32)
    qrc = _sb(st, nc, pre + "qrc", [128, W // 16], F32)
    qacc = _sb(st, nc, pre + "qacc", [128, W], F32)
    qacc2 = _sb(st, nc, pre + "qacc2", [128, W], F32)
    qout = _sb(st, nc, pre + "qout", [128, W], BF16)
    identb = _sb(st, nc, pre + "identb", [128, 128], BF16)
    obuf = _sb(st, nc, pre + "obuf", [128, NFI * 128], BF16)

    psT0 = _ps(st, nc, pre + "psT0", [128, 128], BF16)
    psT1 = _ps(st, nc, pre + "psT1", [128, 128], BF16)

    wl = nc.alloc_semaphore(pre + "wl")
    ql = nc.alloc_semaphore(pre + "ql")
    sa = nc.alloc_semaphore(pre + "sa")
    vq = nc.alloc_semaphore(pre + "vq")
    sve = nc.alloc_semaphore(pre + "sve")
    tp = nc.alloc_semaphore(pre + "tp")
    sow = nc.alloc_semaphore(pre + "sow")

    with nc.Block() as b:
        @b.gpsimd
        def _(e):
            e.dma_start(identb[:], idb[:]).then_inc(wl, 16)
        @b.sync
        def _(e):
            for j in range(NJ):
                if j >= 1:
                    e.wait_ge(vq, j * VQ - 2)
                e.dma_start(qa[:], wtiles[j]).then_inc(ql, 16)
        @b.scalar
        def _(e):
            for j in range(NJ):
                e.wait_ge(ql, 16 * (j + 1))
                if j >= 1:
                    e.wait_ge(vq, j * VQ)
                _emit_quant_scalar(e, sa, qa, qabs, qsg, W)
        @b.vector
        def _(e):
            for j in range(NJ):
                e.wait_ge(sa, 2 * (j + 1))
                _emit_quant_vector(e, vq, j * VQ, qa, qabs, qsg, qtmp, l2t, l3t,
                                   qmc, qsc, qrc, qacc, qacc2, qout, W)
                for kc in range(NFI):
                    e.wait_ge(tp, j * NFI + kc + 1)
                    if kc == 0:
                        e.wait_ge(sow, 16 * j)
                    ps = (psT0, psT1)[kc % 2]
                    e.tensor_copy(obuf[:, kc * 128:(kc + 1) * 128], ps[:]).then_inc(sve, 1)
        @b.tensor
        def _(e):
            e.wait_ge(wl, 16)
            for j in range(NJ):
                e.wait_ge(vq, (j + 1) * VQ)
                for kc in range(NFI):
                    if kc >= 2:
                        e.wait_ge(sve, j * NFI + kc - 1)
                    elif j >= 1:
                        e.wait_ge(sve, j * NFI)
                    ps = (psT0, psT1)[kc % 2]
                    e.transpose(ps[:], qout[:, kc * 128:(kc + 1) * 128],
                                identb[:]).then_inc(tp, 1)
        @b.gpsimd
        def _(e):
            for j in range(NJ):
                e.wait_ge(sve, (j + 1) * NFI)
                e.dma_start(wq_o[j].rearrange("(kc p) c -> p kc c", p=128),
                            obuf[:].rearrange("p (kc c) -> p kc c", c=128)).then_inc(sow, 16)
            e.wait_ge(sow, 16 * NJ)
    return


def build_launchC0():
    nc = bass.Bass("TRN2", target_bir_lowering=False)
    io = {}
    io["wtiles"] = nc.dram_tensor("wtiles", [7, 128, FFI], F32, kind="ExternalInput").ap()
    io["idb"] = nc.dram_tensor("idb", [128, 128], BF16, kind="ExternalInput").ap()
    io["wq_o"] = nc.dram_tensor("wq_o", [7, FFI, 128], BF16, kind="ExternalOutput").ap()
    st = ExitStack()
    emit_segC0(nc, "C0_", io, st)
    st.close()
    return nc


def _launchC0(inputs):
    wg = np.asarray(inputs["L0_ffn_w_gate"], np.float32)
    wu = np.asarray(inputs["L0_ffn_w_up"], np.float32)
    wd = np.asarray(inputs["L0_ffn_w_down"], np.float32)
    jobs = []
    for w in (wg, wu):
        for t in range(NFI):
            tile = np.zeros((128, FFI), np.float32)
            tile[:, 0:D] = w[t * 128:(t + 1) * 128, :]
            jobs.append(tile)
    for t in range(NC):
        jobs.append(np.ascontiguousarray(wd[t * 128:(t + 1) * 128, :]))
    while len(jobs) % 8 != 0 or len(jobs) < 56:
        jobs.append(np.zeros((128, FFI), np.float32))
    NJ = len(jobs) // 8
    ident_bf = np.eye(128, dtype=np.float32).astype(BFDT)
    ncC0 = build_launchC0()
    in_maps = [{"wtiles": np.stack(jobs[c * NJ:(c + 1) * NJ]), "idb": ident_bf}
               for c in range(8)]
    res = run_bass_kernel_spmd(ncC0, in_maps, core_ids=list(range(8)))
    outs = np.concatenate([np.asarray(res.results[c]["wq_o"]) for c in range(8)], axis=0)
    wgT = np.concatenate([outs[t][0:D, :] for t in range(NFI)], axis=1)
    wuT = np.concatenate([outs[NFI + t][0:D, :] for t in range(NFI)], axis=1)
    wdT = np.concatenate([outs[2 * NFI + t] for t in range(NC)], axis=1)
    return wgT, wuT, wdT





def emit_segCa(nc, pre, io, st, from_jobs=False):
    xn_i = io["xn_i"]
    wgT_i = io.get("wgT_i")
    wuT_i = io.get("wuT_i")
    idb = io["idb"]
    h_o = io["h_o"]

    W = D
    qa = _sb(st, nc, pre + "qa", [128, W], F32)
    qabs = _sb(st, nc, pre + "qabs", [128, W], F32)
    qsg = _sb(st, nc, pre + "qsg", [128, W], F32)
    qtmp = _sb(st, nc, pre + "qtmp", [128, W], F32)
    l2t = _sb(st, nc, pre + "l2t", [128, W // 4], F32)
    l3t = _sb(st, nc, pre + "l3t", [128, W // 8], F32)
    qmc = _sb(st, nc, pre + "qmc", [128, W // 16], F32)
    qsc = _sb(st, nc, pre + "qsc", [128, W // 16], F32)
    qrc = _sb(st, nc, pre + "qrc", [128, W // 16], F32)
    qacc = _sb(st, nc, pre + "qacc", [128, W], F32)
    qacc2 = _sb(st, nc, pre + "qacc2", [128, W], F32)
    qout = _sb(st, nc, pre + "qout", [128, W], BF16)
    identb = _sb(st, nc, pre + "identb", [128, 128], BF16)
    xqT = _sb(st, nc, pre + "xqT", [128, NC * T], BF16)
    wgsb = _sb(st, nc, pre + "wgsb", [128, NC * FFI], BF16)
    wusb = _sb(st, nc, pre + "wusb", [128, NC * FFI], BF16)
    hsil = _sb(st, nc, pre + "hsil", [128, 2 * 512], F32)
    hbuf = _sb(st, nc, pre + "hbuf", [128, 2 * 512], F32)

    psT0 = _ps(st, nc, pre + "psT0", [128, 128], BF16)
    psT1 = _ps(st, nc, pre + "psT1", [128, 128], BF16)
    psG0 = _ps(st, nc, pre + "psG0", [128, 512], F32)
    psG1 = _ps(st, nc, pre + "psG1", [128, 512], F32)
    psU0 = _ps(st, nc, pre + "psU0", [128, 512], F32)
    psU1 = _ps(st, nc, pre + "psU1", [128, 512], F32)

    wl = nc.alloc_semaphore(pre + "wl")
    il = nc.alloc_semaphore(pre + "il")
    ql = nc.alloc_semaphore(pre + "ql")
    sa = nc.alloc_semaphore(pre + "sa")
    vq = nc.alloc_semaphore(pre + "vq")
    sve = nc.alloc_semaphore(pre + "sve")
    tp = nc.alloc_semaphore(pre + "tp")
    mmg = nc.alloc_semaphore(pre + "mmg")
    sil = nc.alloc_semaphore(pre + "sil")
    sv = nc.alloc_semaphore(pre + "sv")
    sho = [nc.alloc_semaphore(f"{pre}sho{i}") for i in range(2)]

    WL_ALL = 16 * (2 * NC)
    FCW = [512] * 5 + [256]
    FCO = [0, 512, 1024, 1536, 2048, 2560]

    with nc.Block() as b:
        @b.gpsimd
        def _(e):
            e.dma_start(identb[:], idb[:]).then_inc(il, 16)
            for kc in range(NC):
                if from_jobs:
                    wj = io["wjobs"]
                    sg = bass.AP(wj.tensor, wj.offset + kc * 128 * 128,
                                 [[128, 128], [FFI * 128, NFI], [1, 128]])
                    su = bass.AP(wj.tensor, wj.offset + NFI * FFI * 128 + kc * 128 * 128,
                                 [[128, 128], [FFI * 128, NFI], [1, 128]])
                    e.dma_start(wgsb[:, kc * FFI:(kc + 1) * FFI]
                                .rearrange("p (t c) -> p t c", c=128), sg).then_inc(wl, 16)
                    e.dma_start(wusb[:, kc * FFI:(kc + 1) * FFI]
                                .rearrange("p (t c) -> p t c", c=128), su).then_inc(wl, 16)
                else:
                    e.dma_start(wgsb[:, kc * FFI:(kc + 1) * FFI],
                                wgT_i[kc * 128:(kc + 1) * 128, :]).then_inc(wl, 16)
                    e.dma_start(wusb[:, kc * FFI:(kc + 1) * FFI],
                                wuT_i[kc * 128:(kc + 1) * 128, :]).then_inc(wl, 16)
        @b.sync
        def _(e):
            for tt in range(NT):
                if tt >= 1:
                    e.wait_ge(vq, tt * VQ - 2)
                e.dma_start(qa[:], xn_i[tt * 128:(tt + 1) * 128, :]).then_inc(ql, 16)
        @b.scalar
        def _(e):
            for tt in range(NT):
                e.wait_ge(ql, 16 * (tt + 1))
                if tt >= 1:
                    e.wait_ge(vq, tt * VQ)
                _emit_quant_scalar(e, sa, qa, qabs, qsg, W)
        @b.vector
        def _(e):
            for tt in range(NT):
                e.wait_ge(sa, 2 * (tt + 1))
                _emit_quant_vector(e, vq, tt * VQ, qa, qabs, qsg, qtmp, l2t, l3t,
                                   qmc, qsc, qrc, qacc, qacc2, qout, W)
                for kc in range(NC):
                    e.wait_ge(tp, tt * NC + kc + 1)
                    ps = (psT0, psT1)[kc % 2]
                    e.tensor_copy(xqT[:, kc * T + tt * 128:kc * T + (tt + 1) * 128],
                                  ps[:]).then_inc(sve, 1)
        @b.tensor
        def _(e):
            e.wait_ge(il, 16)
            for tt in range(NT):
                e.wait_ge(vq, (tt + 1) * VQ)
                for kc in range(NC):
                    if kc >= 2:
                        e.wait_ge(sve, tt * NC + kc - 1)
                    elif tt >= 1:
                        e.wait_ge(sve, tt * NC)
                    ps = (psT0, psT1)[kc % 2]
                    e.transpose(ps[:], qout[:, kc * 128:(kc + 1) * 128],
                                identb[:]).then_inc(tp, 1)
        @b.tensor
        def _(e):
            e.wait_ge(wl, WL_ALL)
            j = 0
            for tt in range(NT):
                e.wait_ge(sve, (tt + 1) * NC)
                for fc in range(6):
                    par = j % 2
                    psg = (psG0, psG1)[par]
                    psu = (psU0, psU1)[par]
                    if j >= 2:
                        e.wait_ge(sil, j - 1)
                        e.wait_ge(sv, j - 1)
                    wd = FCW[fc]
                    for kc in range(NC):
                        e.matmul(psg[:, 0:wd], xqT[:, kc * T + tt * 128:kc * T + (tt + 1) * 128],
                                 wgsb[:, kc * FFI + FCO[fc]:kc * FFI + FCO[fc] + wd],
                                 start=(kc == 0), stop=(kc == NC - 1))
                        mmi = e.matmul(psu[:, 0:wd], xqT[:, kc * T + tt * 128:kc * T + (tt + 1) * 128],
                                       wusb[:, kc * FFI + FCO[fc]:kc * FFI + FCO[fc] + wd],
                                       start=(kc == 0), stop=(kc == NC - 1))
                        if kc == NC - 1:
                            mmi.then_inc(mmg, 1)
                    j += 1
        @b.scalar
        def _(e):
            j = 0
            for tt in range(NT):
                for fc in range(6):
                    par = j % 2
                    psg = (psG0, psG1)[par]
                    e.wait_ge(mmg, j + 1)
                    if j >= 2:
                        e.wait_ge(sv, j - 1)
                    e.activation(hsil[:, par * 512:par * 512 + FCW[fc]],
                                 psg[:, 0:FCW[fc]], AF.Silu).then_inc(sil, 1)
                    j += 1
        @b.vector
        def _(e):
            j = 0
            for tt in range(NT):
                for fc in range(6):
                    par = j % 2
                    psu = (psU0, psU1)[par]
                    e.wait_ge(sil, j + 1)
                    if j >= 2:
                        e.wait_ge(sho[par], 16 * (j // 2))
                    e.tensor_tensor(hbuf[:, par * 512:par * 512 + FCW[fc]],
                                    hsil[:, par * 512:par * 512 + FCW[fc]],
                                    psu[:, 0:FCW[fc]], ALU.mult).then_inc(sv, 1)
                    j += 1
        @b.gpsimd
        def _(e):
            j = 0
            for tt in range(NT):
                for fc in range(6):
                    par = j % 2
                    e.wait_ge(sv, j + 1)
                    e.dma_start(h_o[tt * 128:(tt + 1) * 128, FCO[fc]:FCO[fc] + FCW[fc]],
                                hbuf[:, par * 512:par * 512 + FCW[fc]]).then_inc(sho[par], 16)
                    j += 1
            for i in range(2):
                e.wait_ge(sho[i], 16 * 48)
    return


def build_launchC1a():
    nc = bass.Bass("TRN2", target_bir_lowering=False)
    io = {}
    io["xn_i"] = nc.dram_tensor("xn_i", [T, D], F32, kind="ExternalInput").ap()
    io["wgT_i"] = nc.dram_tensor("wgT_i", [D, FFI], BF16, kind="ExternalInput").ap()
    io["wuT_i"] = nc.dram_tensor("wuT_i", [D, FFI], BF16, kind="ExternalInput").ap()
    io["idb"] = nc.dram_tensor("idb", [128, 128], BF16, kind="ExternalInput").ap()
    io["h_o"] = nc.dram_tensor("h_o", [T, FFI], F32, kind="ExternalOutput").ap()
    st = ExitStack()
    emit_segCa(nc, "Ca_", io, st)
    st.close()
    return nc



def emit_segCb(nc, pre, io, st, from_jobs=False):
    h_i = io["h_i"]
    wdT_i = io.get("wdT_i")
    xin = io["xin"]
    yat_i = io["yat_i"]
    idb = io["idb"]
    x1_o = io["x1_o"]

    W = FFI
    qa = _sb(st, nc, pre + "qa", [128, W], F32)
    qabs = _sb(st, nc, pre + "qabs", [128, W], F32)
    qsg = _sb(st, nc, pre + "qsg", [128, W], F32)
    qtmp = _sb(st, nc, pre + "qtmp", [128, W], F32)
    l2t = _sb(st, nc, pre + "l2t", [128, W // 4], F32)
    l3t = _sb(st, nc, pre + "l3t", [128, W // 8], F32)
    qmc = _sb(st, nc, pre + "qmc", [128, W // 16], F32)
    qsc = _sb(st, nc, pre + "qsc", [128, W // 16], F32)
    qrc = _sb(st, nc, pre + "qrc", [128, W // 16], F32)
    qacc = _sb(st, nc, pre + "qacc", [128, W], F32)
    qacc2 = _sb(st, nc, pre + "qacc2", [128, W], F32)
    qout = _sb(st, nc, pre + "qout", [128, W], BF16)
    identb = _sb(st, nc, pre + "identb", [128, 128], BF16)
    hqT = _sb(st, nc, pre + "hqT", [128, NFI * 128], BF16)
    wdsb = _sb(st, nc, pre + "wdsb", [128, NFI * D], BF16)
    xres = _sb(st, nc, pre + "xres", [128, 2 * D], F32)
    yres = _sb(st, nc, pre + "yres", [128, 2 * D], F32)
    tadd = _sb(st, nc, pre + "tadd", [128, 512], F32)
    obuf = _sb(st, nc, pre + "obuf", [128, 2 * 1024], F32)

    psT0 = _ps(st, nc, pre + "psT0", [128, 128], BF16)
    psT1 = _ps(st, nc, pre + "psT1", [128, 128], BF16)
    psD0 = _ps(st, nc, pre + "psD0", [128, 512], F32)
    psD1 = _ps(st, nc, pre + "psD1", [128, 512], F32)

    wl = nc.alloc_semaphore(pre + "wl")
    il = nc.alloc_semaphore(pre + "il")
    ql = nc.alloc_semaphore(pre + "ql")
    rl = [nc.alloc_semaphore(f"{pre}rl{i}") for i in range(2)]
    sa = nc.alloc_semaphore(pre + "sa")
    vq = nc.alloc_semaphore(pre + "vq")
    sve = nc.alloc_semaphore(pre + "sve")
    tp = nc.alloc_semaphore(pre + "tp")
    mmd = nc.alloc_semaphore(pre + "mmd")
    sv = nc.alloc_semaphore(pre + "sv")
    sxo = [nc.alloc_semaphore(f"{pre}sxo{i}") for i in range(2)]

    WL_ALL = 16 * NFI
    VE = VQ + NFI + 4

    with nc.Block() as b:
        @b.gpsimd
        def _(e):
            e.dma_start(identb[:], idb[:]).then_inc(il, 16)
            for fc in range(NFI):
                if from_jobs:
                    wj = io["wjobs"]
                    sd = bass.AP(wj.tensor, wj.offset + 2 * NFI * FFI * 128 + fc * 128 * 128,
                                 [[128, 128], [FFI * 128, NC], [1, 128]])
                    e.dma_start(wdsb[:, fc * D:(fc + 1) * D]
                                .rearrange("p (t c) -> p t c", c=128), sd).then_inc(wl, 16)
                else:
                    e.dma_start(wdsb[:, fc * D:(fc + 1) * D],
                                wdT_i[fc * 128:(fc + 1) * 128, :]).then_inc(wl, 16)
        @b.sync
        def _(e):
            for tt in range(NT):
                if tt >= 1:
                    e.wait_ge(sv, (tt - 1) * VE + 21)
                e.dma_start(qa[:], h_i[tt * 128:(tt + 1) * 128, :]).then_inc(ql, 16)
        @b.gpsimd
        def _(e):
            for tt in range(NT + 1):
                par = tt % 2
                if tt < NT:
                    if tt >= 2:
                        e.wait_ge(sv, (tt - 1) * VE)
                    e.dma_start(xres[:, par * D:(par + 1) * D],
                                xin[tt * 128:(tt + 1) * 128, :]).then_inc(rl[par], 16)
                    e.dma_start(yres[:, par * D:(par + 1) * D],
                                yat_i[tt * 128:(tt + 1) * 128, :]).then_inc(rl[par], 16)
                if tt >= 1:
                    st = tt - 1
                    e.wait_ge(sv, (st + 1) * VE)
                    e.dma_start(x1_o[st * 128:(st + 1) * 128, :],
                                obuf[:, (st % 2) * 1024:(st % 2) * 1024 + 1024]).then_inc(sxo[st % 2], 16)
            for i in range(2):
                e.wait_ge(sxo[i], 16 * 8)
        @b.scalar
        def _(e):
            for tt in range(NT):
                e.wait_ge(ql, 16 * (tt + 1))
                if tt >= 1:
                    e.wait_ge(sv, tt * VE)
                _emit_quant_scalar(e, sa, qa, qabs, qsg, W)
        @b.vector
        def _(e):
            for tt in range(NT):
                par = tt % 2
                e.wait_ge(sa, 2 * (tt + 1))
                _emit_quant_vector(e, sv, tt * VE, qa, qabs, qsg, qtmp, l2t, l3t,
                                   qmc, qsc, qrc, qacc, qacc2, qout, W)
                nv = tt * VE + VQ
                for fc in range(NFI):
                    e.wait_ge(tp, tt * NFI + fc + 1)
                    ps = (psT0, psT1)[fc % 2]
                    e.tensor_copy(hqT[:, fc * 128:(fc + 1) * 128], ps[:]).then_inc(sv, 1)
                    nv += 1
                e.wait_ge(rl[par], 32 * (tt // 2 + 1))
                for half in range(2):
                    psd = (psD0, psD1)[half]
                    e.wait_ge(mmd, tt * 2 + half + 1)
                    if tt >= 2:
                        e.wait_ge(sxo[par], 16 * (tt // 2))
                    e.wait_ge(sv, nv)
                    e.tensor_tensor(tadd[:], psd[:],
                                    xres[:, par * D + half * 512:par * D + half * 512 + 512],
                                    ALU.add).then_inc(sv, 1)
                    nv += 1
                    e.wait_ge(sv, nv)
                    e.tensor_tensor(obuf[:, par * 1024 + half * 512:par * 1024 + (half + 1) * 512],
                                    tadd[:],
                                    yres[:, par * D + half * 512:par * D + half * 512 + 512],
                                    ALU.add).then_inc(sv, 1)
                    nv += 1
        @b.tensor
        def _(e):
            e.wait_ge(il, 16)
            e.wait_ge(wl, WL_ALL)
            for tt in range(NT):
                e.wait_ge(sv, tt * VE + VQ)
                for fc in range(NFI):
                    if fc >= 2:
                        e.wait_ge(sv, tt * VE + VQ + fc - 1)
                    elif tt >= 1:
                        e.wait_ge(sv, tt * VE)
                    ps = (psT0, psT1)[fc % 2]
                    e.transpose(ps[:], qout[:, fc * 128:(fc + 1) * 128],
                                identb[:]).then_inc(tp, 1)
                e.wait_ge(sv, tt * VE + VQ + NFI)
                for half in range(2):
                    psd = (psD0, psD1)[half]
                    for fc in range(NFI):
                        mmi = e.matmul(psd[:], hqT[:, fc * 128:(fc + 1) * 128],
                                       wdsb[:, fc * D + half * 512:fc * D + (half + 1) * 512],
                                       start=(fc == 0), stop=(fc == NFI - 1))
                        if fc == NFI - 1:
                            mmi.then_inc(mmd, 1)
    return


def build_launchC1b():
    nc = bass.Bass("TRN2", target_bir_lowering=False)
    io = {}
    io["h_i"] = nc.dram_tensor("h_i", [T, FFI], F32, kind="ExternalInput").ap()
    io["wdT_i"] = nc.dram_tensor("wdT_i", [FFI, D], BF16, kind="ExternalInput").ap()
    io["xin"] = nc.dram_tensor("xin", [T, D], F32, kind="ExternalInput").ap()
    io["yat_i"] = nc.dram_tensor("yat_i", [T, D], F32, kind="ExternalInput").ap()
    io["idb"] = nc.dram_tensor("idb", [128, 128], BF16, kind="ExternalInput").ap()
    io["x1_o"] = nc.dram_tensor("x1_o", [T, D], F32, kind="ExternalOutput").ap()
    st = ExitStack()
    emit_segCb(nc, "Cb_", io, st)
    st.close()
    return nc


def _launchC(inputs, resA, resB, wgT, wuT, wdT):
    ident_bf = np.eye(128, dtype=np.float32).astype(BFDT)
    x = np.asarray(inputs["L0_attn_x"], np.float32)
    xs = _shard_rows(x)
    ncA = build_launchC1a()
    in_maps = []
    for c in range(8):
        in_maps.append({
            "xn_i": np.asarray(resA.results[c]["xn_o"]),
            "wgT_i": wgT, "wuT_i": wuT, "idb": ident_bf,
        })
    res1a = run_bass_kernel_spmd(ncA, in_maps, core_ids=list(range(8)))
    ncB = build_launchC1b()
    in_maps = []
    for c in range(8):
        in_maps.append({
            "h_i": np.asarray(res1a.results[c]["h_o"]),
            "wdT_i": wdT,
            "xin": xs[c],
            "yat_i": np.asarray(resB.results[c]["yat_o"]),
            "idb": ident_bf,
        })
    return run_bass_kernel_spmd(ncB, in_maps, core_ids=list(range(8))), res1a





def emit_segDd(nc, pre, io, st, ag2_hook=None):
    x1_i = io["x1_i"]
    Wq_i = io["Wq_i"]
    Wk_i = io["Wk_i"]
    Wv_i = io["Wv_i"]
    bq_i = io["bq_i"]
    bk_i = io["bk_i"]
    bv_i = io["bv_i"]
    idb = io["idb"]
    qt_o = io["qt_o"]
    kt_o = io["kt_o"]
    vt_o = io["vt_o"]
    qTf_o = io["qTf_o"]
    kTf_o = io["kTf_o"]
    xn1T_o = io["xn1T_o"]

    xa = _sb(st, nc, pre + "xa", [128, 2 * D], F32)
    xnb = _sb(st, nc, pre + "xnb", [128, 2 * D], BF16)
    dump = _sb(st, nc, pre + "dump", [128, D], F32)
    dump2 = _sb(st, nc, pre + "dump2", [128, D], F32)
    cols = _sb(st, nc, pre + "cols", [128, 16], F32)
    epscol = _sb(st, nc, pre + "epscol", [128, 1], F32)
    onesr = _sb(st, nc, pre + "onesr", [1, 128], BF16)
    identb = _sb(st, nc, pre + "identb", [128, 128], BF16)
    x1T = _sb(st, nc, pre + "x1T", [128, NC * T], BF16)
    wsb = _sb(st, nc, pre + "wsb", [128, 3 * NC * D], BF16)
    wTsb = _sb(st, nc, pre + "wTsb", [128, 3 * NC * D], BF16)
    brow = _sb(st, nc, pre + "brow", [1, 3 * D], BF16)
    bcolq = _sb(st, nc, pre + "bcolq", [128, 8], F32)
    bcolk = _sb(st, nc, pre + "bcolk", [128, 8], F32)
    obuf = _sb(st, nc, pre + "obuf", [128, 6 * 512], BF16)
    fbuf = _sb(st, nc, pre + "fbuf", [128, 4 * 512], BF16)

    psT0 = _ps(st, nc, pre + "psT0", [128, 128], BF16)
    psT1 = _ps(st, nc, pre + "psT1", [128, 128], BF16)
    psA = _ps(st, nc, pre + "psA", [128, 512], F32)
    psB = _ps(st, nc, pre + "psB", [128, 512], F32)
    psC = _ps(st, nc, pre + "psC", [128, 512], F32)
    psD2 = _ps(st, nc, pre + "psD2", [128, 512], F32)
    psE = _ps(st, nc, pre + "psE", [128, 512], F32)
    psF = _ps(st, nc, pre + "psF", [128, 512], F32)

    wl = nc.alloc_semaphore(pre + "wl")
    il = nc.alloc_semaphore(pre + "il")
    xl = [nc.alloc_semaphore(f"{pre}xl{i}") for i in range(2)]
    sa = nc.alloc_semaphore(pre + "sa")
    sv = nc.alloc_semaphore(pre + "sv")
    sq = nc.alloc_semaphore(pre + "sq")
    tp = nc.alloc_semaphore(pre + "tp")
    mm = nc.alloc_semaphore(pre + "mm")
    mo = nc.alloc_semaphore(pre + "mo")
    so = [nc.alloc_semaphore(f"{pre}so{i}") for i in range(2)]
    sq6 = [nc.alloc_semaphore(f"{pre}sq6_{i}") for i in range(6)]
    sf = [nc.alloc_semaphore(f"{pre}sf{i}") for i in range(4)]

    WL_ALL = 16 * (3 * NC + 3 + 16)

    with nc.Block() as b:
        @b.gpsimd
        def _(e):
            e.dma_start(identb[:], idb[:]).then_inc(il, 16)
            for i, w in enumerate((Wq_i, Wk_i, Wv_i)):
                for oc in range(NC):
                    e.dma_start(wsb[:, (i * NC + oc) * D:(i * NC + oc + 1) * D],
                                w[oc * 128:(oc + 1) * 128, :]).then_inc(wl, 16)
            for i, bb in enumerate((bq_i, bk_i, bv_i)):
                e.dma_start(brow[0:1, i * D:(i + 1) * D], bb[:]).then_inc(wl, 16)
            for oc in range(NC):
                e.dma_start(bcolq[:, oc:oc + 1], bq_i[0:1, oc * 128:(oc + 1) * 128]
                            .rearrange("a p -> p a")).then_inc(wl, 16)
                e.dma_start(bcolk[:, oc:oc + 1], bk_i[0:1, oc * 128:(oc + 1) * 128]
                            .rearrange("a p -> p a")).then_inc(wl, 16)
        @b.vector
        def _(e):
            e.memset(epscol[:], EPS)
            e.memset(onesr[:], 1.0).then_inc(sv, 1)

    with nc.Block() as b:
        @b.tensor
        def _(e):
            e.wait_ge(il, 16)
            e.wait_ge(wl, WL_ALL)
            n = 0
            for i in range(3):
                for oc in range(NC):
                    for ic in range(NC):
                        if n >= 2:
                            e.wait_ge(sv, 1 + n - 1)
                        ps = (psT0, psT1)[n % 2]
                        e.transpose(ps[:], wsb[:, (i * NC + oc) * D + ic * 128:
                                                (i * NC + oc) * D + (ic + 1) * 128],
                                    identb[:]).then_inc(tp, 1)
                        n += 1
        @b.vector
        def _(e):
            n = 0
            for i in range(3):
                for oc in range(NC):
                    for ic in range(NC):
                        e.wait_ge(tp, n + 1)
                        ps = (psT0, psT1)[n % 2]
                        e.tensor_copy(wTsb[:, (i * NC + ic) * D + oc * 128:
                                            (i * NC + ic) * D + (oc + 1) * 128],
                                      ps[:]).then_inc(sv, 1)
                        n += 1

    SV0 = 1 + 3 * NC * NC
    TP0 = 3 * NC * NC
    VPT = 7 + NC

    with nc.Block() as b:
        @b.sync
        def _(e):
            for tt in range(NT):
                if tt >= 2:
                    e.wait_ge(sv, SV0 + (tt - 2) * VPT + 7)
                e.dma_start(xa[:, (tt % 2) * D:(tt % 2 + 1) * D],
                            x1_i[tt * 128:(tt + 1) * 128, :]).then_inc(xl[tt % 2], 16)
        @b.scalar
        def _(e):
            for tt in range(NT):
                par = tt % 2
                xat = xa[:, par * D:(par + 1) * D]
                e.wait_ge(xl[par], 16 * (tt // 2 + 1))
                if tt >= 1:
                    e.wait_ge(sv, SV0 + (tt - 1) * VPT + 2)
                e.activation(dump[:], xat, AF.Copy, accum_out=cols[:, 0:1]).then_inc(sa, 1)
                e.activation(dump2[:], xat, AF.Square, accum_out=cols[:, 1:2]).then_inc(sa, 1)
                e.wait_ge(sv, SV0 + tt * VPT + 3)
                e.activation(cols[:, 5:6], cols[:, 4:5], AF.Sqrt, bias=epscol[:]).then_inc(sq, 1)
        @b.vector
        def _(e):
            c = 1.0 / D
            for tt in range(NT):
                par = tt % 2
                B0 = SV0 + tt * VPT
                xat = xa[:, par * D:(par + 1) * D]
                xnt_ = xnb[:, par * D:(par + 1) * D]
                e.wait_ge(sa, tt * 2 + 2)
                e.tensor_scalar_mul(cols[:, 2:3], cols[:, 0:1], c).then_inc(sv, 1)
                e.wait_ge(sv, B0 + 1)
                e.tensor_mul(cols[:, 3:4], cols[:, 2:3], cols[:, 2:3]).then_inc(sv, 1)
                e.wait_ge(sv, B0 + 2)
                e.scalar_tensor_tensor(cols[:, 4:5], cols[:, 1:2], c, cols[:, 3:4],
                                       ALU.mult, ALU.subtract).then_inc(sv, 1)
                e.wait_ge(sq, tt + 1)
                e.reciprocal(cols[:, 6:7], cols[:, 5:6]).then_inc(sv, 1)
                e.wait_ge(sv, B0 + 4)
                e.scalar_tensor_tensor(cols[:, 7:8], cols[:, 2:3], -1.0, cols[:, 6:7],
                                       ALU.mult, ALU.mult).then_inc(sv, 1)
                e.wait_ge(sv, B0 + 5)
                if tt >= 2:
                    e.wait_ge(tp, TP0 + (tt - 1) * NC)
                e.tensor_scalar(xnt_, xat, cols[:, 6:7], cols[:, 7:8], ALU.mult,
                                ALU.add).then_inc(sv, 2)
                for kc in range(NC):
                    e.wait_ge(tp, TP0 + tt * NC + kc + 1)
                    ps = (psT0, psT1)[kc % 2]
                    e.tensor_copy(x1T[:, kc * T + tt * 128:kc * T + (tt + 1) * 128],
                                  ps[:]).then_inc(sv, 1)
        @b.tensor
        def _(e):
            for tt in range(NT):
                e.wait_ge(sv, SV0 + tt * VPT + 7)
                for kc in range(NC):
                    if kc >= 2:
                        e.wait_ge(sv, SV0 + tt * VPT + 7 + kc - 1)
                    ps = (psT0, psT1)[kc % 2]
                    e.transpose(ps[:], xnb[:, (tt % 2) * D + kc * 128:(tt % 2) * D + (kc + 1) * 128],
                                identb[:]).then_inc(tp, 1)

    SV1 = SV0 + NT * VPT
    TP1 = TP0 + NT * NC

    with nc.Block() as b:
        @b.sync
        def _(e):
            for j in range(16):
                kc, tc2 = j // 2, j % 2
                e.wait_ge(sv, SV1)
                e.dma_start(xn1T_o[kc * 128:(kc + 1) * 128, tc2 * 1024:(tc2 + 1) * 1024],
                            x1T[:, kc * T + tc2 * 1024:kc * T + (tc2 + 1) * 1024]).then_inc(so[0], 16)
        @b.tensor
        def _(e):
            n = 0
            for tt in range(NT):
                e.wait_ge(sv, SV1)
                for mat in range(3):
                    for half in range(2):
                        ps = (psA, psB, psC, psD2, psE, psF)[(mat * 2 + half)]
                        if n >= 6:
                            e.wait_ge(sv, SV1 + n - 5)
                        for kc in range(NC):
                            e.matmul(ps[:], x1T[:, kc * T + tt * 128:kc * T + (tt + 1) * 128],
                                     wTsb[:, (mat * NC + kc) * D + half * 512:
                                          (mat * NC + kc) * D + (half + 1) * 512],
                                     start=(kc == 0), stop=False)
                        mmi = e.matmul(ps[:], onesr[:],
                                       brow[0:1, mat * D + half * 512:mat * D + (half + 1) * 512],
                                       start=False, stop=True)
                        mmi.then_inc(mm, 1)
                        n += 1
        @b.vector
        def _(e):
            n = 0
            for tt in range(NT):
                for mat in range(3):
                    for half in range(2):
                        e.wait_ge(mm, n + 1)
                        slot = n % 6
                        if n >= 6:
                            e.wait_ge(sq6[slot], 16 * (n // 6))
                        ps = (psA, psB, psC, psD2, psE, psF)[(mat * 2 + half)]
                        e.tensor_copy(obuf[:, slot * 512:(slot + 1) * 512],
                                      ps[:]).then_inc(sv, 1)
                        n += 1
        @b.gpsimd
        def _(e):
            n = 0
            for tt in range(NT):
                for mat in range(3):
                    dst = (qt_o, kt_o, vt_o)[mat]
                    for half in range(2):
                        e.wait_ge(sv, SV1 + n + 1)
                        slot = n % 6
                        e.dma_start(dst[tt * 128:(tt + 1) * 128, half * 512:(half + 1) * 512],
                                    obuf[:, slot * 512:(slot + 1) * 512]).then_inc(sq6[slot], 16)
                        n += 1
            for i in range(6):
                e.wait_ge(sq6[i], 16 * 16)
            e.wait_ge(so[0], 16 * 16)

    if ag2_hook is not None:
        ag2_hook()

    SV2 = SV1 + 96
    MM2 = 96

    with nc.Block() as b:
        @b.tensor
        def _(e):
            n = 0
            for mat in range(2):
                for ot in range(NC):
                    for tc in range(4):
                        ps = (psA, psB)[n % 2]
                        if n >= 2:
                            e.wait_ge(sv, SV2 + n - 1)
                        for kc in range(NC):
                            mmi = e.matmul(ps[:], wTsb[:, (mat * NC + kc) * D + ot * 128:
                                                       (mat * NC + kc) * D + (ot + 1) * 128],
                                           x1T[:, kc * T + tc * 512:kc * T + (tc + 1) * 512],
                                           start=(kc == 0), stop=(kc == NC - 1))
                            if kc == NC - 1:
                                mmi.then_inc(mo, 1)
                        n += 1
        @b.vector
        def _(e):
            n = 0
            for mat in range(2):
                bcol = (bcolq, bcolk)[mat]
                for ot in range(NC):
                    for tc in range(4):
                        e.wait_ge(mo, n + 1)
                        if n >= 4:
                            e.wait_ge(sf[n % 4], 16 * (n // 4))
                        ps = (psA, psB)[n % 2]
                        e.tensor_scalar_add(fbuf[:, (n % 4) * 512:(n % 4 + 1) * 512],
                                            ps[:], bcol[:, ot:ot + 1]).then_inc(sv, 1)
                        n += 1
        @b.sync
        def _(e):
            n = 0
            for mat in range(2):
                dst = (qTf_o, kTf_o)[mat]
                for ot in range(NC):
                    for tc in range(4):
                        e.wait_ge(sv, SV2 + n + 1)
                        e.dma_start(dst[ot * 128:(ot + 1) * 128, tc * 512:(tc + 1) * 512],
                                    fbuf[:, (n % 4) * 512:(n % 4 + 1) * 512]).then_inc(sf[n % 4], 16)
                        n += 1
            for i in range(4):
                e.wait_ge(sf[i], 16 * 16)
    return


def build_launchD():
    nc = bass.Bass("TRN2", target_bir_lowering=False)
    io = {}
    io["x1_i"] = nc.dram_tensor("x1_i", [T, D], F32, kind="ExternalInput").ap()
    io["Wq_i"] = nc.dram_tensor("Wq_i", [D, D], F32, kind="ExternalInput").ap()
    io["Wk_i"] = nc.dram_tensor("Wk_i", [D, D], F32, kind="ExternalInput").ap()
    io["Wv_i"] = nc.dram_tensor("Wv_i", [D, D], F32, kind="ExternalInput").ap()
    io["bq_i"] = nc.dram_tensor("bq_i", [1, D], F32, kind="ExternalInput").ap()
    io["bk_i"] = nc.dram_tensor("bk_i", [1, D], F32, kind="ExternalInput").ap()
    io["bv_i"] = nc.dram_tensor("bv_i", [1, D], F32, kind="ExternalInput").ap()
    io["idb"] = nc.dram_tensor("idb", [128, 128], BF16, kind="ExternalInput").ap()
    io["qt_o"] = nc.dram_tensor("qt_o", [T, D], BF16, kind="ExternalOutput").ap()
    io["kt_o"] = nc.dram_tensor("kt_o", [T, D], BF16, kind="ExternalOutput").ap()
    io["vt_o"] = nc.dram_tensor("vt_o", [T, D], BF16, kind="ExternalOutput").ap()
    io["qTf_o"] = nc.dram_tensor("qTf_o", [D, T], BF16, kind="ExternalOutput").ap()
    io["kTf_o"] = nc.dram_tensor("kTf_o", [D, T], BF16, kind="ExternalOutput").ap()
    io["xn1T_o"] = nc.dram_tensor("xn1T_o", [D, T], BF16, kind="ExternalOutput").ap()
    st = ExitStack()
    emit_segDd(nc, "Dd_", io, st)
    st.close()
    return nc


def _launchD(resC, inputs):
    ident_bf = np.eye(128, dtype=np.float32).astype(BFDT)
    ncD = build_launchD()
    in_maps = []
    for c in range(8):
        in_maps.append({
            "x1_i": np.asarray(resC.results[c]["x1_o"]),
            "Wq_i": np.asarray(inputs["L1_attn_Wq"], np.float32),
            "Wk_i": np.asarray(inputs["L1_attn_Wk"], np.float32),
            "Wv_i": np.asarray(inputs["L1_attn_Wv"], np.float32),
            "bq_i": np.asarray(inputs["L1_attn_bq"], np.float32).reshape(1, D),
            "bk_i": np.asarray(inputs["L1_attn_bk"], np.float32).reshape(1, D),
            "bv_i": np.asarray(inputs["L1_attn_bv"], np.float32).reshape(1, D),
            "idb": ident_bf,
        })
    return run_bass_kernel_spmd(ncD, in_maps, core_ids=list(range(8)))





def emit_segEe(nc, pre, io, st, dyn_windows=False):
    qTf_i = io["qTf_i"]
    kTf_i = io["kTf_i"]
    qt_i = io["qt_i"]
    kwin_i = io["kwin_i"]
    vwin_i = io["vwin_i"]
    vfull_i = io["vfull_i"]
    Wo_i = io["Wo_i"]
    bo_i = io["bo_i"]
    mtt_i = io["mtt_i"]
    mtp_i = io["mtp_i"]
    ntt_i = io["ntt_i"]
    ntp_i = io["ntp_i"]
    idb = io["idb"]
    y1_o = io["y1_o"]

    NW = 48
    OFFT = [2, 4, 8, 16, 32]

    qTsb = _sb(st, nc, pre + "qTsb", [128, NP * T], BF16)
    kTsb = _sb(st, nc, pre + "kTsb", [128, NP * T], BF16)
    qtok = _sb(st, nc, pre + "qtok", [128, 2 * D], BF16)
    attn = _sb(st, nc, pre + "attn", [128, NT * D], BF16)
    woTsb = _sb(st, nc, pre + "woTsb", [128, NC * D], BF16)
    kw31 = _sb(st, nc, pre + "kw31", [128, D], BF16)
    kprevT = _sb(st, nc, pre + "kprevT", [128, NC * 128], BF16)
    vsrow = _sb(st, nc, pre + "vsrow", [1, D], F32)
    vsb = _sb(st, nc, pre + "vsb", [1, D], BF16)
    borow = _sb(st, nc, pre + "borow", [1, D], BF16)
    masks = _sb(st, nc, pre + "masks", [128, 4 * 128], BF16)
    identb = _sb(st, nc, pre + "identb", [128, 128], BF16)
    onesr = _sb(st, nc, pre + "onesr", [1, 128], BF16)
    onescol = _sb(st, nc, pre + "onescol", [128, 1], BF16)
    wtile = _sb(st, nc, pre + "wtile", [128, 2 * 256], F32)
    wmb = _sb(st, nc, pre + "wmb", [128, 2 * 256], BF16)
    uall = _sb(st, nc, pre + "uall", [128, 2 * D], F32)
    zall = _sb(st, nc, pre + "zall", [128, 2 * 16], F32)
    pfull = _sb(st, nc, pre + "pfull", [128, D], F32)
    ftmp = _sb(st, nc, pre + "ftmp", [128, D], F32)
    sfar = _sb(st, nc, pre + "sfar", [128, 80], F32)
    efar = _sb(st, nc, pre + "efar", [128, 80], F32)
    wfar = _sb(st, nc, pre + "wfar", [128, 80], F32)
    zf16 = _sb(st, nc, pre + "zf16", [128, 16], F32)
    zz16 = _sb(st, nc, pre + "zz16", [128, 16], F32)
    rz16 = _sb(st, nc, pre + "rz16", [128, 16], F32)
    attnT = _sb(st, nc, pre + "attnT", [128, NC * 128], BF16)
    ybuf = _sb(st, nc, pre + "ybuf", [128, 2 * 512], F32)
    pst = ExitStack()
    wosb = _sb(pst, nc, pre + "wosb", [128, NC * D], BF16)
    vf = _sb(pst, nc, pre + "vf", [128, 2 * D], BF16)

    psN0 = _ps(st, nc, pre + "psN0", [128, 256], F32)
    psN1 = _ps(st, nc, pre + "psN1", [128, 256], F32)
    psU0 = _ps(st, nc, pre + "psU0", [128, DH], F32)
    psU1 = _ps(st, nc, pre + "psU1", [128, DH], F32)
    psZ0 = _ps(st, nc, pre + "psZ0", [128, 1], F32)
    psZ1 = _ps(st, nc, pre + "psZ1", [128, 1], F32)
    psO0 = _ps(st, nc, pre + "psO0", [128, 512], F32)
    psO1 = _ps(st, nc, pre + "psO1", [128, 512], F32)
    psT0 = psN0.bitcast(BF16)[:, 0:128]

    wl = nc.alloc_semaphore(pre + "wl")
    il = nc.alloc_semaphore(pre + "il")
    ol = nc.alloc_semaphore(pre + "ol")
    vfl = [nc.alloc_semaphore(f"{pre}vfl{i}") for i in range(2)]
    klw = [nc.alloc_semaphore(f"{pre}klw{i}") for i in range(2)]
    vlw = [nc.alloc_semaphore(f"{pre}vlw{i}") for i in range(2)]
    sa = nc.alloc_semaphore(pre + "sa")
    sfx = nc.alloc_semaphore(pre + "sfx")
    sv = nc.alloc_semaphore(pre + "sv")
    mm = nc.alloc_semaphore(pre + "mm")
    wot = nc.alloc_semaphore(pre + "wot")
    mvs = nc.alloc_semaphore(pre + "mvs")
    vss = nc.alloc_semaphore(pre + "vss")
    atp = nc.alloc_semaphore(pre + "atp")
    mo = nc.alloc_semaphore(pre + "mo")
    sva = nc.alloc_semaphore(pre + "sva")
    svy = nc.alloc_semaphore(pre + "svy")
    syo = [nc.alloc_semaphore(f"{pre}syo{i}") for i in range(2)]

    WL_ALL = 16 * (8 + 8 + 1)
    WOV = 64 + 8
    VE2 = 16 * 4 + 25
    VH0 = WOV

    def VB(t):
        return VH0 + t * VE2

    with nc.Block() as b:
        @b.gpsimd
        def _(e):
            e.dma_start(identb[:], idb[:]).then_inc(il, 16)
            e.dma_start(masks[:, 0:128], mtt_i[:]).then_inc(il, 16)
            e.dma_start(masks[:, 128:256], mtp_i[:]).then_inc(il, 16)
            e.dma_start(masks[:, 256:384], ntt_i[:]).then_inc(il, 16)
            e.dma_start(masks[:, 384:512], ntp_i[:]).then_inc(il, 16)
            e.dma_start(borow[:], bo_i[:]).then_inc(il, 16)
            for oc in range(NC):
                e.dma_start(wosb[:, oc * D:(oc + 1) * D],
                            Wo_i[oc * 128:(oc + 1) * 128, :]).then_inc(il, 16)
            for pr in range(NP):
                e.dma_start(qTsb[:, pr * T:(pr + 1) * T],
                            qTf_i[pr * 128:(pr + 1) * 128, :]).then_inc(wl, 16)
                e.dma_start(kTsb[:, pr * T:(pr + 1) * T],
                            kTf_i[pr * 128:(pr + 1) * 128, :]).then_inc(wl, 16)
            e.dma_start(kw31[:], kwin_i[31 * 128:32 * 128, :]).then_inc(wl, 16)
        @b.vector
        def _(e):
            e.memset(onesr[:], 1.0)
            e.memset(onescol[:], 1.0).then_inc(ol, 1)
        @b.tensor
        def _(e):
            e.wait_ge(il, 16 * 14)
            n = 0
            for oc in range(NC):
                for ic in range(NC):
                    e.transpose(psT0, wosb[:, oc * D + ic * 128:oc * D + (ic + 1) * 128],
                                identb[:]).then_inc(wot, 1)
                    e.wait_ge(sv, n + 1)
                    n += 1
            e.wait_ge(wl, WL_ALL)
            for kc in range(NC):
                e.transpose(psT0, kw31[:, kc * 128:(kc + 1) * 128],
                            identb[:]).then_inc(wot, 1)
                e.wait_ge(sv, 64 + kc + 1)
        @b.vector
        def _(e):
            n = 0
            for oc in range(NC):
                for ic in range(NC):
                    e.wait_ge(wot, n + 1)
                    e.tensor_copy(woTsb[:, ic * D + oc * 128:ic * D + (oc + 1) * 128],
                                  psT0).then_inc(sv, 1)
                    n += 1
            for kc in range(NC):
                e.wait_ge(wot, 64 + kc + 1)
                e.tensor_copy(kprevT[:, kc * 128:(kc + 1) * 128], psT0).then_inc(sv, 1)

    with nc.Block() as b:
        @b.gpsimd
        def _(e):
            for i in range(SEQ // 128):
                if i >= 2:
                    e.wait_ge(mvs, i - 1)
                e.dma_start(vf[:, (i % 2) * D:(i % 2 + 1) * D],
                            vfull_i[i * 128:(i + 1) * 128, :]).then_inc(vfl[i % 2], 16)
        @b.tensor
        def _(e):
            e.wait_ge(ol, 1)
            for i in range(SEQ // 128):
                e.wait_ge(vfl[i % 2], 16 * (i // 2 + 1))
                first = (i == 0)
                last = (i == SEQ // 128 - 1)
                e.matmul(psO0[0:1, :], onescol[:], vf[:, (i % 2) * D:(i % 2) * D + 512],
                         start=first, stop=last)
                e.matmul(psO1[0:1, :], onescol[:], vf[:, (i % 2) * D + 512:(i % 2 + 1) * D],
                         start=first, stop=last).then_inc(mvs, 1)
        @b.vector
        def _(e):
            e.wait_ge(mvs, SEQ // 128)
            e.tensor_copy(vsrow[0:1, 0:512], psO0[0:1, :]).then_inc(vss, 1)
            e.tensor_copy(vsrow[0:1, 512:1024], psO1[0:1, :]).then_inc(vss, 1)
            e.wait_ge(vss, 2)
            e.tensor_copy(vsb[:], vsrow[:]).then_inc(vss, 1)

    pst.close()
    kfb = _sb(st, nc, pre + "kfb", [128, 2 * 5 * D], BF16)
    vfb = _sb(st, nc, pre + "vfb", [128, 2 * 5 * D], BF16)
    vnear = _sb(st, nc, pre + "vnear", [128, 2 * 2 * D], BF16)

    with nc.Block() as b:
        @b.gpsimd
        def _(e):
            for t in range(NT):
                par = t % 2
                if t >= 2:
                    e.wait_ge(sv, VB(t - 1))
                ksrc = lambda widx: kwin_i[widx * 128:(widx + 1) * 128, :]
                vsrc = lambda widx: vwin_i[widx * 128:(widx + 1) * 128, :]
                for i5, off in enumerate(OFFT):
                    widx = 32 + t - off
                    e.dma_start(kfb[:, (par * 5 + i5) * D:(par * 5 + i5 + 1) * D],
                                ksrc(widx)).then_inc(klw[par], 16)
                    e.dma_start(vfb[:, (par * 5 + i5) * D:(par * 5 + i5 + 1) * D],
                                vsrc(widx)).then_inc(klw[par], 16)
                for j, widx in enumerate((32 + t, 31 + t)):
                    e.dma_start(vnear[:, (par * 2 + j) * D:(par * 2 + j + 1) * D],
                                vsrc(widx)).then_inc(vlw[par], 16)
                e.dma_start(qtok[:, par * D:(par + 1) * D],
                            qt_i[t * 128:(t + 1) * 128, :]).then_inc(vlw[par], 16)
        @b.tensor
        def _(e):
            e.wait_ge(vss, 3)
            for t in range(NT):
                par = t % 2
                e.wait_ge(vlw[par], 48 * (t // 2 + 1))
                for h in range(H):
                    n = t * 16 + h
                    hp, hh = h // 2, h % 2
                    psN = (psN0, psN1)[n % 2]
                    psU = (psU0, psU1)[n % 2]
                    psZ = (psZ0, psZ1)[n % 2]
                    if n >= 2:
                        e.wait_ge(sa, n - 1)
                    qmv = qTsb[hh * 64:(hh + 1) * 64, hp * T + t * 128:hp * T + (t + 1) * 128]
                    e.matmul(psN[:, 0:128],
                             kTsb[hh * 64:(hh + 1) * 64, hp * T + t * 128:hp * T + (t + 1) * 128],
                             qmv, start=True, stop=True)
                    if t == 0:
                        lhs_prev = kprevT[hh * 64:(hh + 1) * 64, hp * 128:(hp + 1) * 128]
                    else:
                        lhs_prev = kTsb[hh * 64:(hh + 1) * 64,
                                        hp * T + (t - 1) * 128:hp * T + t * 128]
                    e.matmul(psN[:, 128:256], lhs_prev, qmv,
                             start=True, stop=True).then_inc(mm, 1)
                    e.wait_ge(sv, VB(t) + 4 * h + 2)
                    if n >= 2:
                        e.wait_ge(sv, VB(t) + 4 * h - 8 + 3) if False else None
                        tp2, hp2 = (n - 2) // 16, (n - 2) % 16
                        e.wait_ge(sv, VB(tp2) + 4 * hp2 + 3)
                    wpar = n % 2
                    vtt = vnear[:, (par * 2 + 0) * D + h * DH:(par * 2 + 0) * D + (h + 1) * DH]
                    vtp_ = vnear[:, (par * 2 + 1) * D + h * DH:(par * 2 + 1) * D + (h + 1) * DH]
                    wm0 = wmb[:, wpar * 256:wpar * 256 + 128]
                    wm1 = wmb[:, wpar * 256 + 128:wpar * 256 + 256]
                    e.matmul(psU[:], wm0, vtt, start=True, stop=False)
                    e.matmul(psU[:], wm1, vtp_, start=False, stop=False)
                    e.matmul(psU[:], masks[:, 256:384], vtt, start=False, stop=False)
                    e.matmul(psU[:], masks[:, 384:512], vtp_, start=False, stop=False)
                    e.matmul(psU[:], onesr[:], vsb[0:1, h * DH:(h + 1) * DH],
                             start=False, stop=True).then_inc(mm, 1)
                    if n >= 2:
                        tp2, hp2 = (n - 2) // 16, (n - 2) % 16
                        e.wait_ge(sv, VB(tp2) + 4 * hp2 + 4)
                    e.matmul(psZ[:], wm0, onescol[:], start=True, stop=False)
                    e.matmul(psZ[:], wm1, onescol[:],
                             start=False, stop=True).then_inc(mm, 1)
        @b.scalar
        def _(e):
            for t in range(NT):
                for h in range(H):
                    n = t * 16 + h
                    psN = (psN0, psN1)[n % 2]
                    e.wait_ge(mm, 3 * n + 1)
                    if n >= 2:
                        tp2, hp2 = (n - 2) // 16, (n - 2) % 16
                        e.wait_ge(sv, VB(tp2) + 4 * hp2 + 2)
                    e.activation(wtile[:, (n % 2) * 256:(n % 2 + 1) * 256], psN[:],
                                 AF.Exp, scale=ISC).then_inc(sa, 1)
                e.wait_ge(sv, VB(t) + 64 + 10)
                if t >= 2:
                    e.wait_ge(sv, VB(t - 2) + 64 + 11)
                e.activation(efar[:], sfar[:], AF.Exp, scale=ISC).then_inc(sfx, 1)
        @b.vector
        def _(e):
            for t in range(NT):
                par = t % 2
                nv = VB(t)
                def stepw(inst):
                    nonlocal nv
                    inst.then_inc(sv, 1)
                    nv += 1
                for h in range(H):
                    n = t * 16 + h
                    wpar = n % 2
                    e.wait_ge(sa, n + 1)
                    if n >= 2:
                        e.wait_ge(mm, 3 * (n - 1))
                    stepw(e.tensor_tensor(wmb[:, wpar * 256:wpar * 256 + 128],
                                          wtile[:, wpar * 256:wpar * 256 + 128],
                                          masks[:, 0:128], ALU.mult))
                    stepw(e.tensor_tensor(wmb[:, wpar * 256 + 128:wpar * 256 + 256],
                                          wtile[:, wpar * 256 + 128:wpar * 256 + 256],
                                          masks[:, 128:256], ALU.mult))
                    e.wait_ge(mm, 3 * n + 2)
                    stepw(e.tensor_copy(uall[:, par * D + h * DH:par * D + (h + 1) * DH],
                                        (psU0, psU1)[n % 2][:]))
                    e.wait_ge(mm, 3 * n + 3)
                    stepw(e.tensor_copy(zall[:, par * 16 + h:par * 16 + h + 1],
                                        (psZ0, psZ1)[n % 2][:]))
                e.wait_ge(klw[par], 160 * (t // 2 + 1))
                e.wait_ge(vlw[par], 48 * (t // 2 + 1))
                for i5 in range(5):
                    kslice = kfb[:, (par * 5 + i5) * D:(par * 5 + i5 + 1) * D]
                    e.wait_ge(sv, nv)
                    stepw(e.tensor_tensor(pfull[:], qtok[:, par * D:(par + 1) * D], kslice,
                                          ALU.mult))
                    e.wait_ge(sv, nv)
                    stepw(e.tensor_reduce(
                        bass.AP(sfar[:].tensor, sfar[:].offset + i5,
                                [[80, 128], [5, 16]]),
                        pfull[:].rearrange("p (hh d) -> p hh d", d=DH),
                        AX.X, ALU.add))
                e.wait_ge(sfx, t + 1)
                stepw(e.tensor_single_scalar(wfar[:], efar[:], 1.0, ALU.subtract))
                for i5 in range(5):
                    vslice = vfb[:, (par * 5 + i5) * D:(par * 5 + i5 + 1) * D]
                    wb = bass.AP(wfar[:].tensor, wfar[:].offset + i5,
                                 [[80, 128], [5, 16], [0, DH]])
                    e.wait_ge(sv, nv)
                    stepw(e.tensor_tensor(ftmp[:].rearrange("p (hh d) -> p hh d", d=DH),
                                          vslice.rearrange("p (hh d) -> p hh d", d=DH),
                                          wb, ALU.mult))
                    e.wait_ge(sv, nv)
                    stepw(e.tensor_tensor(uall[:, par * D:(par + 1) * D],
                                          uall[:, par * D:(par + 1) * D],
                                          ftmp[:], ALU.add))
                e.wait_ge(sv, nv)
                stepw(e.tensor_reduce(zf16[:], wfar[:].rearrange("p (hh d5) -> p hh d5", d5=5),
                                      AX.X, ALU.add))
                e.wait_ge(sv, nv)
                stepw(e.scalar_tensor_tensor(zz16[:], zf16[:], float(SEQ - 9),
                                             zall[:, par * 16:(par + 1) * 16],
                                             ALU.add, ALU.add))
                e.wait_ge(sv, nv)
                stepw(e.reciprocal(rz16[:], zz16[:]))
                rzb = rz16[:].unsqueeze(2).broadcast_to((128, 16, DH))
                e.wait_ge(sv, nv)
                stepw(e.tensor_tensor(attn[:, t * D:(t + 1) * D]
                                      .rearrange("p (hh d) -> p hh d", d=DH),
                                      uall[:, par * D:(par + 1) * D]
                                      .rearrange("p (hh d) -> p hh d", d=DH),
                                      rzb, ALU.mult))
                assert nv == VB(t) + VE2, (nv, VB(t), VE2)

    SVH = VH0 + NT * VE2

    with nc.Block() as b:
        @b.tensor
        def _(e):
            e.wait_ge(sv, SVH)
            for t in range(NT):
                for kc in range(NC):
                    na = t * NC + kc
                    if na >= 2:
                        e.wait_ge(sva, na - 1)
                    if kc < 2 and t >= 1:
                        e.wait_ge(mo, 2 * t)
                    ps = (psN0.bitcast(BF16), psN1.bitcast(BF16))[kc % 2]
                    e.transpose(ps[:, 0:128], attn[:, t * D + kc * 128:t * D + (kc + 1) * 128],
                                identb[:]).then_inc(atp, 1)
                e.wait_ge(sva, (t + 1) * NC)
                for half in range(2):
                    pso = (psO0, psO1)[half]
                    if t >= 1:
                        e.wait_ge(svy, 2 * (t - 1) + half + 1)
                    for kc in range(NC):
                        e.matmul(pso[:], attnT[:, kc * 128:(kc + 1) * 128],
                                 woTsb[:, kc * D + half * 512:kc * D + (half + 1) * 512],
                                 start=(kc == 0), stop=False)
                    e.matmul(pso[:], onesr[:],
                             borow[0:1, half * 512:(half + 1) * 512],
                             start=False, stop=True).then_inc(mo, 1)
        @b.vector
        def _(e):
            for t in range(NT):
                for kc in range(NC):
                    na = t * NC + kc
                    e.wait_ge(atp, na + 1)
                    ps = (psN0.bitcast(BF16), psN1.bitcast(BF16))[kc % 2]
                    e.tensor_copy(attnT[:, kc * 128:(kc + 1) * 128], ps[:, 0:128]).then_inc(sva, 1)
                for half in range(2):
                    e.wait_ge(mo, 2 * t + half + 1)
                    if t >= 1:
                        e.wait_ge(syo[half], 16 * t)
                    pso = (psO0, psO1)[half]
                    e.tensor_copy(ybuf[:, half * 512:(half + 1) * 512],
                                  pso[:]).then_inc(svy, 1)
        @b.gpsimd
        def _(e):
            for t in range(NT):
                for half in range(2):
                    e.wait_ge(svy, 2 * t + half + 1)
                    e.dma_start(y1_o[t * 128:(t + 1) * 128, half * 512:(half + 1) * 512],
                                ybuf[:, half * 512:(half + 1) * 512]).then_inc(syo[half], 16)
            for i in range(2):
                e.wait_ge(syo[i], 16 * 16)
    return


def build_launchE():
    nc = bass.Bass("TRN2", target_bir_lowering=False)
    io = {}
    io["qTf_i"] = nc.dram_tensor("qTf_i", [D, T], BF16, kind="ExternalInput").ap()
    io["kTf_i"] = nc.dram_tensor("kTf_i", [D, T], BF16, kind="ExternalInput").ap()
    io["qt_i"] = nc.dram_tensor("qt_i", [T, D], BF16, kind="ExternalInput").ap()
    io["kwin_i"] = nc.dram_tensor("kwin_i", [3 * T, D], BF16, kind="ExternalInput").ap()
    io["vwin_i"] = nc.dram_tensor("vwin_i", [3 * T, D], BF16, kind="ExternalInput").ap()
    io["vfull_i"] = nc.dram_tensor("vfull_i", [SEQ, D], BF16, kind="ExternalInput").ap()
    io["Wo_i"] = nc.dram_tensor("Wo_i", [D, D], F32, kind="ExternalInput").ap()
    io["bo_i"] = nc.dram_tensor("bo_i", [1, D], F32, kind="ExternalInput").ap()
    io["mtt_i"] = nc.dram_tensor("mtt_i", [128, 128], BF16, kind="ExternalInput").ap()
    io["mtp_i"] = nc.dram_tensor("mtp_i", [128, 128], BF16, kind="ExternalInput").ap()
    io["ntt_i"] = nc.dram_tensor("ntt_i", [128, 128], BF16, kind="ExternalInput").ap()
    io["ntp_i"] = nc.dram_tensor("ntp_i", [128, 128], BF16, kind="ExternalInput").ap()
    io["idb"] = nc.dram_tensor("idb", [128, 128], BF16, kind="ExternalInput").ap()
    io["y1_o"] = nc.dram_tensor("y1_o", [T, D], F32, kind="ExternalOutput").ap()
    st = ExitStack()
    emit_segEe(nc, "Ee_", io, st)
    st.close()
    return nc





def emit_segFf(nc, pre, io, st, me_src=None):
    x1_i = io["x1_i"]
    y1_i = io["y1_i"]
    xn1T_i = io["xn1T_i"]
    Wr_i = io["Wr_i"]
    Wi_i = io.get("Wi_i")
    Wo_i = io.get("Wo_i")
    sel_i = io["sel_i"]
    idb = io["idb"]
    out_o = io["out_o"]

    x1T = _sb(st, nc, pre + "x1T", [128, NC * T], BF16)
    wrsb = _sb(st, nc, pre + "wrsb", [128, NC * NEXP], BF16)
    selsb = _sb(st, nc, pre + "selsb", [NEXP, NEXP * 128], BF16)
    identb = _sb(st, nc, pre + "identb", [128, 128], BF16)
    gT = _sb(st, nc, pre + "gT", [NEXP, T], BF16)
    rcol = _sb(st, nc, pre + "rcol", [128, NEXP], F32)
    rexp = _sb(st, nc, pre + "rexp", [128, NEXP], F32)
    rmaxc = _sb(st, nc, pre + "rmaxc", [128, 1], F32)
    nmc = _sb(st, nc, pre + "nmc", [128, 1], F32)
    rsum = _sb(st, nc, pre + "rsum", [128, 1], F32)
    rrec = _sb(st, nc, pre + "rrec", [128, 1], F32)
    m1c = _sb(st, nc, pre + "m1c", [128, 1], F32)
    m2c = _sb(st, nc, pre + "m2c", [128, 1], F32)
    eq1 = _sb(st, nc, pre + "eq1", [128, NEXP], F32)
    eq2 = _sb(st, nc, pre + "eq2", [128, NEXP], F32)
    cs1 = _sb(st, nc, pre + "cs1", [128, NEXP], F32)
    cs2 = _sb(st, nc, pre + "cs2", [128, NEXP], F32)
    cs3 = _sb(st, nc, pre + "cs3", [128, NEXP], F32)
    excl = _sb(st, nc, pre + "excl", [128, NEXP], F32)
    izt = _sb(st, nc, pre + "izt", [128, NEXP], F32)
    first = _sb(st, nc, pre + "first", [128, NEXP], F32)
    second = _sb(st, nc, pre + "second", [128, NEXP], F32)
    maskd = _sb(st, nc, pre + "maskd", [128, NEXP], F32)
    selt = _sb(st, nc, pre + "selt", [128, NEXP], F32)
    gma = _sb(st, nc, pre + "gma", [128, NEXP], F32)
    gmb = _sb(st, nc, pre + "gmb", [128, NEXP], BF16)
    wisb = _sb(st, nc, pre + "wisb", [128, NC * D], BF16)
    wesb = _sb(st, nc, pre + "wesb", [128, NC * D], BF16)
    wiT = _sb(st, nc, pre + "wiT", [128, NC * D], BF16)
    mesb = _sb(st, nc, pre + "mesb", [128, NC * D], BF16)
    bcf = _sb(st, nc, pre + "bcf", [128, 128], F32)
    scb = _sb(st, nc, pre + "scb", [128, NC * 128], BF16)
    ymoe = _sb(st, nc, pre + "ymoe", [128, NT * D], F32)
    xres = _sb(st, nc, pre + "xres", [128, 2 * D], F32)
    yres = _sb(st, nc, pre + "yres", [128, 2 * D], F32)
    tadd = _sb(st, nc, pre + "tadd", [128, D], F32)
    obuf = _sb(st, nc, pre + "obuf", [128, 2 * D], F32)

    psR = _ps(st, nc, pre + "psR", [128, NEXP], F32)
    psT0 = _ps(st, nc, pre + "psT0", [128, 128], BF16)
    psT1 = _ps(st, nc, pre + "psT1", [128, 128], BF16)
    psM0 = _ps(st, nc, pre + "psM0", [128, 512], F32)
    psM1 = _ps(st, nc, pre + "psM1", [128, 512], F32)
    psB = _ps(st, nc, pre + "psB", [128, 128], F32) if me_src is None else None
    psB5 = _ps(st, nc, pre + "psB5", [128, 512], F32) if me_src is not None else None
    psY0 = _ps(st, nc, pre + "psY0", [128, 512], F32)
    psY1 = _ps(st, nc, pre + "psY1", [128, 512], F32)

    il = nc.alloc_semaphore(pre + "il")
    wl = nc.alloc_semaphore(pre + "wl")
    wle = nc.alloc_semaphore(pre + "wle")
    rl = [nc.alloc_semaphore(f"{pre}rl{i}") for i in range(2)]
    sa = nc.alloc_semaphore(pre + "sa")
    sr = nc.alloc_semaphore(pre + "sr")
    mr = nc.alloc_semaphore(pre + "mr")
    gt = nc.alloc_semaphore(pre + "gt")
    wit = nc.alloc_semaphore(pre + "wit")
    sv = nc.alloc_semaphore(pre + "sv")
    mmm = nc.alloc_semaphore(pre + "mmm")
    mby = nc.alloc_semaphore(pre + "mby")
    sxo = [nc.alloc_semaphore(f"{pre}sxo{i}") for i in range(2)]

    WL_ALL = 16 * (NC + NC + NEXP)

    with nc.Block() as b:
        @b.gpsimd
        def _(e):
            e.dma_start(identb[:], idb[:]).then_inc(il, 16)
            for kc in range(NC):
                e.dma_start(x1T[:, kc * T:(kc + 1) * T],
                            xn1T_i[kc * 128:(kc + 1) * 128, :]).then_inc(wl, 16)
                e.dma_start(wrsb[:, kc * NEXP:(kc + 1) * NEXP],
                            Wr_i[kc * 128:(kc + 1) * 128, :]).then_inc(wl, 16)
            for ee in range(NEXP):
                e.dma_start(selsb[:, ee * 128:(ee + 1) * 128], sel_i[ee]).then_inc(wl, 16)

    pos_gb = []
    pos_end = []
    pos_nm = []
    with nc.Block() as b:
        @b.vector
        def _(e):
            nv = 0
            for tt in range(NT):
                def stepw(inst):
                    nonlocal nv
                    inst.then_inc(sr, 1)
                    nv += 1
                e.wait_ge(mr, tt + 1)
                stepw(e.tensor_copy(rcol[:], psR[:]))
                e.wait_ge(sr, nv)
                stepw(e.tensor_reduce(rmaxc[:], rcol[:], AX.X, ALU.max))
                e.wait_ge(sr, nv)
                stepw(e.tensor_scalar_mul(nmc[:], rmaxc[:], -1.0))
                pos_nm.append(nv)
                e.wait_ge(sa, tt + 1)
                stepw(e.reciprocal(rrec[:], rsum[:]))
                stepw(e.tensor_reduce(m1c[:], rexp[:], AX.X, ALU.max))
                e.wait_ge(sr, nv)
                stepw(e.tensor_scalar(eq1[:], rexp[:], m1c[:], None, ALU.is_ge))
                e.wait_ge(sr, nv)
                stepw(e.tensor_copy(cs1[:, 0:1], eq1[:, 0:1]))
                stepw(e.tensor_tensor(cs1[:, 1:8], eq1[:, 1:8], eq1[:, 0:7], ALU.add))
                e.wait_ge(sr, nv)
                stepw(e.tensor_copy(cs2[:, 0:2], cs1[:, 0:2]))
                stepw(e.tensor_tensor(cs2[:, 2:8], cs1[:, 2:8], cs1[:, 0:6], ALU.add))
                e.wait_ge(sr, nv)
                stepw(e.tensor_copy(cs3[:, 0:4], cs2[:, 0:4]))
                stepw(e.tensor_tensor(cs3[:, 4:8], cs2[:, 4:8], cs2[:, 0:4], ALU.add))
                e.wait_ge(sr, nv)
                stepw(e.tensor_sub(excl[:], cs3[:], eq1[:]))
                e.wait_ge(sr, nv)
                stepw(e.tensor_single_scalar(izt[:], excl[:], 0.5, ALU.is_lt))
                e.wait_ge(sr, nv)
                stepw(e.tensor_mul(first[:], eq1[:], izt[:]))
                e.wait_ge(sr, nv)
                stepw(e.scalar_tensor_tensor(maskd[:], first[:], -1e30, rexp[:],
                                             ALU.mult, ALU.add))
                e.wait_ge(sr, nv)
                stepw(e.tensor_reduce(m2c[:], maskd[:], AX.X, ALU.max))
                e.wait_ge(sr, nv)
                stepw(e.tensor_scalar(eq2[:], maskd[:], m2c[:], None, ALU.is_ge))
                e.wait_ge(sr, nv)
                stepw(e.tensor_copy(cs1[:, 0:1], eq2[:, 0:1]))
                stepw(e.tensor_tensor(cs1[:, 1:8], eq2[:, 1:8], eq2[:, 0:7], ALU.add))
                e.wait_ge(sr, nv)
                stepw(e.tensor_copy(cs2[:, 0:2], cs1[:, 0:2]))
                stepw(e.tensor_tensor(cs2[:, 2:8], cs1[:, 2:8], cs1[:, 0:6], ALU.add))
                e.wait_ge(sr, nv)
                stepw(e.tensor_copy(cs3[:, 0:4], cs2[:, 0:4]))
                stepw(e.tensor_tensor(cs3[:, 4:8], cs2[:, 4:8], cs2[:, 0:4], ALU.add))
                e.wait_ge(sr, nv)
                stepw(e.tensor_sub(excl[:], cs3[:], eq2[:]))
                e.wait_ge(sr, nv)
                stepw(e.tensor_single_scalar(izt[:], excl[:], 0.5, ALU.is_lt))
                e.wait_ge(sr, nv)
                stepw(e.tensor_mul(second[:], eq2[:], izt[:]))
                e.wait_ge(sr, nv)
                stepw(e.tensor_add(selt[:], first[:], second[:]))
                e.wait_ge(sr, nv)
                stepw(e.tensor_mul(gma[:], selt[:], rexp[:]))
                e.wait_ge(sr, nv)
                stepw(e.tensor_scalar_mul(gma[:], gma[:], rrec[:]))
                e.wait_ge(sr, nv)
                stepw(e.tensor_copy(gmb[:], gma[:]))
                pos_gb.append(nv)
                e.wait_ge(gt, tt + 1)
                stepw(e.tensor_copy(gT[0:NEXP, tt * 128:(tt + 1) * 128], psT0[0:NEXP, :]))
                pos_end.append(nv)
        @b.scalar
        def _(e):
            for tt in range(NT):
                e.wait_ge(sr, pos_nm[tt])
                e.activation(rexp[:], rcol[:], AF.Exp, bias=nmc[:],
                             accum_out=rsum[:]).then_inc(sa, 1)
        @b.tensor
        def _(e):
            e.wait_ge(wl, WL_ALL)
            e.wait_ge(il, 16)
            for tt in range(NT):
                if tt >= 1:
                    e.wait_ge(sr, pos_end[tt - 1] - 1)
                for kc in range(NC):
                    mi = e.matmul(psR[:], x1T[:, kc * T + tt * 128:kc * T + (tt + 1) * 128],
                                  wrsb[:, kc * NEXP:(kc + 1) * NEXP],
                                  start=(kc == 0), stop=(kc == NC - 1))
                    if kc == NC - 1:
                        mi.then_inc(mr, 1)
                e.wait_ge(sr, pos_gb[tt])
                if tt >= 1:
                    e.wait_ge(sr, pos_end[tt - 1])
                e.transpose(psT0[0:NEXP, :], gmb[:], identb[:]).then_inc(gt, 1)
    RV = 31

    if me_src is not None:
        scb5 = _sb(st, nc, pre + "scb5", [128, NC * 512], BF16)
        bcf5 = _sb(st, nc, pre + "bcf5", [128, 512], F32)
        with nc.Block() as b:
            @b.gpsimd
            def _(e):
                for ee in range(NEXP):
                    if ee >= 1:
                        e.wait_ge(mby, ee * 36)
                    for fi in range(NC):
                        e.dma_start(mesb[:, fi * D:(fi + 1) * D],
                                    me_src[ee * D + fi * 128:ee * D + (fi + 1) * 128, :]
                                    ).then_inc(wle, 16)
            @b.vector
            def _(e):
                for ee in range(NEXP):
                    EB = ee * 68
                    for tc in range(4):
                        b0 = EB + tc * 17
                        e.wait_ge(mby, ee * 36 + tc * 9 + 1)
                        e.tensor_copy(bcf5[:], psB5[:]).then_inc(sv, 1)
                        for kc in range(NC):
                            e.wait_ge(sv, b0 + 1 + kc)
                            e.tensor_tensor(scb5[:, kc * 512:(kc + 1) * 512],
                                            x1T[:, kc * T + tc * 512:kc * T + (tc + 1) * 512],
                                            bcf5[:], ALU.mult).then_inc(sv, 1)
                        for tsub in range(4):
                            tt = tc * 4 + tsub
                            for half in range(2):
                                e.wait_ge(mby, ee * 36 + tc * 9 + 1 + tsub * 2 + half + 1)
                                ps = (psY0, psY1)[half]
                                dst = ymoe[:, tt * D + half * 512:tt * D + (half + 1) * 512]
                                if ee == 0:
                                    e.tensor_copy(dst, ps[:]).then_inc(sv, 1)
                                else:
                                    e.tensor_tensor(dst, dst, ps[:], ALU.add).then_inc(sv, 1)
            @b.tensor
            def _(e):
                e.wait_ge(sr, pos_end[NT - 1])
                for ee in range(NEXP):
                    EB = ee * 68
                    e.wait_ge(wle, 16 * NC * (ee + 1))
                    for tc in range(4):
                        b0 = EB + tc * 17
                        if ee + tc > 0:
                            prev = (EB + (tc - 1) * 17 + 1) if tc >= 1 else (EB - 68 + 3 * 17 + 1)
                            e.wait_ge(sv, prev)
                        e.matmul(psB5[:], selsb[:, ee * 128:(ee + 1) * 128],
                                 gT[0:NEXP, tc * 512:(tc + 1) * 512],
                                 start=True, stop=True).then_inc(mby, 1)
                        e.wait_ge(sv, b0 + 9)
                        for tsub in range(4):
                            for half in range(2):
                                ps = (psY0, psY1)[half]
                                if tsub >= 1:
                                    e.wait_ge(sv, b0 + 9 + (tsub - 1) * 2 + half + 1)
                                elif ee + tc > 0:
                                    pv = b0 - 17 + 9 + 3 * 2 + half + 1 if tc >= 1 else EB - 68 + 3 * 17 + 9 + 3 * 2 + half + 1
                                    e.wait_ge(sv, pv)
                                for kc in range(NC):
                                    mi = e.matmul(ps[:], scb5[:, kc * 512 + tsub * 128:kc * 512 + tsub * 128 + 128],
                                                  mesb[:, kc * D + half * 512:kc * D + (half + 1) * 512],
                                                  start=(kc == 0), stop=(kc == NC - 1))
                                    if kc == NC - 1:
                                        mi.then_inc(mby, 1)
    else:
        with nc.Block() as b:
            @b.gpsimd
            def _(e):
                for ee in range(NEXP):
                    if me_src is not None:
                        if ee >= 1:
                            e.wait_ge(mby, ee * NT * 3)
                        for fi in range(NC):
                            e.dma_start(mesb[:, fi * D:(fi + 1) * D],
                                        me_src[ee * D + fi * 128:ee * D + (fi + 1) * 128, :]
                                        ).then_inc(wle, 16)
                        continue
                    if ee >= 1:
                        e.wait_ge(wit, ee * 64)
                        e.wait_ge(mmm, ee * 16)
                    for kc in range(NC):
                        e.dma_start(wisb[:, kc * D:(kc + 1) * D],
                                    Wi_i[ee, kc * 128:(kc + 1) * 128, :]).then_inc(wle, 16)
                        e.dma_start(wesb[:, kc * D:(kc + 1) * D],
                                    Wo_i[ee, kc * 128:(kc + 1) * 128, :]).then_inc(wle, 16)
            @b.vector
            def _(e):
                nv = 0
                for ee in range(NEXP):
                    if me_src is None:
                        for n in range(64):
                            e.wait_ge(wit, ee * 64 + n + 1)
                            ps = (psT0, psT1)[n % 2]
                            oc, ic = n // NC, n % NC
                            e.tensor_copy(wiT[:, ic * D + oc * 128:ic * D + (oc + 1) * 128],
                                          ps[:]).then_inc(sv, 1)
                            nv += 1
                        for j in range(16):
                            fi, half = j // 2, j % 2
                            e.wait_ge(mmm, ee * 16 + j + 1)
                            ps = (psM0, psM1)[j % 2]
                            e.tensor_copy(mesb[:, fi * D + half * 512:fi * D + (half + 1) * 512],
                                          ps[:]).then_inc(sv, 1)
                            nv += 1
                    for tt in range(NT):
                        e.wait_ge(mby, (ee * NT + tt) * 3 + 1)
                        e.tensor_copy(bcf[:], psB[:]).then_inc(sv, 1)
                        nv += 1
                        for kc in range(NC):
                            e.wait_ge(sv, nv)
                            e.tensor_tensor(scb[:, kc * 128:(kc + 1) * 128],
                                            x1T[:, kc * T + tt * 128:kc * T + (tt + 1) * 128],
                                            bcf[:], ALU.mult).then_inc(sv, 1)
                            nv += 1
                        for half in range(2):
                            e.wait_ge(mby, (ee * NT + tt) * 3 + 1 + half + 1)
                            ps = (psY0, psY1)[half]
                            dst = ymoe[:, tt * D + half * 512:tt * D + (half + 1) * 512]
                            if ee == 0:
                                e.tensor_copy(dst, ps[:]).then_inc(sv, 1)
                            else:
                                e.tensor_tensor(dst, dst, ps[:], ALU.add).then_inc(sv, 1)
                            nv += 1
            @b.tensor
            def _(e):
                VPE = (64 + 16 + NT * 11) if me_src is None else (NT * 11)
                MEOFF = 80 if me_src is None else 0
                e.wait_ge(sr, pos_end[NT - 1])
                for ee in range(NEXP):
                    if me_src is None:
                        e.wait_ge(wle, 16 * 2 * NC * (ee + 1))
                        for n in range(64):
                            oc, ic = n // NC, n % NC
                            if n >= 2:
                                e.wait_ge(sv, ee * VPE + n - 1)
                            elif ee >= 1:
                                e.wait_ge(sv, ee * VPE)
                            ps = (psT0, psT1)[n % 2]
                            e.transpose(ps[:], wisb[:, oc * D + ic * 128:oc * D + (ic + 1) * 128],
                                        identb[:]).then_inc(wit, 1)
                        e.wait_ge(sv, ee * VPE + 64)
                        for j in range(16):
                            fi, half = j // 2, j % 2
                            if j >= 2:
                                e.wait_ge(sv, ee * VPE + 64 + j - 1)
                            ps = (psM0, psM1)[j % 2]
                            for hc in range(NC):
                                mi = e.matmul(ps[:], wiT[:, hc * D + fi * 128:hc * D + (fi + 1) * 128],
                                              wesb[:, hc * D + half * 512:hc * D + (half + 1) * 512],
                                              start=(hc == 0), stop=(hc == NC - 1))
                                if hc == NC - 1:
                                    mi.then_inc(mmm, 1)
                    else:
                        e.wait_ge(wle, 16 * NC * (ee + 1))
                    for tt in range(NT):
                        base = ee * VPE + MEOFF + tt * 11
                        if ee * NT + tt >= 1:
                            e.wait_ge(sv, base - 10)
                        e.matmul(psB[:], selsb[:, ee * 128:(ee + 1) * 128],
                                 gT[0:NEXP, tt * 128:(tt + 1) * 128],
                                 start=True, stop=True).then_inc(mby, 1)
                        e.wait_ge(sv, base + 9)
                        for half in range(2):
                            ps = (psY0, psY1)[half]
                            if ee * NT + tt >= 1:
                                e.wait_ge(sv, base - 2 + half + 1)
                            for kc in range(NC):
                                mi = e.matmul(ps[:], scb[:, kc * 128:(kc + 1) * 128],
                                              mesb[:, kc * D + half * 512:kc * D + (half + 1) * 512],
                                              start=(kc == 0), stop=(kc == NC - 1))
                                if kc == NC - 1:
                                    mi.then_inc(mby, 1)


    SVM = NEXP * ((64 + 16 + NT * 11) if me_src is None else 68)

    with nc.Block() as b:
        @b.gpsimd
        def _(e):
            for tt in range(NT + 1):
                par = tt % 2
                if tt < NT:
                    if tt >= 2:
                        e.wait_ge(sv, SVM + (tt - 1) * 2)
                    e.dma_start(xres[:, par * D:(par + 1) * D],
                                x1_i[tt * 128:(tt + 1) * 128, :]).then_inc(rl[par], 16)
                    e.dma_start(yres[:, par * D:(par + 1) * D],
                                y1_i[tt * 128:(tt + 1) * 128, :]).then_inc(rl[par], 16)
                if tt >= 1:
                    st = tt - 1
                    e.wait_ge(sv, SVM + st * 2 + 2)
                    e.dma_start(out_o[st * 128:(st + 1) * 128, :],
                                obuf[:, (st % 2) * D:(st % 2) * D + D]).then_inc(sxo[st % 2], 16)
            for i in range(2):
                e.wait_ge(sxo[i], 16 * 8)
        @b.vector
        def _(e):
            for tt in range(NT):
                par = tt % 2
                e.wait_ge(rl[par], 32 * (tt // 2 + 1))
                if tt >= 2:
                    e.wait_ge(sxo[par], 16 * (tt // 2))
                e.wait_ge(sv, SVM + tt * 2)
                e.tensor_add(tadd[:], ymoe[:, tt * D:(tt + 1) * D],
                             xres[:, par * D:(par + 1) * D]).then_inc(sv, 1)
                e.wait_ge(sv, SVM + tt * 2 + 1)
                e.tensor_add(obuf[:, par * D:(par + 1) * D], tadd[:],
                             yres[:, par * D:(par + 1) * D]).then_inc(sv, 1)
    return


def build_launchF():
    nc = bass.Bass("TRN2", target_bir_lowering=False)
    io = {}
    io["x1_i"] = nc.dram_tensor("x1_i", [T, D], F32, kind="ExternalInput").ap()
    io["y1_i"] = nc.dram_tensor("y1_i", [T, D], F32, kind="ExternalInput").ap()
    io["xn1T_i"] = nc.dram_tensor("xn1T_i", [D, T], BF16, kind="ExternalInput").ap()
    io["Wr_i"] = nc.dram_tensor("Wr_i", [D, NEXP], F32, kind="ExternalInput").ap()
    io["Wi_i"] = nc.dram_tensor("Wi_i", [NEXP, D, D], F32, kind="ExternalInput").ap()
    io["Wo_i"] = nc.dram_tensor("Wo_i", [NEXP, D, D], F32, kind="ExternalInput").ap()
    io["sel_i"] = nc.dram_tensor("sel_i", [NEXP, NEXP, 128], BF16, kind="ExternalInput").ap()
    io["idb"] = nc.dram_tensor("idb", [128, 128], BF16, kind="ExternalInput").ap()
    io["out_o"] = nc.dram_tensor("out_o", [T, D], F32, kind="ExternalOutput").ap()
    st = ExitStack()
    emit_segFf(nc, "Ff_", io, st)
    st.close()
    return nc


def emit_segEF(nc, pre, io, st, me_src=None, ag3=None, ag2=None):
    qTf_i = io["qTf_i"]
    kTf_i = io["kTf_i"]
    qt_i = io["qt_i"]
    kwin_i = io["kwin_i"]
    vwin_i = io["vwin_i"]
    vfull_i = io["vfull_i"]
    Wo_i = io["Wo_i"]
    bo_i = io["bo_i"]
    mtt_i = io["mtt_i"]
    mtp_i = io["mtp_i"]
    ntt_i = io["ntt_i"]
    ntp_i = io["ntp_i"]
    idb = io["idb"]
    x1_i = io["x1_i"]
    y1_io = io["y1_io"]
    xn1T_i = io["xn1T_i"]
    Wr_i = io["Wr_i"]
    out_o = io["out_o"]

    OFFT = [2, 4, 8, 16, 32]

    identb = _sb(st, nc, pre + "identb", [128, 128], BF16)
    masks = _sb(st, nc, pre + "masks", [128, 6 * 128], BF16)
    onesr = _sb(st, nc, pre + "onesr", [1, 128], BF16)
    onescol = _sb(st, nc, pre + "onescol", [128, 1], BF16)
    borow = _sb(st, nc, pre + "borow", [1, D], BF16)
    vsrow = _sb(st, nc, pre + "vsrow", [1, D], F32)
    vsb = _sb(st, nc, pre + "vsb", [1, D], BF16)
    kprevT = _sb(st, nc, pre + "kprevT", [128, NC * 128], BF16)
    qtok = _sb(st, nc, pre + "qtok", [128, 2 * D], BF16)
    attn = _sb(st, nc, pre + "attn", [128, 2 * D], BF16)
    woTsb = _sb(st, nc, pre + "woTsb", [128, NC * D], BF16)
    wtile = _sb(st, nc, pre + "wtile", [128, 2 * 256], BF16)
    zcol = _sb(st, nc, pre + "zcol", [128, 2], F32)
    uall = _sb(st, nc, pre + "uall", [128, 2 * D], F32)
    zall = _sb(st, nc, pre + "zall", [128, 2 * 16], F32)
    pfull = _sb(st, nc, pre + "pfull", [128, D], F32)
    ftmp = _sb(st, nc, pre + "ftmp", [128, D], F32)
    sfar = _sb(st, nc, pre + "sfar", [128, 80], F32)
    efar = _sb(st, nc, pre + "efar", [128, 80], F32)
    wfar = _sb(st, nc, pre + "wfar", [128, 80], F32)
    zf16 = _sb(st, nc, pre + "zf16", [128, 16], F32)
    zz16 = _sb(st, nc, pre + "zz16", [128, 16], F32)
    rz16 = _sb(st, nc, pre + "rz16", [128, 16], F32)
    attnT = _sb(st, nc, pre + "attnT", [128, NC * 128], BF16)
    ybuf = _sb(st, nc, pre + "ybuf", [128, 2 * 512], F32)
    gall = _sb(st, nc, pre + "gall", [128, NT * NEXP], F32)
    wrsb = _sb(st, nc, pre + "wrsb", [128, NC * NEXP], BF16)
    RW = NT * NEXP
    rcol = _sb(st, nc, pre + "rcol", [128, RW], F32)
    rsh = _sb(st, nc, pre + "rsh", [128, RW], F32)
    rexp = _sb(st, nc, pre + "rexp", [128, RW], F32)
    rmaxc = _sb(st, nc, pre + "rmaxc", [128, NT], F32)
    rsum = _sb(st, nc, pre + "rsum", [128, NT], F32)
    rrec = _sb(st, nc, pre + "rrec", [128, NT], F32)
    m1c = _sb(st, nc, pre + "m1c", [128, NT], F32)
    m2c = _sb(st, nc, pre + "m2c", [128, NT], F32)
    eq1 = _sb(st, nc, pre + "eq1", [128, RW], F32)
    eq2 = _sb(st, nc, pre + "eq2", [128, RW], F32)
    cs1 = _sb(st, nc, pre + "cs1", [128, RW], F32)
    cs2 = _sb(st, nc, pre + "cs2", [128, RW], F32)
    cs3 = _sb(st, nc, pre + "cs3", [128, RW], F32)
    excl = _sb(st, nc, pre + "excl", [128, RW], F32)
    izt = _sb(st, nc, pre + "izt", [128, RW], F32)
    first = _sb(st, nc, pre + "first", [128, RW], F32)
    second = _sb(st, nc, pre + "second", [128, RW], F32)
    maskd = _sb(st, nc, pre + "maskd", [128, RW], F32)
    selt = _sb(st, nc, pre + "selt", [128, RW], F32)
    gma = _sb(st, nc, pre + "gma", [128, RW], F32)
    pst = ExitStack()
    wosb = _sb(pst, nc, pre + "wosb", [128, NC * D], BF16)
    vf = _sb(pst, nc, pre + "vf", [128, 2 * D], BF16)
    kw31 = _sb(pst, nc, pre + "kw31", [128, D], BF16)
    x1Tr = _sb(pst, nc, pre + "x1Tr", [128, 2 * NC * 128], BF16)

    psN0 = _ps(st, nc, pre + "psN0", [128, 256], F32)
    psN1 = _ps(st, nc, pre + "psN1", [128, 256], F32)
    psU0 = _ps(st, nc, pre + "psU0", [128, DH + 1], F32)
    psU1 = _ps(st, nc, pre + "psU1", [128, DH + 1], F32)
    psY0 = _ps(st, nc, pre + "psY0", [128, 512], F32)
    psY1 = _ps(st, nc, pre + "psY1", [128, 512], F32)
    psO0 = _ps(st, nc, pre + "psO0", [128, 512], F32)
    psO1 = _ps(st, nc, pre + "psO1", [128, 512], F32)

    il = nc.alloc_semaphore(pre + "il")
    wl = nc.alloc_semaphore(pre + "wl")
    wrl = nc.alloc_semaphore(pre + "wrl")
    wcs = nc.alloc_semaphore(pre + "wcs")
    ol = nc.alloc_semaphore(pre + "ol")
    vfl = [nc.alloc_semaphore(f"{pre}vfl{i}") for i in range(2)]
    klw = [nc.alloc_semaphore(f"{pre}klw{i}") for i in range(2)]
    vlw = [nc.alloc_semaphore(f"{pre}vlw{i}") for i in range(2)]
    ktl = [nc.alloc_semaphore(f"{pre}ktl{i}") for i in range(3)]
    xrl = [nc.alloc_semaphore(f"{pre}xrl{i}") for i in range(2)]
    xsl = [nc.alloc_semaphore(f"{pre}xsl{i}") for i in range(2)]
    sa = nc.alloc_semaphore(pre + "sa")
    sar = nc.alloc_semaphore(pre + "sar")
    sfx = nc.alloc_semaphore(pre + "sfx")
    sv = nc.alloc_semaphore(pre + "sv")
    sr = nc.alloc_semaphore(pre + "sr")
    svf = nc.alloc_semaphore(pre + "svf")
    mm = nc.alloc_semaphore(pre + "mm")
    mr = nc.alloc_semaphore(pre + "mr")
    mby = nc.alloc_semaphore(pre + "mby")
    mo = nc.alloc_semaphore(pre + "mo")
    wot = nc.alloc_semaphore(pre + "wot")
    mvs = nc.alloc_semaphore(pre + "mvs")
    vss = nc.alloc_semaphore(pre + "vss")
    wle = nc.alloc_semaphore(pre + "wle")
    atp = nc.alloc_semaphore(pre + "atp")
    sva = nc.alloc_semaphore(pre + "sva")
    svy = nc.alloc_semaphore(pre + "svy")
    syo = [nc.alloc_semaphore(f"{pre}syo{i}") for i in range(2)]
    rlx = [nc.alloc_semaphore(f"{pre}rlx{i}") for i in range(2)]
    sxo = [nc.alloc_semaphore(f"{pre}sxo{i}") for i in range(2)]
    svt = nc.alloc_semaphore(pre + "svt")

    IL_N = 22
    WOV = 64 + 8
    VE2 = 16 * 2 + 25
    VH0 = WOV

    def VB(t):
        return VH0 + t * VE2

    SVH = VH0 + NT * VE2

    with nc.Block() as b:
        @b.gpsimd
        def _(e):
            e.dma_start(identb[:], idb[:]).then_inc(il, 16)
            e.dma_start(masks[:, 0:128], mtt_i[:]).then_inc(il, 16)
            e.dma_start(masks[:, 128:256], mtp_i[:]).then_inc(il, 16)
            e.dma_start(masks[:, 256:384], ntt_i[:]).then_inc(il, 16)
            e.dma_start(masks[:, 384:512], ntp_i[:]).then_inc(il, 16)
            e.dma_start(borow[:], bo_i[:]).then_inc(il, 16)
            for oc in range(NC):
                e.dma_start(wosb[:, oc * D:(oc + 1) * D],
                            Wo_i[oc * 128:(oc + 1) * 128, :]).then_inc(il, 16)
            for kc in range(NC):
                e.dma_start(wrsb[:, kc * NEXP:(kc + 1) * NEXP],
                            Wr_i[kc * 128:(kc + 1) * 128, :]).then_inc(wrl, 16)
        @b.sync
        def _(e):
            if ag2 is not None:
                e.wait_ge(ag2[0], ag2[1])
                pid = e.partition_id()
                woff = (pid % 4) * (T * D)
                ktp_i, vtp_i = io["ktp_i"], io["vtp_i"]
                e.dma_start(kwin_i, bass.AP(ktp_i.tensor, woff, [[D, 3 * T], [1, D]])
                            ).then_inc(wcs, 16)
                e.dma_start(vwin_i, bass.AP(vtp_i.tensor, woff, [[D, 3 * T], [1, D]])
                            ).then_inc(wcs, 16)
                e.wait_ge(wcs, 32)
            e.dma_start(kw31[:], kwin_i[31 * 128:32 * 128, :]).then_inc(wl, 16)
        @b.vector
        def _(e):
            e.memset(onesr[:], 1.0)
            e.memset(onescol[:], 1.0).then_inc(ol, 1)
            e.wait_ge(il, 16 * 14)
            e.tensor_scalar(masks[:, 512:640], masks[:, 0:128], 1.0, 3200.0,
                            ALU.subtract, ALU.mult).then_inc(ol, 1)
            e.wait_ge(ol, 2)
            e.tensor_scalar(masks[:, 640:768], masks[:, 128:256], 1.0, 3200.0,
                            ALU.subtract, ALU.mult).then_inc(ol, 1)
        @b.tensor
        def _(e):
            e.wait_ge(il, 16 * 14)
            psT0 = psN0.bitcast(BF16)[:, 0:128]
            n = 0
            for oc in range(NC):
                for ic in range(NC):
                    e.transpose(psT0, wosb[:, oc * D + ic * 128:oc * D + (ic + 1) * 128],
                                identb[:]).then_inc(wot, 1)
                    e.wait_ge(sv, n + 1)
                    n += 1
            e.wait_ge(wl, 16)
            for kc in range(NC):
                e.transpose(psT0, kw31[:, kc * 128:(kc + 1) * 128],
                            identb[:]).then_inc(wot, 1)
                e.wait_ge(sv, 64 + kc + 1)
        @b.vector
        def _(e):
            psT0 = psN0.bitcast(BF16)[:, 0:128]
            n = 0
            for oc in range(NC):
                for ic in range(NC):
                    e.wait_ge(wot, n + 1)
                    e.tensor_copy(woTsb[:, ic * D + oc * 128:ic * D + (oc + 1) * 128],
                                  psT0).then_inc(sv, 1)
                    n += 1
            for kc in range(NC):
                e.wait_ge(wot, 64 + kc + 1)
                e.tensor_copy(kprevT[:, kc * 128:(kc + 1) * 128], psT0).then_inc(sv, 1)

    with nc.Block() as b:
        @b.gpsimd
        def _(e):
            if ag2 is not None:
                e.wait_ge(ag2[0], ag2[1])
            for i in range(SEQ // 128):
                if i >= 2:
                    e.wait_ge(mvs, i - 1)
                e.dma_start(vf[:, (i % 2) * D:(i % 2 + 1) * D],
                            vfull_i[i * 128:(i + 1) * 128, :]).then_inc(vfl[i % 2], 16)
        @b.tensor
        def _(e):
            e.wait_ge(ol, 1)
            for i in range(SEQ // 128):
                e.wait_ge(vfl[i % 2], 16 * (i // 2 + 1))
                first_ = (i == 0)
                last = (i == SEQ // 128 - 1)
                e.matmul(psY0[0:1, :], onescol[:], vf[:, (i % 2) * D:(i % 2) * D + 512],
                         start=first_, stop=last)
                e.matmul(psY1[0:1, :], onescol[:], vf[:, (i % 2) * D + 512:(i % 2 + 1) * D],
                         start=first_, stop=last).then_inc(mvs, 1)
        @b.vector
        def _(e):
            e.wait_ge(mvs, SEQ // 128)
            e.tensor_copy(vsrow[0:1, 0:512], psY0[0:1, :]).then_inc(vss, 1)
            e.tensor_copy(vsrow[0:1, 512:1024], psY1[0:1, :]).then_inc(vss, 1)
            e.wait_ge(vss, 2)
            e.tensor_copy(vsb[:], vsrow[:]).then_inc(vss, 1)

    pos_rsh = [0]
    with nc.Block() as b:
        def g3(tile, off=0, w=NEXP):
            base = tile[:]
            return bass.AP(base.tensor, base.offset + off,
                           [[128, 128], [NEXP, NT], [1, w]])

        def bc16(tile):
            return tile[:].unsqueeze(2).broadcast_to((128, NT, NEXP))

        @b.vector
        def _(e):
            nvr = 0
            def stepw(inst):
                nonlocal nvr
                inst.then_inc(sr, 1)
                nvr += 1
            e.wait_ge(mr, NT)
            stepw(e.tensor_copy(rcol[:], psO0[:, 0:128]))
            e.wait_ge(sr, nvr)
            stepw(e.tensor_reduce(rmaxc[:], g3(rcol), AX.X, ALU.max))
            e.wait_ge(sr, nvr)
            stepw(e.tensor_tensor(g3(rsh), g3(rcol), bc16(rmaxc), ALU.subtract))
            pos_rsh[0] = nvr
            e.wait_ge(sar, 1)
            stepw(e.tensor_reduce(rsum[:], g3(rexp), AX.X, ALU.add))
            e.wait_ge(sr, nvr)
            stepw(e.reciprocal(rrec[:], rsum[:]))
            stepw(e.tensor_reduce(m1c[:], g3(rexp), AX.X, ALU.max))
            e.wait_ge(sr, nvr)
            stepw(e.tensor_tensor(g3(eq1), g3(rexp), bc16(m1c), ALU.is_ge))
            e.wait_ge(sr, nvr)
            stepw(e.tensor_copy(g3(cs1, 0, 1), g3(eq1, 0, 1)))
            stepw(e.tensor_tensor(g3(cs1, 1, 7), g3(eq1, 1, 7), g3(eq1, 0, 7), ALU.add))
            e.wait_ge(sr, nvr)
            stepw(e.tensor_copy(g3(cs2, 0, 2), g3(cs1, 0, 2)))
            stepw(e.tensor_tensor(g3(cs2, 2, 6), g3(cs1, 2, 6), g3(cs1, 0, 6), ALU.add))
            e.wait_ge(sr, nvr)
            stepw(e.tensor_copy(g3(cs3, 0, 4), g3(cs2, 0, 4)))
            stepw(e.tensor_tensor(g3(cs3, 4, 4), g3(cs2, 4, 4), g3(cs2, 0, 4), ALU.add))
            e.wait_ge(sr, nvr)
            stepw(e.tensor_sub(excl[:], cs3[:], eq1[:]))
            e.wait_ge(sr, nvr)
            stepw(e.tensor_single_scalar(izt[:], excl[:], 0.5, ALU.is_lt))
            e.wait_ge(sr, nvr)
            stepw(e.tensor_mul(first[:], eq1[:], izt[:]))
            e.wait_ge(sr, nvr)
            stepw(e.scalar_tensor_tensor(maskd[:], first[:], -1e30, rexp[:],
                                         ALU.mult, ALU.add))
            e.wait_ge(sr, nvr)
            stepw(e.tensor_reduce(m2c[:], g3(maskd), AX.X, ALU.max))
            e.wait_ge(sr, nvr)
            stepw(e.tensor_tensor(g3(eq2), g3(maskd), bc16(m2c), ALU.is_ge))
            e.wait_ge(sr, nvr)
            stepw(e.tensor_copy(g3(cs1, 0, 1), g3(eq2, 0, 1)))
            stepw(e.tensor_tensor(g3(cs1, 1, 7), g3(eq2, 1, 7), g3(eq2, 0, 7), ALU.add))
            e.wait_ge(sr, nvr)
            stepw(e.tensor_copy(g3(cs2, 0, 2), g3(cs1, 0, 2)))
            stepw(e.tensor_tensor(g3(cs2, 2, 6), g3(cs1, 2, 6), g3(cs1, 0, 6), ALU.add))
            e.wait_ge(sr, nvr)
            stepw(e.tensor_copy(g3(cs3, 0, 4), g3(cs2, 0, 4)))
            stepw(e.tensor_tensor(g3(cs3, 4, 4), g3(cs2, 4, 4), g3(cs2, 0, 4), ALU.add))
            e.wait_ge(sr, nvr)
            stepw(e.tensor_sub(excl[:], cs3[:], eq2[:]))
            e.wait_ge(sr, nvr)
            stepw(e.tensor_single_scalar(izt[:], excl[:], 0.5, ALU.is_lt))
            e.wait_ge(sr, nvr)
            stepw(e.tensor_mul(second[:], eq2[:], izt[:]))
            e.wait_ge(sr, nvr)
            stepw(e.tensor_add(selt[:], first[:], second[:]))
            e.wait_ge(sr, nvr)
            stepw(e.tensor_mul(gma[:], selt[:], rexp[:]))
            e.wait_ge(sr, nvr)
            stepw(e.tensor_tensor(g3(gall), g3(gma), bc16(rrec), ALU.mult))
        @b.scalar
        def _(e):
            e.wait_ge(sr, pos_rsh[0])
            e.activation(rexp[:], rsh[:], AF.Exp).then_inc(sar, 1)
        @b.tensor
        def _(e):
            e.wait_ge(wrl, 16 * NC)
            for tt in range(NT):
                e.wait_ge(xrl[tt % 2], 128 * (tt // 2 + 1))
                for kc in range(NC):
                    mi = e.matmul(psO0[:, tt * NEXP:(tt + 1) * NEXP],
                                  x1Tr[:, (tt % 2) * 1024 + kc * 128:(tt % 2) * 1024 + (kc + 1) * 128],
                                  wrsb[:, kc * NEXP:(kc + 1) * NEXP],
                                  start=(kc == 0), stop=(kc == NC - 1))
                    if kc == NC - 1:
                        mi.then_inc(mr, 1)
        @b.gpsimd
        def _(e):
            for tt in range(NT):
                if tt >= 2:
                    e.wait_ge(mr, tt - 1)
                for kc in range(NC):
                    e.dma_start(x1Tr[:, (tt % 2) * 1024 + kc * 128:(tt % 2) * 1024 + (kc + 1) * 128],
                                xn1T_i[kc * 128:(kc + 1) * 128, tt * 128:(tt + 1) * 128]
                                ).then_inc(xrl[tt % 2], 16)

    pst.close()
    kfb = _sb(st, nc, pre + "kfb", [128, 2 * 5 * D], BF16)
    vfb = _sb(st, nc, pre + "vfb", [128, 2 * 5 * D], BF16)
    vnear = _sb(st, nc, pre + "vnear", [128, 2 * 2 * D], BF16)
    qts = _sb(st, nc, pre + "qts", [128, 2 * NC * 128], BF16)
    kts = _sb(st, nc, pre + "kts", [128, 3 * NC * 128], BF16)
    x1s = _sb(st, nc, pre + "x1s", [128, 2 * NC * 512], BF16)
    mesb = _sb(st, nc, pre + "mesb", [128, NC * D], BF16)
    ymoe = _sb(st, nc, pre + "ymoe", [128, NT * D], BF16)
    ymt = _sb(st, nc, pre + "ymt", [128, 2 * 512], BF16)

    def emit_outproj(e, j):
        e.wait_ge(sv, VB(j + 1))
        e.wait_ge(sa, min(16 * (j + 2), 16 * NT))
        if j == 0:
            e.wait_ge(sr, 1)
        for kc in range(NC):
            na = j * NC + kc
            if na >= 2:
                e.wait_ge(sva, na - 1)
            ps = (psN0, psN1)[kc % 2].bitcast(BF16)
            e.transpose(ps[:, 0:128],
                        attn[:, (j % 2) * D + kc * 128:(j % 2) * D + (kc + 1) * 128],
                        identb[:]).then_inc(atp, 1)
        e.wait_ge(sva, (j + 1) * NC)
        for half in range(2):
            pso = (psO0, psO1)[half]
            if j >= 1:
                e.wait_ge(svy, 2 * (j - 1) + half + 1)
            for kc in range(NC):
                e.matmul(pso[:], attnT[:, kc * 128:(kc + 1) * 128],
                         woTsb[:, kc * D + half * 512:kc * D + (half + 1) * 512],
                         start=(kc == 0), stop=False)
            e.matmul(pso[:], onesr[:],
                     borow[0:1, half * 512:(half + 1) * 512],
                     start=False, stop=True).then_inc(mo, 1)

    def emit_outproj_evac(e, j):
        for kc in range(NC):
            na = j * NC + kc
            e.wait_ge(atp, na + 1)
            ps = (psN0, psN1)[kc % 2].bitcast(BF16)
            e.tensor_copy(attnT[:, kc * 128:(kc + 1) * 128], ps[:, 0:128]).then_inc(sva, 1)
        for half in range(2):
            e.wait_ge(mo, 2 * j + half + 1)
            if j >= 1:
                e.wait_ge(syo[half], 16 * j)
            pso = (psO0, psO1)[half]
            e.tensor_copy(ybuf[:, half * 512:(half + 1) * 512], pso[:]).then_inc(svy, 1)

    with nc.Block() as b:
        @b.gpsimd
        def _(e):
            for t in range(NT):
                par = t % 2
                if t == 0:
                    e.wait_ge(mvs, SEQ // 128)
                    e.wait_ge(wot, 72)
                    e.wait_ge(mr, NT)
                    if ag2 is not None:
                        e.wait_ge(wcs, 32)
                if t >= 2:
                    e.wait_ge(sv, VB(t - 1))
                ksrc = lambda widx: kwin_i[widx * 128:(widx + 1) * 128, :]
                vsrc = lambda widx: vwin_i[widx * 128:(widx + 1) * 128, :]
                for i5, off in enumerate(OFFT):
                    widx = 32 + t - off
                    e.dma_start(kfb[:, (par * 5 + i5) * D:(par * 5 + i5 + 1) * D],
                                ksrc(widx)).then_inc(klw[par], 16)
                    e.dma_start(vfb[:, (par * 5 + i5) * D:(par * 5 + i5 + 1) * D],
                                vsrc(widx)).then_inc(klw[par], 16)
                for j, widx in enumerate((32 + t, 31 + t)):
                    e.dma_start(vnear[:, (par * 2 + j) * D:(par * 2 + j + 1) * D],
                                vsrc(widx)).then_inc(vlw[par], 16)
                e.dma_start(qtok[:, par * D:(par + 1) * D],
                            qt_i[t * 128:(t + 1) * 128, :]).then_inc(vlw[par], 16)
                for pr in range(NP):
                    e.dma_start(qts[:, par * 1024 + pr * 128:par * 1024 + (pr + 1) * 128],
                                qTf_i[pr * 128:(pr + 1) * 128, t * 128:(t + 1) * 128]
                                ).then_inc(vlw[par], 16)
                for pr in range(NP):
                    e.dma_start(kts[:, (t % 3) * 1024 + pr * 128:(t % 3) * 1024 + (pr + 1) * 128],
                                kTf_i[pr * 128:(pr + 1) * 128, t * 128:(t + 1) * 128]
                                ).then_inc(ktl[t % 3], 16)
        @b.sync
        def _(e):
            if ag3 is not None:
                e.wait_ge(ag3[0], ag3[1])
            e.wait_ge(mvs, SEQ // 128)
            e.wait_ge(wot, 72)
            e.wait_ge(mr, NT)
            for fi in range(NC):
                e.dma_start(mesb[:, fi * D:(fi + 1) * D],
                            me_src[fi * 128:(fi + 1) * 128, :]).then_inc(wle, 16)
            for c in (0, 1):
                tc = c % 4
                for kc in range(NC):
                    e.dma_start(x1s[:, (c % 2) * 4096 + kc * 512:(c % 2) * 4096 + (kc + 1) * 512],
                                xn1T_i[kc * 128:(kc + 1) * 128, tc * 512:(tc + 1) * 512]
                                ).then_inc(xsl[c % 2], 16)
            for t in range(NT):
                if t >= 3:
                    jj = t - 3
                    e.wait_ge(svy, 2 * jj + 2)
                    for half in range(2):
                        e.dma_start(y1_io[jj * 128:(jj + 1) * 128, half * 512:(half + 1) * 512],
                                    ybuf[:, half * 512:(half + 1) * 512]).then_inc(syo[half], 16)
                if t % 2 == 1:
                    ee_next = (t + 1) // 2
                    if ee_next <= NEXP - 1:
                        e.wait_ge(mby, 32 * ee_next)
                        for fi in range(NC):
                            e.dma_start(mesb[:, fi * D:(fi + 1) * D],
                                        me_src[ee_next * D + fi * 128:ee_next * D + (fi + 1) * 128, :]
                                        ).then_inc(wle, 16)
                for j2 in (0, 1):
                    c = 2 * t + 2 + j2
                    if c < 32:
                        e.wait_ge(mby, (c - 1) * 8)
                        tc = c % 4
                        for kc in range(NC):
                            e.dma_start(x1s[:, (c % 2) * 4096 + kc * 512:(c % 2) * 4096 + (kc + 1) * 512],
                                        xn1T_i[kc * 128:(kc + 1) * 128, tc * 512:(tc + 1) * 512]
                                        ).then_inc(xsl[c % 2], 16)
            for jj in (NT - 3, NT - 2, NT - 1):
                e.wait_ge(svy, 2 * jj + 2)
                for half in range(2):
                    e.dma_start(y1_io[jj * 128:(jj + 1) * 128, half * 512:(half + 1) * 512],
                                ybuf[:, half * 512:(half + 1) * 512]).then_inc(syo[half], 16)
        pos_qk = {}
        pos_num = {}
        pos_z = {}

        @b.tensor
        def _(e):
            e.wait_ge(vss, 3)
            mmc = 0

            def minc(inst, book, n):
                nonlocal mmc
                inst.then_inc(mm, 1)
                mmc += 1
                book[n] = mmc

            def emit_qk(n):
                t, h = n // 16, n % 16
                par = t % 2
                hp, hh = h // 2, h % 2
                psN = (psN0, psN1)[n % 2]
                if n >= 2:
                    e.wait_ge(sa, n - 1)
                qmv = qts[hh * 64:(hh + 1) * 64, par * 1024 + hp * 128:par * 1024 + (hp + 1) * 128]
                e.matmul(psN[:, 0:128],
                         kts[hh * 64:(hh + 1) * 64,
                             (t % 3) * 1024 + hp * 128:(t % 3) * 1024 + (hp + 1) * 128],
                         qmv, start=True, stop=False)
                e.matmul(psN[:, 0:128], identb[:], masks[:, 512:640],
                         start=False, stop=True)
                if t == 0:
                    lhs_prev = kprevT[hh * 64:(hh + 1) * 64, hp * 128:(hp + 1) * 128]
                else:
                    lhs_prev = kts[hh * 64:(hh + 1) * 64,
                                   ((t - 1) % 3) * 1024 + hp * 128:((t - 1) % 3) * 1024 + (hp + 1) * 128]
                e.matmul(psN[:, 128:256], lhs_prev, qmv, start=True, stop=False)
                minc(e.matmul(psN[:, 128:256], identb[:], masks[:, 640:768],
                              start=False, stop=True), pos_qk, n)

            def emit_numz(n):
                t, h = n // 16, n % 16
                par = t % 2
                psU = (psU0, psU1)[n % 2]
                e.wait_ge(sa, n + 1)
                if n >= 2:
                    tp2, hp2 = (n - 2) // 16, (n - 2) % 16
                    e.wait_ge(sv, VB(tp2) + 2 * hp2 + 2)
                wpar = n % 2
                vtt = vnear[:, (par * 2 + 0) * D + h * DH:(par * 2 + 0) * D + (h + 1) * DH]
                vtp_ = vnear[:, (par * 2 + 1) * D + h * DH:(par * 2 + 1) * D + (h + 1) * DH]
                wm0 = wtile[:, wpar * 256:wpar * 256 + 128]
                wm1 = wtile[:, wpar * 256 + 128:wpar * 256 + 256]
                e.matmul(psU[:, 0:DH], wm0, vtt, start=True, stop=False)
                e.matmul(psU[:, 0:DH], wm1, vtp_, start=False, stop=False)
                e.matmul(psU[:, 0:DH], masks[:, 256:384], vtt, start=False, stop=False)
                e.matmul(psU[:, 0:DH], masks[:, 384:512], vtp_, start=False, stop=False)
                minc(e.matmul(psU[:, 0:DH], onesr[:], vsb[0:1, h * DH:(h + 1) * DH],
                              start=False, stop=True), pos_num, n)

            for t in range(NT):
                par = t % 2
                if t >= 2:
                    emit_outproj(e, t - 2)
                e.wait_ge(vlw[par], 176 * (t // 2 + 1))
                e.wait_ge(ktl[t % 3], 128 * (t // 3 + 1))
                for h in range(H):
                    n = t * 16 + h
                    if t >= 2 and h == 0:
                        e.wait_ge(sva, NC * (t - 1))
                    if n == 0:
                        e.wait_ge(ol, 3)
                    emit_qk(n)
                    if h >= 1:
                        emit_numz(n - 1)
                emit_numz(t * 16 + 15)
                for j2 in (0, 1):
                    c = 2 * t + j2
                    ee, tc = c // 4, c % 4
                    e.wait_ge(wle, 128 * (ee + 1))
                    e.wait_ge(xsl[c % 2], 128 * (c // 2 + 1))
                    if c == 0:
                        e.wait_ge(vss, 3)
                    for tsub in range(4):
                        g = c * 4 + tsub
                        for half in range(2):
                            ps = (psY0, psY1)[half]
                            if g >= 1:
                                e.wait_ge(svf, (g - 1) * 4 + 2 * half + 1)
                            for kc in range(NC):
                                mi = e.matmul(ps[:],
                                              x1s[:, (c % 2) * 4096 + kc * 512 + tsub * 128:
                                                  (c % 2) * 4096 + kc * 512 + tsub * 128 + 128],
                                              mesb[:, kc * D + half * 512:kc * D + (half + 1) * 512],
                                              start=(kc == 0), stop=(kc == NC - 1))
                                if kc == NC - 1:
                                    mi.then_inc(mby, 1)
            emit_outproj(e, NT - 2)
            emit_outproj(e, NT - 1)
        @b.scalar
        def _(e):
            PH = 2 * H
            for t in range(NT):
                for h in range(H):
                    n = t * 16 + h
                    psN = (psN0, psN1)[n % 2]
                    e.wait_ge(mm, pos_qk[n])
                    if n >= 2:
                        tp2, hp2 = (n - 2) // 16, (n - 2) % 16
                        e.wait_ge(sv, VB(tp2) + 2 * hp2 + 2)
                    e.activation(wtile[:, (n % 2) * 256:(n % 2 + 1) * 256], psN[:, 0:256],
                                 AF.Exp, scale=ISC,
                                 accum_out=zcol[:, n % 2:n % 2 + 1]).then_inc(sa, 1)
                e.wait_ge(sv, VB(t) + PH + 10)
                if t >= 2:
                    e.wait_ge(sv, VB(t - 2) + PH + 11)
                e.activation(efar[:], sfar[:], AF.Exp, scale=ISC).then_inc(sfx, 1)
        @b.vector
        def _(e):
            for t in range(NT):
                par = t % 2
                if t >= 2:
                    emit_outproj_evac(e, t - 2)
                nv = VB(t)
                def stepw(inst):
                    nonlocal nv
                    inst.then_inc(sv, 1)
                    nv += 1
                for h in range(H):
                    n = t * 16 + h
                    e.wait_ge(mm, pos_num[n])
                    stepw(e.tensor_copy(uall[:, par * D + h * DH:par * D + (h + 1) * DH],
                                        (psU0, psU1)[n % 2][:, 0:DH]))
                    e.wait_ge(sa, n + 1)
                    stepw(e.tensor_copy(zall[:, par * 16 + h:par * 16 + h + 1],
                                        zcol[:, n % 2:n % 2 + 1]))
                e.wait_ge(klw[par], 160 * (t // 2 + 1))
                e.wait_ge(vlw[par], 176 * (t // 2 + 1))
                for i5 in range(5):
                    kslice = kfb[:, (par * 5 + i5) * D:(par * 5 + i5 + 1) * D]
                    e.wait_ge(sv, nv)
                    stepw(e.tensor_tensor(pfull[:], qtok[:, par * D:(par + 1) * D], kslice,
                                          ALU.mult))
                    e.wait_ge(sv, nv)
                    stepw(e.tensor_reduce(
                        bass.AP(sfar[:].tensor, sfar[:].offset + i5,
                                [[80, 128], [5, 16]]),
                        pfull[:].rearrange("p (hh d) -> p hh d", d=DH),
                        AX.X, ALU.add))
                e.wait_ge(sfx, t + 1)
                stepw(e.tensor_single_scalar(wfar[:], efar[:], 1.0, ALU.subtract))
                for i5 in range(5):
                    vslice = vfb[:, (par * 5 + i5) * D:(par * 5 + i5 + 1) * D]
                    wb = bass.AP(wfar[:].tensor, wfar[:].offset + i5,
                                 [[80, 128], [5, 16], [0, DH]])
                    e.wait_ge(sv, nv)
                    stepw(e.tensor_tensor(ftmp[:].rearrange("p (hh d) -> p hh d", d=DH),
                                          vslice.rearrange("p (hh d) -> p hh d", d=DH),
                                          wb, ALU.mult))
                    e.wait_ge(sv, nv)
                    stepw(e.tensor_tensor(uall[:, par * D:(par + 1) * D],
                                          uall[:, par * D:(par + 1) * D],
                                          ftmp[:], ALU.add))
                e.wait_ge(sv, nv)
                stepw(e.tensor_reduce(zf16[:], wfar[:].rearrange("p (hh d5) -> p hh d5", d5=5),
                                      AX.X, ALU.add))
                e.wait_ge(sv, nv)
                stepw(e.scalar_tensor_tensor(zz16[:], zf16[:], float(SEQ - 9),
                                             zall[:, par * 16:(par + 1) * 16],
                                             ALU.add, ALU.add))
                e.wait_ge(sv, nv)
                stepw(e.reciprocal(rz16[:], zz16[:]))
                rzb = rz16[:].unsqueeze(2).broadcast_to((128, 16, DH))
                e.wait_ge(sv, nv)
                if t >= 2:
                    e.wait_ge(atp, NC * (t - 1))
                stepw(e.tensor_tensor(attn[:, par * D:(par + 1) * D]
                                      .rearrange("p (hh d) -> p hh d", d=DH),
                                      uall[:, par * D:(par + 1) * D]
                                      .rearrange("p (hh d) -> p hh d", d=DH),
                                      rzb, ALU.mult))
                assert nv == VB(t) + VE2, (nv, VB(t), VE2)
                for j2 in (0, 1):
                    c = 2 * t + j2
                    ee, tc = c // 4, c % 4
                    for tsub in range(4):
                        g = c * 4 + tsub
                        tt = tc * 4 + tsub
                        for half in range(2):
                            e.wait_ge(mby, c * 8 + tsub * 2 + half + 1)
                            if g >= 1:
                                e.wait_ge(svf, (g - 1) * 4 + 2 * half + 2)
                            gcol = gall[:, tt * NEXP + ee:tt * NEXP + ee + 1]
                            e.tensor_scalar(ymt[:, half * 512:(half + 1) * 512],
                                            (psY0, psY1)[half][:], gcol, None,
                                            ALU.mult).then_inc(svf, 1)
                            e.wait_ge(svf, g * 4 + 2 * half + 1)
                            dst = ymoe[:, tt * D + half * 512:tt * D + (half + 1) * 512]
                            if ee == 0:
                                e.tensor_copy(dst, ymt[:, half * 512:(half + 1) * 512]
                                              ).then_inc(svf, 1)
                            else:
                                e.tensor_tensor(dst, dst, ymt[:, half * 512:(half + 1) * 512],
                                                ALU.add).then_inc(svf, 1)
            emit_outproj_evac(e, NT - 2)
            emit_outproj_evac(e, NT - 1)

    kbc = kfb.bitcast(F32)
    vbc = vfb.bitcast(F32)
    with nc.Block() as b:
        @b.gpsimd
        def _(e):
            e.wait_ge(sv, SVH)
            e.wait_ge(svf, 32 * 16)
            e.wait_ge(syo[0], 16 * NT)
            e.wait_ge(syo[1], 16 * NT)
            for tt in range(NT + 1):
                par = tt % 2
                if tt < NT:
                    if tt >= 2:
                        e.wait_ge(svt, 3 * (tt - 2) + 1)
                    e.dma_start(kbc[:, par * D:(par + 1) * D],
                                x1_i[tt * 128:(tt + 1) * 128, :]).then_inc(rlx[par], 16)
                    e.dma_start(kbc[:, (2 + par) * D:(3 + par) * D],
                                y1_io[tt * 128:(tt + 1) * 128, :]).then_inc(rlx[par], 16)
                if tt >= 1:
                    jj = tt - 1
                    e.wait_ge(svt, 3 * jj + 3)
                    e.dma_start(out_o[jj * 128:(jj + 1) * 128, :],
                                vbc[:, (jj % 2) * D:(jj % 2 + 1) * D]).then_inc(sxo[jj % 2], 16)
            for i in range(2):
                e.wait_ge(sxo[i], 16 * 8)
        @b.vector
        def _(e):
            for tt in range(NT):
                par = tt % 2
                e.wait_ge(rlx[par], 32 * (tt // 2 + 1))
                if tt >= 2:
                    e.wait_ge(sxo[par], 16 * (tt // 2))
                if tt >= 1:
                    e.wait_ge(svt, 3 * tt)
                e.tensor_tensor(kbc[:, 4 * D:5 * D], kbc[:, par * D:(par + 1) * D],
                                kbc[:, (2 + par) * D:(3 + par) * D], ALU.add).then_inc(svt, 1)
                e.tensor_copy(vbc[:, 2 * D:3 * D], ymoe[:, tt * D:(tt + 1) * D]).then_inc(svt, 1)
                e.wait_ge(svt, 3 * tt + 2)
                e.tensor_tensor(vbc[:, par * D:(par + 1) * D], kbc[:, 4 * D:5 * D],
                                vbc[:, 2 * D:3 * D], ALU.add).then_inc(svt, 1)
    return


def build_launchEF():
    nc = bass.Bass("TRN2", target_bir_lowering=False)
    io = {}
    io["qTf_i"] = nc.dram_tensor("qTf_i", [D, T], BF16, kind="ExternalInput").ap()
    io["kTf_i"] = nc.dram_tensor("kTf_i", [D, T], BF16, kind="ExternalInput").ap()
    io["qt_i"] = nc.dram_tensor("qt_i", [T, D], BF16, kind="ExternalInput").ap()
    io["kwin_i"] = nc.dram_tensor("kwin_i", [3 * T, D], BF16, kind="ExternalInput").ap()
    io["vwin_i"] = nc.dram_tensor("vwin_i", [3 * T, D], BF16, kind="ExternalInput").ap()
    io["vfull_i"] = nc.dram_tensor("vfull_i", [SEQ, D], BF16, kind="ExternalInput").ap()
    io["Wo_i"] = nc.dram_tensor("Wo_i", [D, D], F32, kind="ExternalInput").ap()
    io["bo_i"] = nc.dram_tensor("bo_i", [1, D], F32, kind="ExternalInput").ap()
    io["mtt_i"] = nc.dram_tensor("mtt_i", [128, 128], BF16, kind="ExternalInput").ap()
    io["mtp_i"] = nc.dram_tensor("mtp_i", [128, 128], BF16, kind="ExternalInput").ap()
    io["ntt_i"] = nc.dram_tensor("ntt_i", [128, 128], BF16, kind="ExternalInput").ap()
    io["ntp_i"] = nc.dram_tensor("ntp_i", [128, 128], BF16, kind="ExternalInput").ap()
    io["idb"] = nc.dram_tensor("idb", [128, 128], BF16, kind="ExternalInput").ap()
    io["x1_i"] = nc.dram_tensor("x1_i", [T, D], F32, kind="ExternalInput").ap()
    io["xn1T_i"] = nc.dram_tensor("xn1T_i", [D, T], BF16, kind="ExternalInput").ap()
    io["Wr_i"] = nc.dram_tensor("Wr_i", [D, NEXP], F32, kind="ExternalInput").ap()
    meg = nc.dram_tensor("meg_i", [NEXP * D, D], BF16, kind="ExternalInput").ap()
    io["y1_io"] = nc.dram_tensor("y1_b", [T, D], F32, kind="Internal").ap()
    io["out_o"] = nc.dram_tensor("out_o", [T, D], F32, kind="ExternalOutput").ap()
    st = ExitStack()
    emit_segEF(nc, "EF_", io, st, me_src=meg)
    st.close()
    return nc


def _selectors():
    sel = np.zeros((NEXP, NEXP, 128), np.float32)
    for ee in range(NEXP):
        sel[ee, ee, :] = 1.0
    return sel.astype(BFDT)


def _launchF(inputs, resC, resD, resE):
    ident_bf = np.eye(128, dtype=np.float32).astype(BFDT)
    ncF = build_launchF()
    in_maps = []
    for c in range(8):
        in_maps.append({
            "x1_i": np.asarray(resC.results[c]["x1_o"]),
            "y1_i": np.asarray(resE.results[c]["y1_o"]),
            "xn1T_i": np.asarray(resD.results[c]["xn1T_o"]),
            "Wr_i": np.asarray(inputs["L1_ffn_Wr"], np.float32),
            "Wi_i": np.asarray(inputs["L1_ffn_Wi"], np.float32),
            "Wo_i": np.asarray(inputs["L1_ffn_Wo"], np.float32),
            "sel_i": _selectors(),
            "idb": ident_bf,
        })
    return run_bass_kernel_spmd(ncF, in_maps, core_ids=list(range(8)))



_A_CACHE = {}


def _launchA(inputs):
    x = np.asarray(inputs["L0_attn_x"], np.float32)
    xs = _shard_rows(x)
    cos128, sin128s = _rope_tables()
    ident_bf = np.eye(128, dtype=np.float32).astype(BFDT)
    ncA = build_launchA()
    in_maps = []
    for c in range(8):
        r = c % 4
        in_maps.append({
            "xin": xs[c],
            "wqkv": np.asarray(inputs["L0_attn_w_qkv"], np.float32),
            "gam": np.asarray(inputs["L0_attn_ln_gamma"], np.float32).reshape(1, D),
            "bet": np.asarray(inputs["L0_attn_ln_beta"], np.float32).reshape(1, D),
            "cosd": np.ascontiguousarray(cos128[:, r * T:(r + 1) * T]),
            "sind": np.ascontiguousarray(sin128s[:, r * T:(r + 1) * T]),
            "idb": ident_bf,
        })
    return run_bass_kernel_spmd(ncA, in_maps, core_ids=list(range(8)))





def emit_segMe(nc, pre, io, st):
    Wi_i, Wo_i, idb, me_o = io["WiM_i"], io["WoM_i"], io["idb"], io["me_o"]
    wisb = _sb(st, nc, pre + "wisb", [128, NC * D], BF16)
    wesb = _sb(st, nc, pre + "wesb", [128, NC * D], BF16)
    wiT = _sb(st, nc, pre + "wiT", [128, NC * D], BF16)
    mesb = _sb(st, nc, pre + "mesb", [128, NC * D], BF16)
    identb = _sb(st, nc, pre + "identb", [128, 128], BF16)
    psT0 = _ps(st, nc, pre + "psT0", [128, 128], BF16)
    psT1 = _ps(st, nc, pre + "psT1", [128, 128], BF16)
    psM0 = _ps(st, nc, pre + "psM0", [128, 512], F32)
    psM1 = _ps(st, nc, pre + "psM1", [128, 512], F32)
    il = nc.alloc_semaphore(pre + "il")
    wle = nc.alloc_semaphore(pre + "wle")
    wit = nc.alloc_semaphore(pre + "wit")
    mmm = nc.alloc_semaphore(pre + "mmm")
    sv = nc.alloc_semaphore(pre + "sv")
    smo = nc.alloc_semaphore(pre + "smo")
    with nc.Block() as b:
        @b.gpsimd
        def _(e):
            e.dma_start(identb[:], idb[:]).then_inc(il, 16)
            pid = e.partition_id()
            off = pid * (D * D)
            for kc in range(NC):
                swi = bass.AP(Wi_i.tensor, off + kc * 128 * D, [[D, 128], [1, D]])
                swo = bass.AP(Wo_i.tensor, off + kc * 128 * D, [[D, 128], [1, D]])
                e.dma_start(wisb[:, kc * D:(kc + 1) * D], swi).then_inc(wle, 16)
                e.dma_start(wesb[:, kc * D:(kc + 1) * D], swo).then_inc(wle, 16)
            e.wait_ge(sv, 80)
            for fi in range(NC):
                e.dma_start(me_o[fi * 128:(fi + 1) * 128, :],
                            mesb[:, fi * D:(fi + 1) * D]).then_inc(smo, 16)
            e.wait_ge(smo, 16 * NC)
        @b.tensor
        def _(e):
            e.wait_ge(il, 16)
            e.wait_ge(wle, 16 * 2 * NC)
            for n in range(64):
                oc, ic = n // NC, n % NC
                if n >= 2:
                    e.wait_ge(sv, n - 1)
                ps = (psT0, psT1)[n % 2]
                e.transpose(ps[:], wisb[:, oc * D + ic * 128:oc * D + (ic + 1) * 128],
                            identb[:]).then_inc(wit, 1)
            e.wait_ge(sv, 64)
            for j in range(16):
                fi, half = j // 2, j % 2
                if j >= 2:
                    e.wait_ge(sv, 64 + j - 1)
                ps = (psM0, psM1)[j % 2]
                for hc in range(NC):
                    mi = e.matmul(ps[:], wiT[:, hc * D + fi * 128:hc * D + (fi + 1) * 128],
                                  wesb[:, hc * D + half * 512:hc * D + (half + 1) * 512],
                                  start=(hc == 0), stop=(hc == NC - 1))
                    if hc == NC - 1:
                        mi.then_inc(mmm, 1)
        @b.vector
        def _(e):
            for n in range(64):
                e.wait_ge(wit, n + 1)
                ps = (psT0, psT1)[n % 2]
                oc, ic = n // NC, n % NC
                e.tensor_copy(wiT[:, ic * D + oc * 128:ic * D + (oc + 1) * 128],
                              ps[:]).then_inc(sv, 1)
            for j in range(16):
                fi, half = j // 2, j % 2
                e.wait_ge(mmm, j + 1)
                ps = (psM0, psM1)[j % 2]
                e.tensor_copy(mesb[:, fi * D + half * 512:fi * D + (half + 1) * 512],
                              ps[:]).then_inc(sv, 1)
    return



def build_fused():
    nc = bass.Bass("TRN2", target_bir_lowering=False)
    EI = lambda n, s, d: nc.dram_tensor(n, s, d, kind="ExternalInput").ap()
    IN = lambda n, s, d: nc.dram_tensor(n, s, d, kind="Internal").ap()
    xin = EI("xin", [T, D], F32)
    wqkv = EI("wqkv", [D, 3 * D], F32)
    gam = EI("gam", [1, D], F32)
    bet = EI("bet", [1, D], F32)
    cosd = EI("cosd", [128, T], F32)
    sind = EI("sind", [128, T], F32)
    idb = EI("idb", [128, 128], BF16)
    wout = EI("wout", [D, D], F32)
    wtiles = EI("wtiles", [7, 128, FFI], F32)
    Wq_i = EI("Wq_i", [D, D], F32)
    Wk_i = EI("Wk_i", [D, D], F32)
    Wv_i = EI("Wv_i", [D, D], F32)
    bq_i = EI("bq_i", [1, D], F32)
    bk_i = EI("bk_i", [1, D], F32)
    bv_i = EI("bv_i", [1, D], F32)
    WoL1 = EI("WoL1", [D, D], F32)
    bo_i = EI("bo_i", [1, D], F32)
    mtt_i = EI("mtt_i", [128, 128], BF16)
    mtp_i = EI("mtp_i", [128, 128], BF16)
    ntt_i = EI("ntt_i", [128, 128], BF16)
    ntp_i = EI("ntp_i", [128, 128], BF16)
    Wr_i = EI("Wr_i", [D, NEXP], F32)
    WiM = EI("WiM", [NEXP, D, D], F32)
    WoM = EI("WoM", [NEXP, D, D], F32)
    sel_i = EI("sel_i", [NEXP, NEXP, 128], BF16)
    out_o = nc.dram_tensor("out_o", [T, D], F32, kind="ExternalOutput").ap()

    qT_r = IN("qT_r", [D, T], BF16)
    kT_r = IN("kT_r", [D, T], BF16)
    v_t = IN("v_t", [T, D], BF16)
    xn_t = IN("xn_t", [T, D], F32)
    kTg = IN("kTg", [4 * D, T], BF16)
    vg = IN("vg", [SEQ, D], BF16)
    yat = IN("yat", [T, D], F32)
    wq_c = IN("wq_c", [7, FFI, 128], BF16)
    wjobs = IN("wjobs", [56, FFI, 128], BF16)
    h_t = IN("h_t", [T, FFI], F32)
    x1 = IN("x1", [T, D], F32)
    qt = IN("qt", [T, D], BF16)
    kt = IN("kt", [T, D], BF16)
    vt = IN("vt", [T, D], BF16)
    qTf = IN("qTf", [D, T], BF16)
    kTf = IN("kTf", [D, T], BF16)
    xn1T = IN("xn1T", [D, T], BF16)
    ktp = IN("ktp", [2 * T + SEQ, D], BF16)
    vtp = IN("vtp", [2 * T + SEQ, D], BF16)
    y1 = IN("y1", [T, D], F32)
    me = IN("me", [D, D], BF16)
    meg = IN("meg", [NEXP * D, D], BF16)

    ALU_ = ALU

    def seg(fn, pre, io, **kw):
        snap = nc.snapshot_sems()
        st = ExitStack()
        fn(nc, pre, io, st, **kw)
        st.close()
        nc.all_engine_barrier()
        nc.clear_and_free_semaphores(nc.allocated_since(snap))

    def coll_issue(pre, pairs, groups):
        cc = nc.alloc_semaphore(pre + "cc")
        with nc.Block() as b:
            @b.gpsimd
            def _(e):
                for src_ap, dst_ap in pairs:
                    e.collective_compute("AllGather", ALU_.bypass,
                                         replica_groups=groups,
                                         ins=[src_ap.opt()], outs=[dst_ap.opt()]).then_inc(cc, 1)
        return cc, len(pairs)

    def coll_wait(tok):
        cc, n = tok
        with nc.Block() as b:
            @b.gpsimd
            def _(e):
                e.wait_ge(cc, n)
        nc.all_engine_barrier()

    grp4 = [[0, 1, 2, 3], [4, 5, 6, 7]]
    grp8 = [list(range(8))]

    zst = ExitStack()
    zt = _sb(zst, nc, "zpad", [128, D], BF16)
    zs = nc.alloc_semaphore("zps")
    zd = nc.alloc_semaphore("zpd")
    with nc.Block() as b:
        @b.vector
        def _(e):
            e.memset(zt[:], 0.0).then_inc(zs, 1)
        @b.gpsimd
        def _(e):
            e.wait_ge(zs, 1)
            for i in range(2 * T // 128):
                e.dma_start(ktp[i * 128:(i + 1) * 128, :], zt[:]).then_inc(zd, 16)
                e.dma_start(vtp[i * 128:(i + 1) * 128, :], zt[:]).then_inc(zd, 16)
            e.wait_ge(zd, 16 * 2 * (2 * T // 128))
    zst.close()
    nc.all_engine_barrier()
    nc.clear_and_free_semaphores([zs, zd])

    ag1k_cc = nc.alloc_semaphore("ag1kcc")

    def _ag1k_hook(sro):
        with nc.Block() as b:
            @b.gpsimd
            def _(e):
                for i in range(2):
                    e.wait_ge(sro[i], 16 * 32)
                e.collective_compute("AllGather", ALU_.bypass, replica_groups=grp4,
                                     ins=[kT_r.opt()], outs=[kTg.opt()]
                                     ).then_inc(ag1k_cc, 1)

    seg(emit_segA, "a_", {"xin": xin, "wqkv": wqkv, "gam": gam, "bet": bet,
                          "cosd": cosd, "sind": sind, "idb": idb,
                          "qT_o": qT_r, "kT_o": kT_r, "v_o": v_t, "xn_o": xn_t},
        ag1k_hook=_ag1k_hook)
    ag1v = coll_issue("ag1v", [(v_t, vg)], grp4)
    seg(emit_segC0, "c0_", {"wtiles": wtiles, "idb": idb, "wq_o": wq_c})
    agw = coll_issue("agw", [(wq_c, wjobs)], grp8)
    seg(emit_segMe, "me_", {"WiM_i": WiM, "WoM_i": WoM, "idb": idb, "me_o": me})
    ag3 = coll_issue("ag3", [(me, meg)], grp8)
    seg(emit_segB, "b_", {"qT_i": qT_r, "kT_i": kTg, "v_i": vg, "wout": wout,
                          "yat_o": yat}, gathered=True,
        ag1k=(ag1k_cc, 1), ag1v=ag1v)
    coll_wait(agw)
    seg(emit_segCa, "ca_", {"xn_i": xn_t, "wjobs": wjobs, "idb": idb, "h_o": h_t},
        from_jobs=True)
    seg(emit_segCb, "cb_", {"h_i": h_t, "wjobs": wjobs, "xin": xin, "yat_i": yat,
                            "idb": idb, "x1_o": x1}, from_jobs=True)
    snapD = nc.snapshot_sems()
    stD = ExitStack()
    ag2box = []

    def _ag2_hook():
        ag2box.append(coll_issue("ag2", [(kt, ktp[2 * T:2 * T + SEQ, :]),
                                         (vt, vtp[2 * T:2 * T + SEQ, :])], grp4))

    emit_segDd(nc, "d_", {"x1_i": x1, "Wq_i": Wq_i, "Wk_i": Wk_i, "Wv_i": Wv_i,
                          "bq_i": bq_i, "bk_i": bk_i, "bv_i": bv_i, "idb": idb,
                          "qt_o": qt, "kt_o": kt, "vt_o": vt, "qTf_o": qTf,
                          "kTf_o": kTf, "xn1T_o": xn1T}, stD, ag2_hook=_ag2_hook)
    stD.close()
    ag2 = ag2box[0]
    nc.all_engine_barrier()
    nc.clear_and_free_semaphores(nc.allocated_since(snapD)[:-1])
    kwin_l = IN("kwin_l", [3 * T, D], BF16)
    vwin_l = IN("vwin_l", [3 * T, D], BF16)
    seg(emit_segEF, "e_", {"qTf_i": qTf, "kTf_i": kTf, "qt_i": qt,
                           "kwin_i": kwin_l, "vwin_i": vwin_l,
                           "vfull_i": vtp[2 * T:2 * T + SEQ, :],
                           "ktp_i": ktp, "vtp_i": vtp,
                           "Wo_i": WoL1, "bo_i": bo_i, "mtt_i": mtt_i,
                           "mtp_i": mtp_i, "ntt_i": ntt_i, "ntp_i": ntp_i,
                           "idb": idb, "x1_i": x1, "y1_io": y1,
                           "xn1T_i": xn1T, "Wr_i": Wr_i, "out_o": out_o},
        me_src=meg, ag3=ag3, ag2=ag2)
    return nc


def _c0_jobs(inputs):
    wg = np.asarray(inputs["L0_ffn_w_gate"], np.float32)
    wu = np.asarray(inputs["L0_ffn_w_up"], np.float32)
    wd = np.asarray(inputs["L0_ffn_w_down"], np.float32)
    jobs = []
    for w in (wg, wu):
        for t in range(NFI):
            tile = np.zeros((128, FFI), np.float32)
            tile[:, 0:D] = w[t * 128:(t + 1) * 128, :]
            jobs.append(tile)
    for t in range(NC):
        jobs.append(np.ascontiguousarray(wd[t * 128:(t + 1) * 128, :]))
    while len(jobs) < 56:
        jobs.append(np.zeros((128, FFI), np.float32))
    return jobs


def kernel_fused(inputs):
    x = np.asarray(inputs["L0_attn_x"], np.float32)
    xs = _shard_rows(x)
    cos128, sin128s = _rope_tables()
    ident_bf = np.eye(128, dtype=np.float32).astype(BFDT)
    mtt, mtp = _l1_masks()
    jobs = _c0_jobs(inputs)
    nc = build_fused()
    in_maps = []
    for c in range(8):
        r = c % 4
        in_maps.append({
            "xin": xs[c],
            "wqkv": np.asarray(inputs["L0_attn_w_qkv"], np.float32),
            "gam": np.asarray(inputs["L0_attn_ln_gamma"], np.float32).reshape(1, D),
            "bet": np.asarray(inputs["L0_attn_ln_beta"], np.float32).reshape(1, D),
            "cosd": np.ascontiguousarray(cos128[:, r * T:(r + 1) * T]),
            "sind": np.ascontiguousarray(sin128s[:, r * T:(r + 1) * T]),
            "idb": ident_bf,
            "wout": np.asarray(inputs["L0_attn_w_out"], np.float32),
            "wtiles": np.stack(jobs[c * 7:(c + 1) * 7]),
            "Wq_i": np.asarray(inputs["L1_attn_Wq"], np.float32),
            "Wk_i": np.asarray(inputs["L1_attn_Wk"], np.float32),
            "Wv_i": np.asarray(inputs["L1_attn_Wv"], np.float32),
            "bq_i": np.asarray(inputs["L1_attn_bq"], np.float32).reshape(1, D),
            "bk_i": np.asarray(inputs["L1_attn_bk"], np.float32).reshape(1, D),
            "bv_i": np.asarray(inputs["L1_attn_bv"], np.float32).reshape(1, D),
            "WoL1": np.asarray(inputs["L1_attn_Wo"], np.float32),
            "bo_i": np.asarray(inputs["L1_attn_bo"], np.float32).reshape(1, D),
            "mtt_i": mtt.astype(BFDT),
            "mtp_i": mtp.astype(BFDT),
            "ntt_i": (-mtt).astype(BFDT),
            "ntp_i": (-mtp).astype(BFDT),
            "Wr_i": np.asarray(inputs["L1_ffn_Wr"], np.float32),
            "WiM": np.asarray(inputs["L1_ffn_Wi"], np.float32),
            "WoM": np.asarray(inputs["L1_ffn_Wo"], np.float32),
            "sel_i": _selectors(),
        })
    res = run_bass_kernel_spmd(nc, in_maps, core_ids=list(range(8)))
    out = np.empty((2, SEQ, D), np.float32)
    for c in range(8):
        b, r = c // 4, c % 4
        out[b, r * T:(r + 1) * T, :] = np.asarray(res.results[c]["out_o"], np.float32)
    return out


def _l1_masks():
    Dfull = {1, 2, 4, 8, 16, 32, 64, 128}
    mtt = np.zeros((128, 128), np.float32)
    mtp = np.zeros((128, 128), np.float32)
    for rk in range(128):
        for rq in range(128):
            d = rq - rk
            if d == 0 or (d > 0 and d in Dfull):
                mtt[rk, rq] = 1.0
            if (rq + 128 - rk) in Dfull:
                mtp[rk, rq] = 1.0
    return mtt, mtp


def _launchE(resC, resD, inputs):
    ident_bf = np.eye(128, dtype=np.float32).astype(BFDT)
    mtt, mtp = _l1_masks()
    kts = [np.asarray(resD.results[c]["kt_o"]) for c in range(8)]
    vts = [np.asarray(resD.results[c]["vt_o"]) for c in range(8)]
    ncE = build_launchE()
    in_maps = []
    for c in range(8):
        b, r = c // 4, c % 4
        grp = list(range(b * 4, b * 4 + 4))
        k_full = np.concatenate([kts[g] for g in grp], axis=0)
        v_full = np.concatenate([vts[g] for g in grp], axis=0)
        kwin = np.zeros((3 * T, D), BFDT)
        vwin = np.zeros((3 * T, D), BFDT)
        lo = r * T - 2 * T
        g0, g1 = max(0, lo), min(SEQ, lo + 3 * T)
        kwin[g0 - lo:g1 - lo] = k_full[g0:g1]
        vwin[g0 - lo:g1 - lo] = v_full[g0:g1]
        in_maps.append({
            "qTf_i": np.asarray(resD.results[c]["qTf_o"]),
            "kTf_i": np.asarray(resD.results[c]["kTf_o"]),
            "qt_i": np.asarray(resD.results[c]["qt_o"]),
            "kwin_i": kwin,
            "vwin_i": vwin,
            "vfull_i": np.ascontiguousarray(v_full),
            "Wo_i": np.asarray(inputs["L1_attn_Wo"], np.float32),
            "bo_i": np.asarray(inputs["L1_attn_bo"], np.float32).reshape(1, D),
            "mtt_i": mtt.astype(BFDT),
            "mtp_i": mtp.astype(BFDT),
            "ntt_i": (-mtt).astype(BFDT),
            "ntp_i": (-mtp).astype(BFDT),
            "idb": ident_bf,
        })
    return run_bass_kernel_spmd(ncE, in_maps, core_ids=list(range(8)))


FUSED = True


def kernel(**inputs):
    if FUSED:
        return kernel_fused(inputs)
    resA = _launchA(inputs)
    resB = _launchB(resA, inputs)
    wgT, wuT, wdT = _launchC0(inputs)
    resC, _ = _launchC(inputs, resA, resB, wgT, wuT, wdT)
    resD = _launchD(resC, inputs)
    resE = _launchE(resC, resD, inputs)
    resF = _launchF(inputs, resC, resD, resE)
    out = np.empty((2, SEQ, D), np.float32)
    for c in range(8):
        b, r = c // 4, c % 4
        out[b, r * T:(r + 1) * T, :] = np.asarray(resF.results[c]["out_o"], np.float32)
    return out


# version 18 (final)
# speedup vs baseline: 1.1741x; 1.1741x over previous
import numpy as np
import ml_dtypes
import concourse.bass as bass
import concourse.mybir as mybir
from concourse.bass_utils import run_bass_kernel_spmd

F32 = mybir.dt.float32
BF16 = mybir.dt.bfloat16
AF = mybir.ActivationFunctionType
ALU = mybir.AluOpType
AX = mybir.AxisListType
BFDT = ml_dtypes.bfloat16

D = 1024
T = 2048
NT = T // 128
NC = D // 128
H = 16
DH = 64
NP = 8
SEQ = 8192
EPS = 1e-5
ISC = 0.125
FFI = 2816
NFI = FFI // 128
NEXP = 8

QMID = [0.25, 0.75, 1.25, 1.75, 2.5, 3.5, 5.0]
QINC = [0.5, 0.5, 0.5, 0.5, 1.0, 1.0, 2.0]



from contextlib import ExitStack


def _sb(st, nc, name, shape, dtype):
    return st.enter_context(nc.sbuf_tensor(name, shape, dtype))


def _ps(st, nc, name, shape, dtype):
    return st.enter_context(nc.psum_tensor(name, shape, dtype))



def _rope_tables():
    inv = (10000.0 ** (-(np.arange(0, 64, 2, dtype=np.float32)) / 64.0)).astype(np.float32)
    pos = np.arange(SEQ, dtype=np.float32) - 1.0
    ang = pos[None, :] * inv[:, None]
    cos32 = np.cos(ang).astype(np.float32)
    sin32 = np.sin(ang).astype(np.float32)
    cos32[:, 0] = 1.0
    sin32[:, 0] = 0.0
    cos128 = np.tile(cos32, (4, 1))
    sin128 = np.tile(sin32, (4, 1))
    sgn = np.ones((128, 1), np.float32)
    sgn[(np.arange(128) % 64) < 32] = -1.0
    return cos128, sin128 * sgn


def _shard_rows(x):
    out = []
    for c in range(8):
        b, r = c // 4, c % 4
        out.append(np.ascontiguousarray(x[b, r * T:(r + 1) * T, :]))
    return out



def _emit_ln_cols(e_vec, cols, base, c, sv):
    raise NotImplementedError



def emit_segA(nc, pre, io, st, ag1k_hook=None):
    xin = io["xin"]
    wqkv = io["wqkv"]
    gam = io["gam"]
    bet = io["bet"]
    cosd = io["cosd"]
    sind = io["sind"]
    idb = io["idb"]

    qT_o = io["qT_o"]
    kT_o = io["kT_o"]
    v_o = io["v_o"]
    xn_o = io["xn_o"]

    xa = _sb(st, nc, pre + "xa", [128, 2 * D], F32)
    xnt = _sb(st, nc, pre + "xnt", [128, 2 * D], F32)
    xsc = _sb(st, nc, pre + "xsc", [128, D], F32)
    xsc2 = _sb(st, nc, pre + "xsc2", [128, D], F32)
    xb2 = _sb(st, nc, pre + "xb2", [128, 2 * D], BF16)
    dump = _sb(st, nc, pre + "dump", [128, D], F32)
    dump2 = _sb(st, nc, pre + "dump2", [128, D], F32)
    cols = _sb(st, nc, pre + "cols", [128, 32], F32)
    onesr = _sb(st, nc, pre + "onesr", [1, 128], F32)
    epscol = _sb(st, nc, pre + "epscol", [128, 1], F32)
    gbc = _sb(st, nc, pre + "gbc", [128, D], F32)
    bbc = _sb(st, nc, pre + "bbc", [128, D], F32)
    cosT = _sb(st, nc, pre + "cosT", [128, T], F32)
    sinT = _sb(st, nc, pre + "sinT", [128, T], F32)
    identb = _sb(st, nc, pre + "identb", [128, 128], BF16)
    x2T = _sb(st, nc, pre + "x2T", [128, NC * T], BF16)
    wqk = _sb(st, nc, pre + "wqk", [128, NC * 2 * D], BF16)
    wv = _sb(st, nc, pre + "wv", [128, NC * D], BF16)
    rot = _sb(st, nc, pre + "rot", [128, 512], F32)
    rtmp = _sb(st, nc, pre + "rtmp", [128, 512], F32)
    rtmp2 = _sb(st, nc, pre + "rtmp2", [128, 512], F32)
    obuf = _sb(st, nc, pre + "obuf", [128, 2 * 512], BF16)
    vbuf = _sb(st, nc, pre + "vbuf", [128, 4 * 512], BF16)

    psG = _ps(st, nc, pre + "psG", [128, 512], F32)
    psT0 = _ps(st, nc, pre + "psT0", [128, 128], BF16)
    psT1 = _ps(st, nc, pre + "psT1", [128, 128], BF16)
    psQ0 = _ps(st, nc, pre + "psQ0", [128, 512], F32)
    psQ1 = _ps(st, nc, pre + "psQ1", [128, 512], F32)
    psV0 = _ps(st, nc, pre + "psV0", [128, 512], F32)
    psV1 = _ps(st, nc, pre + "psV1", [128, 512], F32)

    wl = nc.alloc_semaphore(pre + "wl")
    xl = [nc.alloc_semaphore(f"{pre}xl{i}") for i in range(2)]
    sxn = [nc.alloc_semaphore(f"{pre}sxn{i}") for i in range(2)]
    sro = [nc.alloc_semaphore(f"{pre}sro{i}") for i in range(2)]
    svo = [nc.alloc_semaphore(f"{pre}svo{i}") for i in range(4)]
    sa = nc.alloc_semaphore(pre + "sa")
    sv = nc.alloc_semaphore(pre + "sv")
    sq = nc.alloc_semaphore(pre + "sq")
    tp = nc.alloc_semaphore(pre + "tp")
    mm = nc.alloc_semaphore(pre + "mm")

    WL_ALL = 16 * (5 + 16)

    with nc.Block() as b:
        @b.gpsimd
        def _(e):
            e.dma_start(gbc[0:1, :], gam[:]).then_inc(wl, 16)
            e.dma_start(bbc[0:1, :], bet[:]).then_inc(wl, 16)
            e.dma_start(cosT[:], cosd[:]).then_inc(wl, 16)
            e.dma_start(sinT[:], sind[:]).then_inc(wl, 16)
            e.dma_start(identb[:], idb[:]).then_inc(wl, 16)
            for kc in range(NC):
                e.dma_start(wqk[:, kc * 2 * D:(kc + 1) * 2 * D],
                            wqkv[kc * 128:(kc + 1) * 128, 0:2 * D]).then_inc(wl, 16)
                e.dma_start(wv[:, kc * D:(kc + 1) * D],
                            wqkv[kc * 128:(kc + 1) * 128, 2 * D:3 * D]).then_inc(wl, 16)
        @b.vector
        def _(e):
            e.memset(epscol[:], EPS)
            e.memset(onesr[:], 1.0).then_inc(sv, 1)
        @b.tensor
        def _(e):
            e.wait_ge(wl, WL_ALL)
            e.wait_ge(sv, 1)
            for j in range(4):
                if j >= 1:
                    e.wait_ge(sv, 1 + j)
                src = (gbc, gbc, bbc, bbc)[j]
                half = j % 2
                e.matmul(psG[:], onesr[:], src[0:1, half * 512:(half + 1) * 512],
                         start=True, stop=True).then_inc(mm, 1)
        @b.vector
        def _(e):
            for j in range(4):
                e.wait_ge(mm, j + 1)
                dst = (gbc, gbc, bbc, bbc)[j]
                half = j % 2
                e.tensor_copy(dst[:, half * 512:(half + 1) * 512], psG[:]).then_inc(sv, 1)

    SVB = 5
    MMB = 4
    VPT = 22

    with nc.Block() as b:
        @b.sync
        def _(e):
            for tt in range(NT):
                if tt >= 2:
                    e.wait_ge(sv, SVB + (tt - 2) * VPT + 6)
                e.dma_start(xa[:, (tt % 2) * D:(tt % 2 + 1) * D],
                            xin[tt * 128:(tt + 1) * 128, :]).then_inc(xl[tt % 2], 16)
        @b.scalar
        def _(e):
            for tt in range(NT):
                par = tt % 2
                o = 16 * par
                xat = xa[:, par * D:(par + 1) * D]
                xn_ = xnt[:, par * D:(par + 1) * D]
                cl = lambda j: cols[:, o + j:o + j + 1]
                e.wait_ge(xl[tt % 2], 16 * (tt // 2 + 1))
                e.activation(dump[:], xat, AF.Copy, accum_out=cl(0)).then_inc(sa, 1)
                e.activation(dump2[:], xat, AF.Square, accum_out=cl(1)).then_inc(sa, 1)
                e.wait_ge(sv, SVB + tt * VPT + 3)
                e.activation(cl(5), cl(4), AF.Sqrt, bias=epscol[:]).then_inc(sq, 1)
                e.wait_ge(sv, SVB + tt * VPT + 6)
                e.activation(dump[:], xn_, AF.Copy, accum_out=cl(8)).then_inc(sa, 1)
                e.activation(dump2[:], xn_, AF.Square, accum_out=cl(9)).then_inc(sa, 1)
                e.wait_ge(sv, SVB + tt * VPT + 9)
                e.activation(cl(13), cl(12), AF.Sqrt, bias=epscol[:]).then_inc(sq, 1)
        @b.vector
        def _(e):
            c = 1.0 / D
            for tt in range(NT):
                par = tt % 2
                o = 16 * par
                xat = xa[:, par * D:(par + 1) * D]
                xn_ = xnt[:, par * D:(par + 1) * D]
                xb2t = xb2[:, par * D:(par + 1) * D]
                cl = lambda j: cols[:, o + j:o + j + 1]
                B0 = SVB + tt * VPT
                e.wait_ge(sa, tt * 4 + 2)
                e.tensor_scalar_mul(cl(2), cl(0), c).then_inc(sv, 1)
                e.wait_ge(sv, B0 + 1)
                e.tensor_mul(cl(3), cl(2), cl(2)).then_inc(sv, 1)
                e.wait_ge(sv, B0 + 2)
                e.scalar_tensor_tensor(cl(4), cl(1), c, cl(3),
                                       ALU.mult, ALU.subtract).then_inc(sv, 1)
                e.wait_ge(sq, tt * 2 + 1)
                e.reciprocal(cl(6), cl(5)).then_inc(sv, 1)
                e.wait_ge(sv, B0 + 4)
                e.scalar_tensor_tensor(cl(7), cl(2), -1.0, cl(6),
                                       ALU.mult, ALU.mult).then_inc(sv, 1)
                if tt >= 2:
                    e.wait_ge(sxn[tt % 2], 16 * (tt // 2))
                e.wait_ge(sv, B0 + 5)
                e.tensor_scalar(xn_, xat, cl(6), cl(7), ALU.mult,
                                ALU.add).then_inc(sv, 1)
                e.wait_ge(sa, tt * 4 + 4)
                e.tensor_scalar_mul(cl(10), cl(8), c).then_inc(sv, 1)
                e.wait_ge(sv, B0 + 7)
                e.tensor_mul(cl(11), cl(10), cl(10)).then_inc(sv, 1)
                e.wait_ge(sv, B0 + 8)
                e.scalar_tensor_tensor(cl(12), cl(9), c, cl(11),
                                       ALU.mult, ALU.subtract).then_inc(sv, 1)
                e.wait_ge(sq, tt * 2 + 2)
                e.reciprocal(cl(14), cl(13)).then_inc(sv, 1)
                e.wait_ge(sv, B0 + 10)
                e.scalar_tensor_tensor(cl(15), cl(10), -1.0, cl(14),
                                       ALU.mult, ALU.mult).then_inc(sv, 1)
                e.wait_ge(sv, B0 + 11)
                e.tensor_scalar(xsc[:], xn_, cl(14), cl(15), ALU.mult,
                                ALU.add).then_inc(sv, 1)
                e.wait_ge(sv, B0 + 12)
                e.tensor_mul(xsc2[:], xsc[:], gbc[:]).then_inc(sv, 1)
                e.wait_ge(sv, B0 + 13)
                e.tensor_add(xb2t, xsc2[:], bbc[:]).then_inc(sv, 1)
                for kc in range(NC):
                    e.wait_ge(tp, tt * NC + kc + 1)
                    ps = (psT0, psT1)[kc % 2]
                    e.tensor_copy(x2T[:, kc * T + tt * 128:kc * T + (tt + 1) * 128],
                                  ps[:]).then_inc(sv, 1)
        @b.gpsimd
        def _(e):
            for tt in range(NT):
                e.wait_ge(sv, SVB + tt * VPT + 6)
                e.dma_start(xn_o[tt * 128:(tt + 1) * 128, :],
                            xnt[:, (tt % 2) * D:(tt % 2 + 1) * D]).then_inc(sxn[tt % 2], 16)
        @b.tensor
        def _(e):
            for tt in range(NT):
                e.wait_ge(sv, SVB + tt * VPT + 14)
                for kc in range(NC):
                    if kc >= 2:
                        e.wait_ge(sv, SVB + tt * VPT + 14 + kc - 1)
                    ps = (psT0, psT1)[kc % 2]
                    e.transpose(ps[:], xb2[:, (tt % 2) * D + kc * 128:(tt % 2) * D + (kc + 1) * 128],
                                identb[:]).then_inc(tp, 1)

    SV1 = SVB + NT * VPT
    MM1 = MMB
    SO1 = NT * 16

    with nc.Block() as b:
        @b.tensor
        def _(e):
            e.wait_ge(sv, SV1)
            n = 0
            for ot in range(16):
                for tc in range(4):
                    ps = (psQ0, psQ1)[n % 2]
                    if n >= 2:
                        e.wait_ge(sv, SV1 + (n - 1) * 7)
                    for kc in range(NC):
                        m = e.matmul(ps[:], wqk[:, kc * 2 * D + ot * 128:kc * 2 * D + (ot + 1) * 128],
                                     x2T[:, kc * T + tc * 512:kc * T + (tc + 1) * 512],
                                     start=(kc == 0), stop=(kc == NC - 1))
                        if kc == NC - 1:
                            m.then_inc(mm, 1)
                    n += 1
        @b.vector
        def _(e):
            n = 0
            for ot in range(16):
                for tc in range(4):
                    par = n % 2
                    ps = (psQ0, psQ1)[par]
                    e.wait_ge(mm, MM1 + n + 1)
                    cs = cosT[:, tc * 512:(tc + 1) * 512]
                    sn = sinT[:, tc * 512:(tc + 1) * 512]
                    e.tensor_copy(rot[0:32, :], ps[32:64, :]).then_inc(sv, 1)
                    e.tensor_copy(rot[32:64, :], ps[0:32, :]).then_inc(sv, 1)
                    e.tensor_copy(rot[64:96, :], ps[96:128, :]).then_inc(sv, 1)
                    e.tensor_copy(rot[96:128, :], ps[64:96, :]).then_inc(sv, 1)
                    e.wait_ge(sv, SV1 + n * 7 + 4)
                    e.tensor_mul(rtmp[:], rot[:], sn).then_inc(sv, 1)
                    e.tensor_mul(rtmp2[:], ps[:], cs).then_inc(sv, 1)
                    if n >= 2:
                        e.wait_ge(sro[n % 2], 16 * (n // 2))
                    e.wait_ge(sv, SV1 + n * 7 + 6)
                    e.tensor_add(obuf[:, par * 512:(par + 1) * 512], rtmp2[:],
                                 rtmp[:]).then_inc(sv, 1)
                    n += 1
        @b.sync
        def _(e):
            n = 0
            for ot in range(16):
                for tc in range(4):
                    e.wait_ge(sv, SV1 + (n + 1) * 7)
                    dst = qT_o if ot < 8 else kT_o
                    orow = (ot % 8) * 128
                    e.dma_start(dst[orow:orow + 128, tc * 512:(tc + 1) * 512],
                                obuf[:, (n % 2) * 512:(n % 2 + 1) * 512]).then_inc(sro[n % 2], 16)
                    n += 1


    if ag1k_hook is not None:
        ag1k_hook(sro)

    SV2 = SV1 + 64 * 7
    MM2 = MM1 + 64
    SO2 = SO1 + 64 * 16

    with nc.Block() as b:
        @b.tensor
        def _(e):
            n = 0
            for tt in range(NT):
                for vc in range(2):
                    ps = (psV0, psV1)[n % 2]
                    if n >= 2:
                        e.wait_ge(sv, SV2 + n - 1)
                    for kc in range(NC):
                        m = e.matmul(ps[:], x2T[:, kc * T + tt * 128:kc * T + (tt + 1) * 128],
                                     wv[:, kc * D + vc * 512:kc * D + (vc + 1) * 512],
                                     start=(kc == 0), stop=(kc == NC - 1))
                        if kc == NC - 1:
                            m.then_inc(mm, 1)
                    n += 1
        @b.vector
        def _(e):
            n = 0
            for tt in range(NT):
                for vc in range(2):
                    e.wait_ge(mm, MM2 + n + 1)
                    if n >= 4:
                        e.wait_ge(svo[n % 4], 16 * (n // 4))
                    ps = (psV0, psV1)[n % 2]
                    e.tensor_copy(vbuf[:, (n % 4) * 512:(n % 4 + 1) * 512],
                                  ps[:]).then_inc(sv, 1)
                    n += 1
        @b.sync
        def _(e):
            n = 0
            for tt in range(NT):
                for vc in range(2):
                    e.wait_ge(sv, SV2 + n + 1)
                    e.dma_start(v_o[tt * 128:(tt + 1) * 128, vc * 512:(vc + 1) * 512],
                                vbuf[:, (n % 4) * 512:(n % 4 + 1) * 512]).then_inc(svo[n % 4], 16)
                    n += 1
            for i in range(2):
                e.wait_ge(sxn[i], 16 * 8)
                e.wait_ge(sro[i], 16 * 32)
            for i in range(4):
                e.wait_ge(svo[i], 16 * 8)
    return


def build_launchA():
    nc = bass.Bass("TRN2", target_bir_lowering=False)
    io = {}
    io["xin"] = nc.dram_tensor("xin", [T, D], F32, kind="ExternalInput").ap()
    io["wqkv"] = nc.dram_tensor("wqkv", [D, 3 * D], F32, kind="ExternalInput").ap()
    io["gam"] = nc.dram_tensor("gam", [1, D], F32, kind="ExternalInput").ap()
    io["bet"] = nc.dram_tensor("bet", [1, D], F32, kind="ExternalInput").ap()
    io["cosd"] = nc.dram_tensor("cosd", [128, T], F32, kind="ExternalInput").ap()
    io["sind"] = nc.dram_tensor("sind", [128, T], F32, kind="ExternalInput").ap()
    io["idb"] = nc.dram_tensor("idb", [128, 128], BF16, kind="ExternalInput").ap()
    io["qT_o"] = nc.dram_tensor("qT_o", [D, T], BF16, kind="ExternalOutput").ap()
    io["kT_o"] = nc.dram_tensor("kT_o", [D, T], BF16, kind="ExternalOutput").ap()
    io["v_o"] = nc.dram_tensor("v_o", [T, D], BF16, kind="ExternalOutput").ap()
    io["xn_o"] = nc.dram_tensor("xn_o", [T, D], F32, kind="ExternalOutput").ap()
    st = ExitStack()
    emit_segA(nc, "A_", io, st)
    st.close()
    return nc





def emit_segB(nc, pre, io, st, gathered=False, ag1k=None, ag1v=None):
    qT_i = io["qT_i"]
    kT_i = io["kT_i"]
    v_i = io["v_i"]
    wout = io["wout"]
    yat_o = io["yat_o"]

    qsb = _sb(st, nc, pre + "qsb", [128, NP * T], BF16)
    kpair = _sb(st, nc, pre + "kpair", [128, 3 * SEQ], BF16)
    vaug = _sb(st, nc, pre + "vaug", [128, 3 * 2 * (64 * 65 + 64)], BF16)
    pbuf = _sb(st, nc, pre + "pbuf", [128, 4 * 1024], BF16)
    absb = _sb(st, nc, pre + "absb", [128, NP * T], BF16)
    wosb = _sb(st, nc, pre + "wosb", [128, NC * D], BF16)
    zrow = _sb(st, nc, pre + "zrow", [1, 1024], F32)
    rz = _sb(st, nc, pre + "rz", [1, 1024], F32)
    rzb = _sb(st, nc, pre + "rzb", [64, 1024], F32)
    ones64 = _sb(st, nc, pre + "ones64", [1, 64], F32)
    ybuf = _sb(st, nc, pre + "ybuf", [128, 2 * 512], F32)

    psS0 = _ps(st, nc, pre + "psS0", [128, 1024], F32)
    psS1 = _ps(st, nc, pre + "psS1", [128, 1024], F32)
    psUA = _ps(st, nc, pre + "psUA", [128, 512], F32)
    psUB = _ps(st, nc, pre + "psUB", [128, 512], F32)
    psBC = _ps(st, nc, pre + "psBC", [64, 1024], F32)

    wl = nc.alloc_semaphore(pre + "wl")
    kl = [nc.alloc_semaphore(f"{pre}kl{i}") for i in range(3)]
    vl = [nc.alloc_semaphore(f"{pre}vl{i}") for i in range(3)]
    syo = [nc.alloc_semaphore(f"{pre}syo{i}") for i in range(2)]
    mmq = nc.alloc_semaphore(pre + "mmq")
    se = nc.alloc_semaphore(pre + "se")
    mmv = nc.alloc_semaphore(pre + "mmv")
    sz = nc.alloc_semaphore(pre + "sz")
    mb = nc.alloc_semaphore(pre + "mb")
    szm = nc.alloc_semaphore(pre + "szm")
    mo = nc.alloc_semaphore(pre + "mo")
    svy = nc.alloc_semaphore(pre + "svy")

    WL_ALL = 16 * 16

    with nc.Block() as b:
        @b.sync
        def _(e):
            for pr in range(NP):
                e.dma_start(qsb[:, pr * T:(pr + 1) * T],
                            qT_i[pr * 128:(pr + 1) * 128, :]).then_inc(wl, 16)
        @b.gpsimd
        def _(e):
            for kc in range(NC):
                e.dma_start(wosb[:, kc * D:(kc + 1) * D],
                            wout[kc * 128:(kc + 1) * 128, :]).then_inc(wl, 16)
        @b.vector
        def _(e):
            e.memset(ones64[:], 1.0).then_inc(szm, 1)

    VH = 64 * 65 + 64

    with nc.Block() as b:
        @b.gpsimd
        def _(e):
            for pr in range(NP):
                sl3 = pr % 3
                if pr == 0 and ag1v is not None:
                    e.wait_ge(ag1v[0], ag1v[1])
                e.wait_ge(szm, 1 + 2 * (pr + 1))
                for hh in range(2):
                    h = 2 * pr + hh
                    dst = vaug[:, sl3 * 2 * VH + hh * VH: sl3 * 2 * VH + hh * VH + 64 * 65]
                    dst3 = dst.rearrange("p (kb c) -> p kb c", kb=64)[:, :, 0:64]
                    src = v_i.rearrange("(kb p) (h d) -> p kb h d", p=128, h=16)[:, :, h, :]
                    e.dma_start(dst3, src).then_inc(vl[sl3], 16)
                if pr == 0 and ag1k is not None:
                    e.wait_ge(ag1k[0], ag1k[1])
                if pr >= 3:
                    e.wait_ge(mmq, 256 * (pr - 2))
                if gathered:
                    for rk in range(4):
                        e.dma_start(kpair[:, sl3 * SEQ + rk * T:sl3 * SEQ + (rk + 1) * T],
                                    kT_i[rk * D + pr * 128:rk * D + (pr + 1) * 128, :]
                                    ).then_inc(kl[sl3], 16)
                else:
                    e.dma_start(kpair[:, sl3 * SEQ:(sl3 + 1) * SEQ],
                                kT_i[pr * 128:(pr + 1) * 128, :]).then_inc(kl[sl3], 16)
        @b.vector
        def _(e):
            def vmemset(j):
                s3 = j % 3
                if j >= 3:
                    e.wait_ge(mmv, 256 * (j - 2))
                e.memset(vaug[:, s3 * 2 * VH:s3 * 2 * VH + VH], 1.0).then_inc(szm, 1)
                e.memset(vaug[:, s3 * 2 * VH + VH:s3 * 2 * VH + 2 * VH], 1.0).then_inc(szm, 1)
            for j in range(3):
                vmemset(j)
            for pr in range(NP):
                for qc in range(4):
                    m = pr * 4 + qc
                    e.wait_ge(mmv, 64 * (m + 1))
                    e.wait_ge(sz, 7 * m)
                    e.tensor_copy(zrow[0:1, 0:512], psUA[64:65, :]).then_inc(sz, 1)
                    e.tensor_copy(zrow[0:1, 512:1024], psUB[64:65, :]).then_inc(sz, 1)
                    e.wait_ge(mb, 2 * m)
                    e.wait_ge(sz, 7 * m + 2)
                    e.reciprocal(rz[:], zrow[:]).then_inc(sz, 1)
                    e.wait_ge(mb, 2 * (m + 1))
                    e.tensor_copy(rzb[:, 0:512], psBC[:, 0:512]).then_inc(sz, 1)
                    e.tensor_copy(rzb[:, 512:1024], psBC[:, 512:1024]).then_inc(sz, 1)
                    dstA = absb[0:64, pr * T + qc * 512:pr * T + (qc + 1) * 512]
                    dstB = absb[64:128, pr * T + qc * 512:pr * T + (qc + 1) * 512]
                    e.wait_ge(sz, 7 * m + 5)
                    e.tensor_tensor(dstA, psUA[0:64, :], rzb[:, 0:512],
                                    ALU.mult).then_inc(sz, 1)
                    e.tensor_tensor(dstB, psUB[0:64, :], rzb[:, 512:1024],
                                    ALU.mult).then_inc(sz, 1)
                if pr + 3 < NP:
                    vmemset(pr + 3)
        @b.tensor
        def _(e):
            for pr in range(NP):
                sl3 = pr % 3
                e.wait_ge(kl[sl3], (64 if gathered else 16) * (pr // 3 + 1))
                e.wait_ge(vl[sl3], 32 * (pr // 3 + 1))
                if pr == 0:
                    e.wait_ge(wl, WL_ALL)
                for qc in range(4):
                    m = pr * 4 + qc
                    qmv = qsb[:, pr * T + qc * 512:pr * T + (qc + 1) * 512]
                    for kb in range(64):
                        kp = kb % 2
                        psS = (psS0, psS1)[kp]
                        if kb >= 2:
                            e.wait_ge(se, 64 * m + kb - 1)
                        elif m >= 1:
                            e.wait_ge(se, 64 * m)
                        e.matmul(psS[:, 0:512],
                                 kpair[0:64, sl3 * SEQ + kb * 128:sl3 * SEQ + (kb + 1) * 128],
                                 qmv[0:64, :], start=True, stop=True,
                                 tile_position=(0, 0))
                        e.matmul(psS[:, 512:1024],
                                 kpair[64:128, sl3 * SEQ + kb * 128:sl3 * SEQ + (kb + 1) * 128],
                                 qmv[64:128, :], start=True, stop=True,
                                 tile_position=(64, 0)).then_inc(mmq, 1)
                        if kb >= 1:
                            pkb = kb - 1
                            pkp = pkb % 4
                            e.wait_ge(se, 64 * m + pkb + 1)
                            if pkb == 0:
                                e.wait_ge(sz, 7 * m)
                            e.matmul(psUA[:], vaug[:, sl3 * 2 * VH + pkb * 65:sl3 * 2 * VH + pkb * 65 + 128],
                                     pbuf[:, pkp * 1024:pkp * 1024 + 512],
                                     start=(pkb == 0), stop=False)
                            e.matmul(psUB[:], vaug[:, sl3 * 2 * VH + VH + pkb * 65:sl3 * 2 * VH + VH + pkb * 65 + 128],
                                     pbuf[:, pkp * 1024 + 512:pkp * 1024 + 1024],
                                     start=(pkb == 0), stop=False).then_inc(mmv, 1)
                    pkb, pkp = 63, 63 % 4
                    e.wait_ge(se, 64 * m + 64)
                    e.matmul(psUA[:], vaug[:, sl3 * 2 * VH + pkb * 65:sl3 * 2 * VH + pkb * 65 + 128],
                             pbuf[:, pkp * 1024:pkp * 1024 + 512],
                             start=False, stop=True)
                    e.matmul(psUB[:], vaug[:, sl3 * 2 * VH + VH + pkb * 65:sl3 * 2 * VH + VH + pkb * 65 + 128],
                             pbuf[:, pkp * 1024 + 512:pkp * 1024 + 1024],
                             start=False, stop=True).then_inc(mmv, 1)
                    e.wait_ge(sz, 7 * m + 3)
                    e.matmul(psBC[:, 0:512], ones64[:], rz[0:1, 0:512],
                             start=True, stop=True)
                    e.matmul(psBC[:, 512:1024], ones64[:], rz[0:1, 512:1024],
                             start=True, stop=True).then_inc(mb, 2)
        @b.scalar
        def _(e):
            for pr in range(NP):
                for qc in range(4):
                    m = pr * 4 + qc
                    for kb in range(64):
                        kp = kb % 2
                        kp4 = kb % 4
                        psS = (psS0, psS1)[kp]
                        e.wait_ge(mmq, 64 * m + kb + 1)
                        if kb >= 4:
                            e.wait_ge(mmv, 64 * m + kb - 3)
                        elif m >= 1:
                            e.wait_ge(mmv, 64 * m)
                        e.activation(pbuf[:, kp4 * 1024:(kp4 + 1) * 1024], psS[:],
                                     AF.Exp, scale=ISC).then_inc(se, 1)

    SZ_ALL = 7 * 32 + 1 + 16
    with nc.Block() as b:
        @b.tensor
        def _(e):
            e.wait_ge(sz, 7 * 32)
            for j in range(32):
                tt, half = j // 2, j % 2
                psO = (psS0, psS1)[j % 2]
                if j >= 2:
                    e.wait_ge(svy, j - 1)
                for kc in range(NC):
                    mmi = e.matmul(psO[:, 0:512],
                                   absb[:, kc * T + tt * 128:kc * T + (tt + 1) * 128],
                                   wosb[:, kc * D + half * 512:kc * D + (half + 1) * 512],
                                   start=(kc == 0), stop=(kc == NC - 1))
                    if kc == NC - 1:
                        mmi.then_inc(mo, 1)
        @b.vector
        def _(e):
            for j in range(32):
                e.wait_ge(mo, j + 1)
                if j >= 2:
                    e.wait_ge(syo[j % 2], 16 * (j // 2))
                psO = (psS0, psS1)[j % 2]
                e.tensor_copy(ybuf[:, (j % 2) * 512:(j % 2 + 1) * 512],
                              psO[:, 0:512]).then_inc(svy, 1)
        @b.sync
        def _(e):
            for j in range(32):
                tt, half = j // 2, j % 2
                e.wait_ge(svy, j + 1)
                e.dma_start(yat_o[tt * 128:(tt + 1) * 128, half * 512:(half + 1) * 512],
                            ybuf[:, (j % 2) * 512:(j % 2 + 1) * 512]).then_inc(syo[j % 2], 16)
            for i in range(2):
                e.wait_ge(syo[i], 16 * 16)
    return


def build_launchB():
    nc = bass.Bass("TRN2", target_bir_lowering=False)
    io = {}
    io["qT_i"] = nc.dram_tensor("qT_i", [D, T], BF16, kind="ExternalInput").ap()
    io["kT_i"] = nc.dram_tensor("kT_i", [D, SEQ], BF16, kind="ExternalInput").ap()
    io["v_i"] = nc.dram_tensor("v_i", [SEQ, D], BF16, kind="ExternalInput").ap()
    io["wout"] = nc.dram_tensor("wout", [D, D], F32, kind="ExternalInput").ap()
    io["yat_o"] = nc.dram_tensor("yat_o", [T, D], F32, kind="ExternalOutput").ap()
    st = ExitStack()
    emit_segB(nc, "B_", io, st)
    st.close()
    return nc


def _launchB(resA, inputs):
    qTs = [np.asarray(resA.results[c]["qT_o"]) for c in range(8)]
    kTs = [np.asarray(resA.results[c]["kT_o"]) for c in range(8)]
    vs = [np.asarray(resA.results[c]["v_o"]) for c in range(8)]
    ncB = build_launchB()
    in_maps = []
    for c in range(8):
        b = c // 4
        grp = range(b * 4, b * 4 + 4)
        kT_full = np.concatenate([kTs[g] for g in grp], axis=1)
        v_full = np.concatenate([vs[g] for g in grp], axis=0)
        in_maps.append({
            "qT_i": qTs[c],
            "kT_i": np.ascontiguousarray(kT_full),
            "v_i": np.ascontiguousarray(v_full),
            "wout": np.asarray(inputs["L0_attn_w_out"], np.float32),
        })
    return run_bass_kernel_spmd(ncB, in_maps, core_ids=list(range(8)))





def _emit_quant_scalar(e, sa, qa, qabs, qsg, W):
    e.activation(qabs[:, 0:W], qa[:, 0:W], AF.Abs).then_inc(sa, 1)
    e.activation(qsg[:, 0:W], qa[:, 0:W], AF.Sign).then_inc(sa, 1)


def _emit_quant_vector(e, vq, base, qa, qabs, qsg, qtmp, l2t, l3t, qmc, qsc, qrc,
                       qacc, qacc2, qout, W):
    BL = W // 16
    n = base

    def step(inst):
        nonlocal n
        inst.then_inc(vq, 1)
        n += 1

    def w(x=None):
        e.wait_ge(vq, n if x is None else x)

    v3 = lambda t, wdt, inner: t[:, 0:wdt].rearrange("p (b i) -> p b i", i=inner)
    a3 = v3(qabs, W, 16)
    step(e.tensor_tensor(v3(qtmp, W // 2, 8), a3[:, :, 0:16:2], a3[:, :, 1:16:2], ALU.max))
    w()
    t3 = v3(qtmp, W // 2, 8)
    step(e.tensor_tensor(v3(l2t, W // 4, 4), t3[:, :, 0:8:2], t3[:, :, 1:8:2], ALU.max))
    w()
    u3 = v3(l2t, W // 4, 4)
    step(e.tensor_tensor(v3(l3t, W // 8, 2), u3[:, :, 0:4:2], u3[:, :, 1:4:2], ALU.max))
    w()
    s3 = v3(l3t, W // 8, 2)
    step(e.tensor_tensor(qmc[:, 0:BL].rearrange("p (b i) -> p b i", i=1),
                         s3[:, :, 0:2:2], s3[:, :, 1:2:2], ALU.max))
    w()
    step(e.tensor_scalar(qsc[:, 0:BL], qmc[:, 0:BL], 1.0 / 6.0, 1e-12, ALU.mult, ALU.max))
    w()
    step(e.reciprocal(qrc[:, 0:BL], qsc[:, 0:BL]))
    w()
    rb = qrc[:, 0:BL].unsqueeze(2).broadcast_to((128, BL, 16))
    step(e.tensor_tensor(v3(qa, W, 16), a3, rb, ALU.mult))
    w()
    step(e.tensor_single_scalar(qacc[:, 0:W], qa[:, 0:W], QMID[0], ALU.is_gt))
    cur, oth = qacc, qacc2
    for k in (1, 2, 3):
        w()
        step(e.scalar_tensor_tensor(oth[:, 0:W], qa[:, 0:W], QMID[k], cur[:, 0:W],
                                    ALU.is_gt, ALU.add))
        cur, oth = oth, cur
    step(e.tensor_single_scalar(qtmp[:, 0:W], qa[:, 0:W], QMID[4], ALU.is_gt))
    w()
    step(e.scalar_tensor_tensor(qtmp[:, 0:W], qa[:, 0:W], QMID[5], qtmp[:, 0:W],
                                ALU.is_gt, ALU.add))
    step(e.tensor_single_scalar(qabs[:, 0:W], qa[:, 0:W], QMID[6], ALU.is_gt))
    w()
    step(e.scalar_tensor_tensor(oth[:, 0:W], cur[:, 0:W], 0.5, qtmp[:, 0:W],
                                ALU.mult, ALU.add))
    cur, oth = oth, cur
    w()
    step(e.scalar_tensor_tensor(oth[:, 0:W], qabs[:, 0:W], 2.0, cur[:, 0:W],
                                ALU.mult, ALU.add))
    cur, oth = oth, cur
    w()
    sb = qsc[:, 0:BL].unsqueeze(2).broadcast_to((128, BL, 16))
    step(e.tensor_tensor(v3(qtmp, W, 16), v3(qsg, W, 16), sb, ALU.mult))
    w()
    step(e.tensor_tensor(qout[:, 0:W], cur[:, 0:W], qtmp[:, 0:W], ALU.mult))
    return n - base


VQ = 18



def emit_segC0(nc, pre, io, st):
    NJ = 7
    W = FFI
    wtiles = io["wtiles"]
    idb = io["idb"]
    wq_o = io["wq_o"]

    qa = _sb(st, nc, pre + "qa", [128, W], F32)
    qabs = _sb(st, nc, pre + "qabs", [128, W], F32)
    qsg = _sb(st, nc, pre + "qsg", [128, W], F32)
    qtmp = _sb(st, nc, pre + "qtmp", [128, W], F32)
    l2t = _sb(st, nc, pre + "l2t", [128, W // 4], F32)
    l3t = _sb(st, nc, pre + "l3t", [128, W // 8], F32)
    qmc = _sb(st, nc, pre + "qmc", [128, W // 16], F32)
    qsc = _sb(st, nc, pre + "qsc", [128, W // 16], F32)
    qrc = _sb(st, nc, pre + "qrc", [128, W // 16], F32)
    qacc = _sb(st, nc, pre + "qacc", [128, W], F32)
    qacc2 = _sb(st, nc, pre + "qacc2", [128, W], F32)
    qout = _sb(st, nc, pre + "qout", [128, W], BF16)
    identb = _sb(st, nc, pre + "identb", [128, 128], BF16)
    obuf = _sb(st, nc, pre + "obuf", [128, NFI * 128], BF16)

    psT0 = _ps(st, nc, pre + "psT0", [128, 128], BF16)
    psT1 = _ps(st, nc, pre + "psT1", [128, 128], BF16)

    wl = nc.alloc_semaphore(pre + "wl")
    ql = nc.alloc_semaphore(pre + "ql")
    sa = nc.alloc_semaphore(pre + "sa")
    vq = nc.alloc_semaphore(pre + "vq")
    sve = nc.alloc_semaphore(pre + "sve")
    tp = nc.alloc_semaphore(pre + "tp")
    sow = nc.alloc_semaphore(pre + "sow")

    with nc.Block() as b:
        @b.gpsimd
        def _(e):
            e.dma_start(identb[:], idb[:]).then_inc(wl, 16)
        @b.sync
        def _(e):
            for j in range(NJ):
                if j >= 1:
                    e.wait_ge(vq, j * VQ - 2)
                e.dma_start(qa[:], wtiles[j]).then_inc(ql, 16)
        @b.scalar
        def _(e):
            for j in range(NJ):
                e.wait_ge(ql, 16 * (j + 1))
                if j >= 1:
                    e.wait_ge(vq, j * VQ)
                _emit_quant_scalar(e, sa, qa, qabs, qsg, W)
        @b.vector
        def _(e):
            for j in range(NJ):
                e.wait_ge(sa, 2 * (j + 1))
                _emit_quant_vector(e, vq, j * VQ, qa, qabs, qsg, qtmp, l2t, l3t,
                                   qmc, qsc, qrc, qacc, qacc2, qout, W)
                for kc in range(NFI):
                    e.wait_ge(tp, j * NFI + kc + 1)
                    if kc == 0:
                        e.wait_ge(sow, 16 * j)
                    ps = (psT0, psT1)[kc % 2]
                    e.tensor_copy(obuf[:, kc * 128:(kc + 1) * 128], ps[:]).then_inc(sve, 1)
        @b.tensor
        def _(e):
            e.wait_ge(wl, 16)
            for j in range(NJ):
                e.wait_ge(vq, (j + 1) * VQ)
                for kc in range(NFI):
                    if kc >= 2:
                        e.wait_ge(sve, j * NFI + kc - 1)
                    elif j >= 1:
                        e.wait_ge(sve, j * NFI)
                    ps = (psT0, psT1)[kc % 2]
                    e.transpose(ps[:], qout[:, kc * 128:(kc + 1) * 128],
                                identb[:]).then_inc(tp, 1)
        @b.gpsimd
        def _(e):
            for j in range(NJ):
                e.wait_ge(sve, (j + 1) * NFI)
                e.dma_start(wq_o[j].rearrange("(kc p) c -> p kc c", p=128),
                            obuf[:].rearrange("p (kc c) -> p kc c", c=128)).then_inc(sow, 16)
            e.wait_ge(sow, 16 * NJ)
    return


def build_launchC0():
    nc = bass.Bass("TRN2", target_bir_lowering=False)
    io = {}
    io["wtiles"] = nc.dram_tensor("wtiles", [7, 128, FFI], F32, kind="ExternalInput").ap()
    io["idb"] = nc.dram_tensor("idb", [128, 128], BF16, kind="ExternalInput").ap()
    io["wq_o"] = nc.dram_tensor("wq_o", [7, FFI, 128], BF16, kind="ExternalOutput").ap()
    st = ExitStack()
    emit_segC0(nc, "C0_", io, st)
    st.close()
    return nc


def _launchC0(inputs):
    wg = np.asarray(inputs["L0_ffn_w_gate"], np.float32)
    wu = np.asarray(inputs["L0_ffn_w_up"], np.float32)
    wd = np.asarray(inputs["L0_ffn_w_down"], np.float32)
    jobs = []
    for w in (wg, wu):
        for t in range(NFI):
            tile = np.zeros((128, FFI), np.float32)
            tile[:, 0:D] = w[t * 128:(t + 1) * 128, :]
            jobs.append(tile)
    for t in range(NC):
        jobs.append(np.ascontiguousarray(wd[t * 128:(t + 1) * 128, :]))
    while len(jobs) % 8 != 0 or len(jobs) < 56:
        jobs.append(np.zeros((128, FFI), np.float32))
    NJ = len(jobs) // 8
    ident_bf = np.eye(128, dtype=np.float32).astype(BFDT)
    ncC0 = build_launchC0()
    in_maps = [{"wtiles": np.stack(jobs[c * NJ:(c + 1) * NJ]), "idb": ident_bf}
               for c in range(8)]
    res = run_bass_kernel_spmd(ncC0, in_maps, core_ids=list(range(8)))
    outs = np.concatenate([np.asarray(res.results[c]["wq_o"]) for c in range(8)], axis=0)
    wgT = np.concatenate([outs[t][0:D, :] for t in range(NFI)], axis=1)
    wuT = np.concatenate([outs[NFI + t][0:D, :] for t in range(NFI)], axis=1)
    wdT = np.concatenate([outs[2 * NFI + t] for t in range(NC)], axis=1)
    return wgT, wuT, wdT





def emit_segCa(nc, pre, io, st, from_jobs=False):
    xn_i = io["xn_i"]
    wgT_i = io.get("wgT_i")
    wuT_i = io.get("wuT_i")
    idb = io["idb"]
    h_o = io["h_o"]

    W = D
    qa = _sb(st, nc, pre + "qa", [128, W], F32)
    qabs = _sb(st, nc, pre + "qabs", [128, W], F32)
    qsg = _sb(st, nc, pre + "qsg", [128, W], F32)
    qtmp = _sb(st, nc, pre + "qtmp", [128, W], F32)
    l2t = _sb(st, nc, pre + "l2t", [128, W // 4], F32)
    l3t = _sb(st, nc, pre + "l3t", [128, W // 8], F32)
    qmc = _sb(st, nc, pre + "qmc", [128, W // 16], F32)
    qsc = _sb(st, nc, pre + "qsc", [128, W // 16], F32)
    qrc = _sb(st, nc, pre + "qrc", [128, W // 16], F32)
    qacc = _sb(st, nc, pre + "qacc", [128, W], F32)
    qacc2 = _sb(st, nc, pre + "qacc2", [128, W], F32)
    qout = _sb(st, nc, pre + "qout", [128, W], BF16)
    identb = _sb(st, nc, pre + "identb", [128, 128], BF16)
    xqT = _sb(st, nc, pre + "xqT", [128, NC * T], BF16)
    wgsb = _sb(st, nc, pre + "wgsb", [128, NC * FFI], BF16)
    wusb = _sb(st, nc, pre + "wusb", [128, NC * FFI], BF16)
    hsil = _sb(st, nc, pre + "hsil", [128, 2 * 512], F32)
    hbuf = _sb(st, nc, pre + "hbuf", [128, 2 * 512], F32)

    psT0 = _ps(st, nc, pre + "psT0", [128, 128], BF16)
    psT1 = _ps(st, nc, pre + "psT1", [128, 128], BF16)
    psG0 = _ps(st, nc, pre + "psG0", [128, 512], F32)
    psG1 = _ps(st, nc, pre + "psG1", [128, 512], F32)
    psU0 = _ps(st, nc, pre + "psU0", [128, 512], F32)
    psU1 = _ps(st, nc, pre + "psU1", [128, 512], F32)

    wl = nc.alloc_semaphore(pre + "wl")
    il = nc.alloc_semaphore(pre + "il")
    ql = nc.alloc_semaphore(pre + "ql")
    sa = nc.alloc_semaphore(pre + "sa")
    vq = nc.alloc_semaphore(pre + "vq")
    sve = nc.alloc_semaphore(pre + "sve")
    tp = nc.alloc_semaphore(pre + "tp")
    mmg = nc.alloc_semaphore(pre + "mmg")
    sil = nc.alloc_semaphore(pre + "sil")
    sv = nc.alloc_semaphore(pre + "sv")
    sho = [nc.alloc_semaphore(f"{pre}sho{i}") for i in range(2)]

    WL_ALL = 16 * (2 * NC)
    FCW = [512] * 5 + [256]
    FCO = [0, 512, 1024, 1536, 2048, 2560]

    with nc.Block() as b:
        @b.gpsimd
        def _(e):
            e.dma_start(identb[:], idb[:]).then_inc(il, 16)
            for kc in range(NC):
                if from_jobs:
                    wj = io["wjobs"]
                    sg = bass.AP(wj.tensor, wj.offset + kc * 128 * 128,
                                 [[128, 128], [FFI * 128, NFI], [1, 128]])
                    su = bass.AP(wj.tensor, wj.offset + NFI * FFI * 128 + kc * 128 * 128,
                                 [[128, 128], [FFI * 128, NFI], [1, 128]])
                    e.dma_start(wgsb[:, kc * FFI:(kc + 1) * FFI]
                                .rearrange("p (t c) -> p t c", c=128), sg).then_inc(wl, 16)
                    e.dma_start(wusb[:, kc * FFI:(kc + 1) * FFI]
                                .rearrange("p (t c) -> p t c", c=128), su).then_inc(wl, 16)
                else:
                    e.dma_start(wgsb[:, kc * FFI:(kc + 1) * FFI],
                                wgT_i[kc * 128:(kc + 1) * 128, :]).then_inc(wl, 16)
                    e.dma_start(wusb[:, kc * FFI:(kc + 1) * FFI],
                                wuT_i[kc * 128:(kc + 1) * 128, :]).then_inc(wl, 16)
        @b.sync
        def _(e):
            for tt in range(NT):
                if tt >= 1:
                    e.wait_ge(vq, tt * VQ - 2)
                e.dma_start(qa[:], xn_i[tt * 128:(tt + 1) * 128, :]).then_inc(ql, 16)
        @b.scalar
        def _(e):
            for tt in range(NT):
                e.wait_ge(ql, 16 * (tt + 1))
                if tt >= 1:
                    e.wait_ge(vq, tt * VQ)
                _emit_quant_scalar(e, sa, qa, qabs, qsg, W)
        @b.vector
        def _(e):
            for tt in range(NT):
                e.wait_ge(sa, 2 * (tt + 1))
                _emit_quant_vector(e, vq, tt * VQ, qa, qabs, qsg, qtmp, l2t, l3t,
                                   qmc, qsc, qrc, qacc, qacc2, qout, W)
                for kc in range(NC):
                    e.wait_ge(tp, tt * NC + kc + 1)
                    ps = (psT0, psT1)[kc % 2]
                    e.tensor_copy(xqT[:, kc * T + tt * 128:kc * T + (tt + 1) * 128],
                                  ps[:]).then_inc(sve, 1)
        @b.tensor
        def _(e):
            e.wait_ge(il, 16)
            for tt in range(NT):
                e.wait_ge(vq, (tt + 1) * VQ)
                for kc in range(NC):
                    if kc >= 2:
                        e.wait_ge(sve, tt * NC + kc - 1)
                    elif tt >= 1:
                        e.wait_ge(sve, tt * NC)
                    ps = (psT0, psT1)[kc % 2]
                    e.transpose(ps[:], qout[:, kc * 128:(kc + 1) * 128],
                                identb[:]).then_inc(tp, 1)
        @b.tensor
        def _(e):
            e.wait_ge(wl, WL_ALL)
            j = 0
            for tt in range(NT):
                e.wait_ge(sve, (tt + 1) * NC)
                for fc in range(6):
                    par = j % 2
                    psg = (psG0, psG1)[par]
                    psu = (psU0, psU1)[par]
                    if j >= 2:
                        e.wait_ge(sil, j - 1)
                        e.wait_ge(sv, j - 1)
                    wd = FCW[fc]
                    for kc in range(NC):
                        e.matmul(psg[:, 0:wd], xqT[:, kc * T + tt * 128:kc * T + (tt + 1) * 128],
                                 wgsb[:, kc * FFI + FCO[fc]:kc * FFI + FCO[fc] + wd],
                                 start=(kc == 0), stop=(kc == NC - 1))
                        mmi = e.matmul(psu[:, 0:wd], xqT[:, kc * T + tt * 128:kc * T + (tt + 1) * 128],
                                       wusb[:, kc * FFI + FCO[fc]:kc * FFI + FCO[fc] + wd],
                                       start=(kc == 0), stop=(kc == NC - 1))
                        if kc == NC - 1:
                            mmi.then_inc(mmg, 1)
                    j += 1
        @b.scalar
        def _(e):
            j = 0
            for tt in range(NT):
                for fc in range(6):
                    par = j % 2
                    psg = (psG0, psG1)[par]
                    e.wait_ge(mmg, j + 1)
                    if j >= 2:
                        e.wait_ge(sv, j - 1)
                    e.activation(hsil[:, par * 512:par * 512 + FCW[fc]],
                                 psg[:, 0:FCW[fc]], AF.Silu).then_inc(sil, 1)
                    j += 1
        @b.vector
        def _(e):
            j = 0
            for tt in range(NT):
                for fc in range(6):
                    par = j % 2
                    psu = (psU0, psU1)[par]
                    e.wait_ge(sil, j + 1)
                    if j >= 2:
                        e.wait_ge(sho[par], 16 * (j // 2))
                    e.tensor_tensor(hbuf[:, par * 512:par * 512 + FCW[fc]],
                                    hsil[:, par * 512:par * 512 + FCW[fc]],
                                    psu[:, 0:FCW[fc]], ALU.mult).then_inc(sv, 1)
                    j += 1
        @b.gpsimd
        def _(e):
            j = 0
            for tt in range(NT):
                for fc in range(6):
                    par = j % 2
                    e.wait_ge(sv, j + 1)
                    e.dma_start(h_o[tt * 128:(tt + 1) * 128, FCO[fc]:FCO[fc] + FCW[fc]],
                                hbuf[:, par * 512:par * 512 + FCW[fc]]).then_inc(sho[par], 16)
                    j += 1
            for i in range(2):
                e.wait_ge(sho[i], 16 * 48)
    return


def build_launchC1a():
    nc = bass.Bass("TRN2", target_bir_lowering=False)
    io = {}
    io["xn_i"] = nc.dram_tensor("xn_i", [T, D], F32, kind="ExternalInput").ap()
    io["wgT_i"] = nc.dram_tensor("wgT_i", [D, FFI], BF16, kind="ExternalInput").ap()
    io["wuT_i"] = nc.dram_tensor("wuT_i", [D, FFI], BF16, kind="ExternalInput").ap()
    io["idb"] = nc.dram_tensor("idb", [128, 128], BF16, kind="ExternalInput").ap()
    io["h_o"] = nc.dram_tensor("h_o", [T, FFI], F32, kind="ExternalOutput").ap()
    st = ExitStack()
    emit_segCa(nc, "Ca_", io, st)
    st.close()
    return nc



def emit_segCb(nc, pre, io, st, from_jobs=False):
    h_i = io["h_i"]
    wdT_i = io.get("wdT_i")
    xin = io["xin"]
    yat_i = io["yat_i"]
    idb = io["idb"]
    x1_o = io["x1_o"]

    W = FFI
    qa = _sb(st, nc, pre + "qa", [128, W], F32)
    qabs = _sb(st, nc, pre + "qabs", [128, W], F32)
    qsg = _sb(st, nc, pre + "qsg", [128, W], F32)
    qtmp = _sb(st, nc, pre + "qtmp", [128, W], F32)
    l2t = _sb(st, nc, pre + "l2t", [128, W // 4], F32)
    l3t = _sb(st, nc, pre + "l3t", [128, W // 8], F32)
    qmc = _sb(st, nc, pre + "qmc", [128, W // 16], F32)
    qsc = _sb(st, nc, pre + "qsc", [128, W // 16], F32)
    qrc = _sb(st, nc, pre + "qrc", [128, W // 16], F32)
    qacc = _sb(st, nc, pre + "qacc", [128, W], F32)
    qacc2 = _sb(st, nc, pre + "qacc2", [128, W], F32)
    qout = _sb(st, nc, pre + "qout", [128, W], BF16)
    identb = _sb(st, nc, pre + "identb", [128, 128], BF16)
    hqT = _sb(st, nc, pre + "hqT", [128, NFI * 128], BF16)
    wdsb = _sb(st, nc, pre + "wdsb", [128, NFI * D], BF16)
    xres = _sb(st, nc, pre + "xres", [128, 2 * D], F32)
    yres = _sb(st, nc, pre + "yres", [128, 2 * D], F32)
    tadd = _sb(st, nc, pre + "tadd", [128, 512], F32)
    obuf = _sb(st, nc, pre + "obuf", [128, 2 * 1024], F32)

    psT0 = _ps(st, nc, pre + "psT0", [128, 128], BF16)
    psT1 = _ps(st, nc, pre + "psT1", [128, 128], BF16)
    psD0 = _ps(st, nc, pre + "psD0", [128, 512], F32)
    psD1 = _ps(st, nc, pre + "psD1", [128, 512], F32)

    wl = nc.alloc_semaphore(pre + "wl")
    il = nc.alloc_semaphore(pre + "il")
    ql = nc.alloc_semaphore(pre + "ql")
    rl = [nc.alloc_semaphore(f"{pre}rl{i}") for i in range(2)]
    sa = nc.alloc_semaphore(pre + "sa")
    vq = nc.alloc_semaphore(pre + "vq")
    sve = nc.alloc_semaphore(pre + "sve")
    tp = nc.alloc_semaphore(pre + "tp")
    mmd = nc.alloc_semaphore(pre + "mmd")
    sv = nc.alloc_semaphore(pre + "sv")
    sxo = [nc.alloc_semaphore(f"{pre}sxo{i}") for i in range(2)]

    WL_ALL = 16 * NFI
    VE = VQ + NFI + 4

    with nc.Block() as b:
        @b.gpsimd
        def _(e):
            e.dma_start(identb[:], idb[:]).then_inc(il, 16)
            for fc in range(NFI):
                if from_jobs:
                    wj = io["wjobs"]
                    sd = bass.AP(wj.tensor, wj.offset + 2 * NFI * FFI * 128 + fc * 128 * 128,
                                 [[128, 128], [FFI * 128, NC], [1, 128]])
                    e.dma_start(wdsb[:, fc * D:(fc + 1) * D]
                                .rearrange("p (t c) -> p t c", c=128), sd).then_inc(wl, 16)
                else:
                    e.dma_start(wdsb[:, fc * D:(fc + 1) * D],
                                wdT_i[fc * 128:(fc + 1) * 128, :]).then_inc(wl, 16)
        @b.sync
        def _(e):
            for tt in range(NT):
                if tt >= 1:
                    e.wait_ge(sv, (tt - 1) * VE + 21)
                e.dma_start(qa[:], h_i[tt * 128:(tt + 1) * 128, :]).then_inc(ql, 16)
        @b.gpsimd
        def _(e):
            for tt in range(NT + 1):
                par = tt % 2
                if tt < NT:
                    if tt >= 2:
                        e.wait_ge(sv, (tt - 1) * VE)
                    e.dma_start(xres[:, par * D:(par + 1) * D],
                                xin[tt * 128:(tt + 1) * 128, :]).then_inc(rl[par], 16)
                    e.dma_start(yres[:, par * D:(par + 1) * D],
                                yat_i[tt * 128:(tt + 1) * 128, :]).then_inc(rl[par], 16)
                if tt >= 1:
                    st = tt - 1
                    e.wait_ge(sv, (st + 1) * VE)
                    e.dma_start(x1_o[st * 128:(st + 1) * 128, :],
                                obuf[:, (st % 2) * 1024:(st % 2) * 1024 + 1024]).then_inc(sxo[st % 2], 16)
            for i in range(2):
                e.wait_ge(sxo[i], 16 * 8)
        @b.scalar
        def _(e):
            for tt in range(NT):
                e.wait_ge(ql, 16 * (tt + 1))
                if tt >= 1:
                    e.wait_ge(sv, tt * VE)
                _emit_quant_scalar(e, sa, qa, qabs, qsg, W)
        @b.vector
        def _(e):
            for tt in range(NT):
                par = tt % 2
                e.wait_ge(sa, 2 * (tt + 1))
                _emit_quant_vector(e, sv, tt * VE, qa, qabs, qsg, qtmp, l2t, l3t,
                                   qmc, qsc, qrc, qacc, qacc2, qout, W)
                nv = tt * VE + VQ
                for fc in range(NFI):
                    e.wait_ge(tp, tt * NFI + fc + 1)
                    ps = (psT0, psT1)[fc % 2]
                    e.tensor_copy(hqT[:, fc * 128:(fc + 1) * 128], ps[:]).then_inc(sv, 1)
                    nv += 1
                e.wait_ge(rl[par], 32 * (tt // 2 + 1))
                for half in range(2):
                    psd = (psD0, psD1)[half]
                    e.wait_ge(mmd, tt * 2 + half + 1)
                    if tt >= 2:
                        e.wait_ge(sxo[par], 16 * (tt // 2))
                    e.wait_ge(sv, nv)
                    e.tensor_tensor(tadd[:], psd[:],
                                    xres[:, par * D + half * 512:par * D + half * 512 + 512],
                                    ALU.add).then_inc(sv, 1)
                    nv += 1
                    e.wait_ge(sv, nv)
                    e.tensor_tensor(obuf[:, par * 1024 + half * 512:par * 1024 + (half + 1) * 512],
                                    tadd[:],
                                    yres[:, par * D + half * 512:par * D + half * 512 + 512],
                                    ALU.add).then_inc(sv, 1)
                    nv += 1
        @b.tensor
        def _(e):
            e.wait_ge(il, 16)
            e.wait_ge(wl, WL_ALL)
            for tt in range(NT):
                e.wait_ge(sv, tt * VE + VQ)
                for fc in range(NFI):
                    if fc >= 2:
                        e.wait_ge(sv, tt * VE + VQ + fc - 1)
                    elif tt >= 1:
                        e.wait_ge(sv, tt * VE)
                    ps = (psT0, psT1)[fc % 2]
                    e.transpose(ps[:], qout[:, fc * 128:(fc + 1) * 128],
                                identb[:]).then_inc(tp, 1)
                e.wait_ge(sv, tt * VE + VQ + NFI)
                for half in range(2):
                    psd = (psD0, psD1)[half]
                    for fc in range(NFI):
                        mmi = e.matmul(psd[:], hqT[:, fc * 128:(fc + 1) * 128],
                                       wdsb[:, fc * D + half * 512:fc * D + (half + 1) * 512],
                                       start=(fc == 0), stop=(fc == NFI - 1))
                        if fc == NFI - 1:
                            mmi.then_inc(mmd, 1)
    return


def build_launchC1b():
    nc = bass.Bass("TRN2", target_bir_lowering=False)
    io = {}
    io["h_i"] = nc.dram_tensor("h_i", [T, FFI], F32, kind="ExternalInput").ap()
    io["wdT_i"] = nc.dram_tensor("wdT_i", [FFI, D], BF16, kind="ExternalInput").ap()
    io["xin"] = nc.dram_tensor("xin", [T, D], F32, kind="ExternalInput").ap()
    io["yat_i"] = nc.dram_tensor("yat_i", [T, D], F32, kind="ExternalInput").ap()
    io["idb"] = nc.dram_tensor("idb", [128, 128], BF16, kind="ExternalInput").ap()
    io["x1_o"] = nc.dram_tensor("x1_o", [T, D], F32, kind="ExternalOutput").ap()
    st = ExitStack()
    emit_segCb(nc, "Cb_", io, st)
    st.close()
    return nc


def _launchC(inputs, resA, resB, wgT, wuT, wdT):
    ident_bf = np.eye(128, dtype=np.float32).astype(BFDT)
    x = np.asarray(inputs["L0_attn_x"], np.float32)
    xs = _shard_rows(x)
    ncA = build_launchC1a()
    in_maps = []
    for c in range(8):
        in_maps.append({
            "xn_i": np.asarray(resA.results[c]["xn_o"]),
            "wgT_i": wgT, "wuT_i": wuT, "idb": ident_bf,
        })
    res1a = run_bass_kernel_spmd(ncA, in_maps, core_ids=list(range(8)))
    ncB = build_launchC1b()
    in_maps = []
    for c in range(8):
        in_maps.append({
            "h_i": np.asarray(res1a.results[c]["h_o"]),
            "wdT_i": wdT,
            "xin": xs[c],
            "yat_i": np.asarray(resB.results[c]["yat_o"]),
            "idb": ident_bf,
        })
    return run_bass_kernel_spmd(ncB, in_maps, core_ids=list(range(8))), res1a





def emit_segDd(nc, pre, io, st, ag2_hook=None):
    x1_i = io["x1_i"]
    Wq_i = io["Wq_i"]
    Wk_i = io["Wk_i"]
    Wv_i = io["Wv_i"]
    bq_i = io["bq_i"]
    bk_i = io["bk_i"]
    bv_i = io["bv_i"]
    idb = io["idb"]
    qt_o = io["qt_o"]
    kt_o = io["kt_o"]
    vt_o = io["vt_o"]
    qTf_o = io["qTf_o"]
    kTf_o = io["kTf_o"]
    xn1T_o = io["xn1T_o"]

    xa = _sb(st, nc, pre + "xa", [128, 2 * D], F32)
    xnb = _sb(st, nc, pre + "xnb", [128, 2 * D], BF16)
    dump = _sb(st, nc, pre + "dump", [128, D], F32)
    dump2 = _sb(st, nc, pre + "dump2", [128, D], F32)
    cols = _sb(st, nc, pre + "cols", [128, 16], F32)
    epscol = _sb(st, nc, pre + "epscol", [128, 1], F32)
    onesr = _sb(st, nc, pre + "onesr", [1, 128], BF16)
    identb = _sb(st, nc, pre + "identb", [128, 128], BF16)
    x1T = _sb(st, nc, pre + "x1T", [128, NC * T], BF16)
    wsb = _sb(st, nc, pre + "wsb", [128, 3 * NC * D], BF16)
    wTsb = _sb(st, nc, pre + "wTsb", [128, 3 * NC * D], BF16)
    brow = _sb(st, nc, pre + "brow", [1, 3 * D], BF16)
    bcolq = _sb(st, nc, pre + "bcolq", [128, 8], F32)
    bcolk = _sb(st, nc, pre + "bcolk", [128, 8], F32)
    obuf = _sb(st, nc, pre + "obuf", [128, 6 * 512], BF16)
    fbuf = _sb(st, nc, pre + "fbuf", [128, 4 * 512], BF16)

    psT0 = _ps(st, nc, pre + "psT0", [128, 128], BF16)
    psT1 = _ps(st, nc, pre + "psT1", [128, 128], BF16)
    psA = _ps(st, nc, pre + "psA", [128, 512], F32)
    psB = _ps(st, nc, pre + "psB", [128, 512], F32)
    psC = _ps(st, nc, pre + "psC", [128, 512], F32)
    psD2 = _ps(st, nc, pre + "psD2", [128, 512], F32)
    psE = _ps(st, nc, pre + "psE", [128, 512], F32)
    psF = _ps(st, nc, pre + "psF", [128, 512], F32)

    wl = nc.alloc_semaphore(pre + "wl")
    il = nc.alloc_semaphore(pre + "il")
    xl = [nc.alloc_semaphore(f"{pre}xl{i}") for i in range(2)]
    sa = nc.alloc_semaphore(pre + "sa")
    sv = nc.alloc_semaphore(pre + "sv")
    sq = nc.alloc_semaphore(pre + "sq")
    tp = nc.alloc_semaphore(pre + "tp")
    mm = nc.alloc_semaphore(pre + "mm")
    mo = nc.alloc_semaphore(pre + "mo")
    so = [nc.alloc_semaphore(f"{pre}so{i}") for i in range(2)]
    sq6 = [nc.alloc_semaphore(f"{pre}sq6_{i}") for i in range(6)]
    sf = [nc.alloc_semaphore(f"{pre}sf{i}") for i in range(4)]

    WL_ALL = 16 * (3 * NC + 3 + 16)

    with nc.Block() as b:
        @b.gpsimd
        def _(e):
            e.dma_start(identb[:], idb[:]).then_inc(il, 16)
            for i, w in enumerate((Wq_i, Wk_i, Wv_i)):
                for oc in range(NC):
                    e.dma_start(wsb[:, (i * NC + oc) * D:(i * NC + oc + 1) * D],
                                w[oc * 128:(oc + 1) * 128, :]).then_inc(wl, 16)
            for i, bb in enumerate((bq_i, bk_i, bv_i)):
                e.dma_start(brow[0:1, i * D:(i + 1) * D], bb[:]).then_inc(wl, 16)
            for oc in range(NC):
                e.dma_start(bcolq[:, oc:oc + 1], bq_i[0:1, oc * 128:(oc + 1) * 128]
                            .rearrange("a p -> p a")).then_inc(wl, 16)
                e.dma_start(bcolk[:, oc:oc + 1], bk_i[0:1, oc * 128:(oc + 1) * 128]
                            .rearrange("a p -> p a")).then_inc(wl, 16)
        @b.vector
        def _(e):
            e.memset(epscol[:], EPS)
            e.memset(onesr[:], 1.0).then_inc(sv, 1)

    with nc.Block() as b:
        @b.tensor
        def _(e):
            e.wait_ge(il, 16)
            e.wait_ge(wl, WL_ALL)
            n = 0
            for i in range(3):
                for oc in range(NC):
                    for ic in range(NC):
                        if n >= 2:
                            e.wait_ge(sv, 1 + n - 1)
                        ps = (psT0, psT1)[n % 2]
                        e.transpose(ps[:], wsb[:, (i * NC + oc) * D + ic * 128:
                                                (i * NC + oc) * D + (ic + 1) * 128],
                                    identb[:]).then_inc(tp, 1)
                        n += 1
        @b.vector
        def _(e):
            n = 0
            for i in range(3):
                for oc in range(NC):
                    for ic in range(NC):
                        e.wait_ge(tp, n + 1)
                        ps = (psT0, psT1)[n % 2]
                        e.tensor_copy(wTsb[:, (i * NC + ic) * D + oc * 128:
                                            (i * NC + ic) * D + (oc + 1) * 128],
                                      ps[:]).then_inc(sv, 1)
                        n += 1

    SV0 = 1 + 3 * NC * NC
    TP0 = 3 * NC * NC
    VPT = 7 + NC

    with nc.Block() as b:
        @b.sync
        def _(e):
            for tt in range(NT):
                if tt >= 2:
                    e.wait_ge(sv, SV0 + (tt - 2) * VPT + 7)
                e.dma_start(xa[:, (tt % 2) * D:(tt % 2 + 1) * D],
                            x1_i[tt * 128:(tt + 1) * 128, :]).then_inc(xl[tt % 2], 16)
        @b.scalar
        def _(e):
            for tt in range(NT):
                par = tt % 2
                xat = xa[:, par * D:(par + 1) * D]
                e.wait_ge(xl[par], 16 * (tt // 2 + 1))
                if tt >= 1:
                    e.wait_ge(sv, SV0 + (tt - 1) * VPT + 2)
                e.activation(dump[:], xat, AF.Copy, accum_out=cols[:, 0:1]).then_inc(sa, 1)
                e.activation(dump2[:], xat, AF.Square, accum_out=cols[:, 1:2]).then_inc(sa, 1)
                e.wait_ge(sv, SV0 + tt * VPT + 3)
                e.activation(cols[:, 5:6], cols[:, 4:5], AF.Sqrt, bias=epscol[:]).then_inc(sq, 1)
        @b.vector
        def _(e):
            c = 1.0 / D
            for tt in range(NT):
                par = tt % 2
                B0 = SV0 + tt * VPT
                xat = xa[:, par * D:(par + 1) * D]
                xnt_ = xnb[:, par * D:(par + 1) * D]
                e.wait_ge(sa, tt * 2 + 2)
                e.tensor_scalar_mul(cols[:, 2:3], cols[:, 0:1], c).then_inc(sv, 1)
                e.wait_ge(sv, B0 + 1)
                e.tensor_mul(cols[:, 3:4], cols[:, 2:3], cols[:, 2:3]).then_inc(sv, 1)
                e.wait_ge(sv, B0 + 2)
                e.scalar_tensor_tensor(cols[:, 4:5], cols[:, 1:2], c, cols[:, 3:4],
                                       ALU.mult, ALU.subtract).then_inc(sv, 1)
                e.wait_ge(sq, tt + 1)
                e.reciprocal(cols[:, 6:7], cols[:, 5:6]).then_inc(sv, 1)
                e.wait_ge(sv, B0 + 4)
                e.scalar_tensor_tensor(cols[:, 7:8], cols[:, 2:3], -1.0, cols[:, 6:7],
                                       ALU.mult, ALU.mult).then_inc(sv, 1)
                e.wait_ge(sv, B0 + 5)
                if tt >= 2:
                    e.wait_ge(tp, TP0 + (tt - 1) * NC)
                e.tensor_scalar(xnt_, xat, cols[:, 6:7], cols[:, 7:8], ALU.mult,
                                ALU.add).then_inc(sv, 2)
                for kc in range(NC):
                    e.wait_ge(tp, TP0 + tt * NC + kc + 1)
                    ps = (psT0, psT1)[kc % 2]
                    e.tensor_copy(x1T[:, kc * T + tt * 128:kc * T + (tt + 1) * 128],
                                  ps[:]).then_inc(sv, 1)
        @b.tensor
        def _(e):
            for tt in range(NT):
                e.wait_ge(sv, SV0 + tt * VPT + 7)
                for kc in range(NC):
                    if kc >= 2:
                        e.wait_ge(sv, SV0 + tt * VPT + 7 + kc - 1)
                    ps = (psT0, psT1)[kc % 2]
                    e.transpose(ps[:], xnb[:, (tt % 2) * D + kc * 128:(tt % 2) * D + (kc + 1) * 128],
                                identb[:]).then_inc(tp, 1)

    SV1 = SV0 + NT * VPT
    TP1 = TP0 + NT * NC

    with nc.Block() as b:
        @b.sync
        def _(e):
            for j in range(16):
                kc, tc2 = j // 2, j % 2
                e.wait_ge(sv, SV1)
                e.dma_start(xn1T_o[kc * 128:(kc + 1) * 128, tc2 * 1024:(tc2 + 1) * 1024],
                            x1T[:, kc * T + tc2 * 1024:kc * T + (tc2 + 1) * 1024]).then_inc(so[0], 16)
        @b.tensor
        def _(e):
            n = 0
            for tt in range(NT):
                e.wait_ge(sv, SV1)
                for mat in range(3):
                    for half in range(2):
                        ps = (psA, psB, psC, psD2, psE, psF)[(mat * 2 + half)]
                        if n >= 6:
                            e.wait_ge(sv, SV1 + n - 5)
                        for kc in range(NC):
                            e.matmul(ps[:], x1T[:, kc * T + tt * 128:kc * T + (tt + 1) * 128],
                                     wTsb[:, (mat * NC + kc) * D + half * 512:
                                          (mat * NC + kc) * D + (half + 1) * 512],
                                     start=(kc == 0), stop=False)
                        mmi = e.matmul(ps[:], onesr[:],
                                       brow[0:1, mat * D + half * 512:mat * D + (half + 1) * 512],
                                       start=False, stop=True)
                        mmi.then_inc(mm, 1)
                        n += 1
        @b.vector
        def _(e):
            n = 0
            for tt in range(NT):
                for mat in range(3):
                    for half in range(2):
                        e.wait_ge(mm, n + 1)
                        slot = n % 6
                        if n >= 6:
                            e.wait_ge(sq6[slot], 16 * (n // 6))
                        ps = (psA, psB, psC, psD2, psE, psF)[(mat * 2 + half)]
                        e.tensor_copy(obuf[:, slot * 512:(slot + 1) * 512],
                                      ps[:]).then_inc(sv, 1)
                        n += 1
        @b.gpsimd
        def _(e):
            n = 0
            for tt in range(NT):
                for mat in range(3):
                    dst = (qt_o, kt_o, vt_o)[mat]
                    for half in range(2):
                        e.wait_ge(sv, SV1 + n + 1)
                        slot = n % 6
                        e.dma_start(dst[tt * 128:(tt + 1) * 128, half * 512:(half + 1) * 512],
                                    obuf[:, slot * 512:(slot + 1) * 512]).then_inc(sq6[slot], 16)
                        n += 1
            for i in range(6):
                e.wait_ge(sq6[i], 16 * 16)
            e.wait_ge(so[0], 16 * 16)

    if ag2_hook is not None:
        ag2_hook()

    SV2 = SV1 + 96
    MM2 = 96

    with nc.Block() as b:
        @b.tensor
        def _(e):
            n = 0
            for mat in range(2):
                for ot in range(NC):
                    for tc in range(4):
                        ps = (psA, psB)[n % 2]
                        if n >= 2:
                            e.wait_ge(sv, SV2 + n - 1)
                        for kc in range(NC):
                            mmi = e.matmul(ps[:], wTsb[:, (mat * NC + kc) * D + ot * 128:
                                                       (mat * NC + kc) * D + (ot + 1) * 128],
                                           x1T[:, kc * T + tc * 512:kc * T + (tc + 1) * 512],
                                           start=(kc == 0), stop=(kc == NC - 1))
                            if kc == NC - 1:
                                mmi.then_inc(mo, 1)
                        n += 1
        @b.vector
        def _(e):
            n = 0
            for mat in range(2):
                bcol = (bcolq, bcolk)[mat]
                for ot in range(NC):
                    for tc in range(4):
                        e.wait_ge(mo, n + 1)
                        if n >= 4:
                            e.wait_ge(sf[n % 4], 16 * (n // 4))
                        ps = (psA, psB)[n % 2]
                        e.tensor_scalar_add(fbuf[:, (n % 4) * 512:(n % 4 + 1) * 512],
                                            ps[:], bcol[:, ot:ot + 1]).then_inc(sv, 1)
                        n += 1
        @b.sync
        def _(e):
            n = 0
            for mat in range(2):
                dst = (qTf_o, kTf_o)[mat]
                for ot in range(NC):
                    for tc in range(4):
                        e.wait_ge(sv, SV2 + n + 1)
                        e.dma_start(dst[ot * 128:(ot + 1) * 128, tc * 512:(tc + 1) * 512],
                                    fbuf[:, (n % 4) * 512:(n % 4 + 1) * 512]).then_inc(sf[n % 4], 16)
                        n += 1
            for i in range(4):
                e.wait_ge(sf[i], 16 * 16)
    return


def build_launchD():
    nc = bass.Bass("TRN2", target_bir_lowering=False)
    io = {}
    io["x1_i"] = nc.dram_tensor("x1_i", [T, D], F32, kind="ExternalInput").ap()
    io["Wq_i"] = nc.dram_tensor("Wq_i", [D, D], F32, kind="ExternalInput").ap()
    io["Wk_i"] = nc.dram_tensor("Wk_i", [D, D], F32, kind="ExternalInput").ap()
    io["Wv_i"] = nc.dram_tensor("Wv_i", [D, D], F32, kind="ExternalInput").ap()
    io["bq_i"] = nc.dram_tensor("bq_i", [1, D], F32, kind="ExternalInput").ap()
    io["bk_i"] = nc.dram_tensor("bk_i", [1, D], F32, kind="ExternalInput").ap()
    io["bv_i"] = nc.dram_tensor("bv_i", [1, D], F32, kind="ExternalInput").ap()
    io["idb"] = nc.dram_tensor("idb", [128, 128], BF16, kind="ExternalInput").ap()
    io["qt_o"] = nc.dram_tensor("qt_o", [T, D], BF16, kind="ExternalOutput").ap()
    io["kt_o"] = nc.dram_tensor("kt_o", [T, D], BF16, kind="ExternalOutput").ap()
    io["vt_o"] = nc.dram_tensor("vt_o", [T, D], BF16, kind="ExternalOutput").ap()
    io["qTf_o"] = nc.dram_tensor("qTf_o", [D, T], BF16, kind="ExternalOutput").ap()
    io["kTf_o"] = nc.dram_tensor("kTf_o", [D, T], BF16, kind="ExternalOutput").ap()
    io["xn1T_o"] = nc.dram_tensor("xn1T_o", [D, T], BF16, kind="ExternalOutput").ap()
    st = ExitStack()
    emit_segDd(nc, "Dd_", io, st)
    st.close()
    return nc


def _launchD(resC, inputs):
    ident_bf = np.eye(128, dtype=np.float32).astype(BFDT)
    ncD = build_launchD()
    in_maps = []
    for c in range(8):
        in_maps.append({
            "x1_i": np.asarray(resC.results[c]["x1_o"]),
            "Wq_i": np.asarray(inputs["L1_attn_Wq"], np.float32),
            "Wk_i": np.asarray(inputs["L1_attn_Wk"], np.float32),
            "Wv_i": np.asarray(inputs["L1_attn_Wv"], np.float32),
            "bq_i": np.asarray(inputs["L1_attn_bq"], np.float32).reshape(1, D),
            "bk_i": np.asarray(inputs["L1_attn_bk"], np.float32).reshape(1, D),
            "bv_i": np.asarray(inputs["L1_attn_bv"], np.float32).reshape(1, D),
            "idb": ident_bf,
        })
    return run_bass_kernel_spmd(ncD, in_maps, core_ids=list(range(8)))





def emit_segEe(nc, pre, io, st, dyn_windows=False):
    qTf_i = io["qTf_i"]
    kTf_i = io["kTf_i"]
    qt_i = io["qt_i"]
    kwin_i = io["kwin_i"]
    vwin_i = io["vwin_i"]
    vfull_i = io["vfull_i"]
    Wo_i = io["Wo_i"]
    bo_i = io["bo_i"]
    mtt_i = io["mtt_i"]
    mtp_i = io["mtp_i"]
    ntt_i = io["ntt_i"]
    ntp_i = io["ntp_i"]
    idb = io["idb"]
    y1_o = io["y1_o"]

    NW = 48
    OFFT = [2, 4, 8, 16, 32]

    qTsb = _sb(st, nc, pre + "qTsb", [128, NP * T], BF16)
    kTsb = _sb(st, nc, pre + "kTsb", [128, NP * T], BF16)
    qtok = _sb(st, nc, pre + "qtok", [128, 2 * D], BF16)
    attn = _sb(st, nc, pre + "attn", [128, NT * D], BF16)
    woTsb = _sb(st, nc, pre + "woTsb", [128, NC * D], BF16)
    kw31 = _sb(st, nc, pre + "kw31", [128, D], BF16)
    kprevT = _sb(st, nc, pre + "kprevT", [128, NC * 128], BF16)
    vsrow = _sb(st, nc, pre + "vsrow", [1, D], F32)
    vsb = _sb(st, nc, pre + "vsb", [1, D], BF16)
    borow = _sb(st, nc, pre + "borow", [1, D], BF16)
    masks = _sb(st, nc, pre + "masks", [128, 4 * 128], BF16)
    identb = _sb(st, nc, pre + "identb", [128, 128], BF16)
    onesr = _sb(st, nc, pre + "onesr", [1, 128], BF16)
    onescol = _sb(st, nc, pre + "onescol", [128, 1], BF16)
    wtile = _sb(st, nc, pre + "wtile", [128, 2 * 256], F32)
    wmb = _sb(st, nc, pre + "wmb", [128, 2 * 256], BF16)
    uall = _sb(st, nc, pre + "uall", [128, 2 * D], F32)
    zall = _sb(st, nc, pre + "zall", [128, 2 * 16], F32)
    pfull = _sb(st, nc, pre + "pfull", [128, D], F32)
    ftmp = _sb(st, nc, pre + "ftmp", [128, D], F32)
    sfar = _sb(st, nc, pre + "sfar", [128, 80], F32)
    efar = _sb(st, nc, pre + "efar", [128, 80], F32)
    wfar = _sb(st, nc, pre + "wfar", [128, 80], F32)
    zf16 = _sb(st, nc, pre + "zf16", [128, 16], F32)
    zz16 = _sb(st, nc, pre + "zz16", [128, 16], F32)
    rz16 = _sb(st, nc, pre + "rz16", [128, 16], F32)
    attnT = _sb(st, nc, pre + "attnT", [128, NC * 128], BF16)
    ybuf = _sb(st, nc, pre + "ybuf", [128, 2 * 512], F32)
    pst = ExitStack()
    wosb = _sb(pst, nc, pre + "wosb", [128, NC * D], BF16)
    vf = _sb(pst, nc, pre + "vf", [128, 2 * D], BF16)

    psN0 = _ps(st, nc, pre + "psN0", [128, 256], F32)
    psN1 = _ps(st, nc, pre + "psN1", [128, 256], F32)
    psU0 = _ps(st, nc, pre + "psU0", [128, DH], F32)
    psU1 = _ps(st, nc, pre + "psU1", [128, DH], F32)
    psZ0 = _ps(st, nc, pre + "psZ0", [128, 1], F32)
    psZ1 = _ps(st, nc, pre + "psZ1", [128, 1], F32)
    psO0 = _ps(st, nc, pre + "psO0", [128, 512], F32)
    psO1 = _ps(st, nc, pre + "psO1", [128, 512], F32)
    psT0 = psN0.bitcast(BF16)[:, 0:128]

    wl = nc.alloc_semaphore(pre + "wl")
    il = nc.alloc_semaphore(pre + "il")
    ol = nc.alloc_semaphore(pre + "ol")
    vfl = [nc.alloc_semaphore(f"{pre}vfl{i}") for i in range(2)]
    klw = [nc.alloc_semaphore(f"{pre}klw{i}") for i in range(2)]
    vlw = [nc.alloc_semaphore(f"{pre}vlw{i}") for i in range(2)]
    sa = nc.alloc_semaphore(pre + "sa")
    sfx = nc.alloc_semaphore(pre + "sfx")
    sv = nc.alloc_semaphore(pre + "sv")
    mm = nc.alloc_semaphore(pre + "mm")
    wot = nc.alloc_semaphore(pre + "wot")
    mvs = nc.alloc_semaphore(pre + "mvs")
    vss = nc.alloc_semaphore(pre + "vss")
    atp = nc.alloc_semaphore(pre + "atp")
    mo = nc.alloc_semaphore(pre + "mo")
    sva = nc.alloc_semaphore(pre + "sva")
    svy = nc.alloc_semaphore(pre + "svy")
    syo = [nc.alloc_semaphore(f"{pre}syo{i}") for i in range(2)]

    WL_ALL = 16 * (8 + 8 + 1)
    WOV = 64 + 8
    VE2 = 16 * 4 + 25
    VH0 = WOV

    def VB(t):
        return VH0 + t * VE2

    with nc.Block() as b:
        @b.gpsimd
        def _(e):
            e.dma_start(identb[:], idb[:]).then_inc(il, 16)
            e.dma_start(masks[:, 0:128], mtt_i[:]).then_inc(il, 16)
            e.dma_start(masks[:, 128:256], mtp_i[:]).then_inc(il, 16)
            e.dma_start(masks[:, 256:384], ntt_i[:]).then_inc(il, 16)
            e.dma_start(masks[:, 384:512], ntp_i[:]).then_inc(il, 16)
            e.dma_start(borow[:], bo_i[:]).then_inc(il, 16)
            for oc in range(NC):
                e.dma_start(wosb[:, oc * D:(oc + 1) * D],
                            Wo_i[oc * 128:(oc + 1) * 128, :]).then_inc(il, 16)
            for pr in range(NP):
                e.dma_start(qTsb[:, pr * T:(pr + 1) * T],
                            qTf_i[pr * 128:(pr + 1) * 128, :]).then_inc(wl, 16)
                e.dma_start(kTsb[:, pr * T:(pr + 1) * T],
                            kTf_i[pr * 128:(pr + 1) * 128, :]).then_inc(wl, 16)
            e.dma_start(kw31[:], kwin_i[31 * 128:32 * 128, :]).then_inc(wl, 16)
        @b.vector
        def _(e):
            e.memset(onesr[:], 1.0)
            e.memset(onescol[:], 1.0).then_inc(ol, 1)
        @b.tensor
        def _(e):
            e.wait_ge(il, 16 * 14)
            n = 0
            for oc in range(NC):
                for ic in range(NC):
                    e.transpose(psT0, wosb[:, oc * D + ic * 128:oc * D + (ic + 1) * 128],
                                identb[:]).then_inc(wot, 1)
                    e.wait_ge(sv, n + 1)
                    n += 1
            e.wait_ge(wl, WL_ALL)
            for kc in range(NC):
                e.transpose(psT0, kw31[:, kc * 128:(kc + 1) * 128],
                            identb[:]).then_inc(wot, 1)
                e.wait_ge(sv, 64 + kc + 1)
        @b.vector
        def _(e):
            n = 0
            for oc in range(NC):
                for ic in range(NC):
                    e.wait_ge(wot, n + 1)
                    e.tensor_copy(woTsb[:, ic * D + oc * 128:ic * D + (oc + 1) * 128],
                                  psT0).then_inc(sv, 1)
                    n += 1
            for kc in range(NC):
                e.wait_ge(wot, 64 + kc + 1)
                e.tensor_copy(kprevT[:, kc * 128:(kc + 1) * 128], psT0).then_inc(sv, 1)

    with nc.Block() as b:
        @b.gpsimd
        def _(e):
            for i in range(SEQ // 128):
                if i >= 2:
                    e.wait_ge(mvs, i - 1)
                e.dma_start(vf[:, (i % 2) * D:(i % 2 + 1) * D],
                            vfull_i[i * 128:(i + 1) * 128, :]).then_inc(vfl[i % 2], 16)
        @b.tensor
        def _(e):
            e.wait_ge(ol, 1)
            for i in range(SEQ // 128):
                e.wait_ge(vfl[i % 2], 16 * (i // 2 + 1))
                first = (i == 0)
                last = (i == SEQ // 128 - 1)
                e.matmul(psO0[0:1, :], onescol[:], vf[:, (i % 2) * D:(i % 2) * D + 512],
                         start=first, stop=last)
                e.matmul(psO1[0:1, :], onescol[:], vf[:, (i % 2) * D + 512:(i % 2 + 1) * D],
                         start=first, stop=last).then_inc(mvs, 1)
        @b.vector
        def _(e):
            e.wait_ge(mvs, SEQ // 128)
            e.tensor_copy(vsrow[0:1, 0:512], psO0[0:1, :]).then_inc(vss, 1)
            e.tensor_copy(vsrow[0:1, 512:1024], psO1[0:1, :]).then_inc(vss, 1)
            e.wait_ge(vss, 2)
            e.tensor_copy(vsb[:], vsrow[:]).then_inc(vss, 1)

    pst.close()
    kfb = _sb(st, nc, pre + "kfb", [128, 2 * 5 * D], BF16)
    vfb = _sb(st, nc, pre + "vfb", [128, 2 * 5 * D], BF16)
    vnear = _sb(st, nc, pre + "vnear", [128, 2 * 2 * D], BF16)

    with nc.Block() as b:
        @b.gpsimd
        def _(e):
            for t in range(NT):
                par = t % 2
                if t >= 2:
                    e.wait_ge(sv, VB(t - 1))
                ksrc = lambda widx: kwin_i[widx * 128:(widx + 1) * 128, :]
                vsrc = lambda widx: vwin_i[widx * 128:(widx + 1) * 128, :]
                for i5, off in enumerate(OFFT):
                    widx = 32 + t - off
                    e.dma_start(kfb[:, (par * 5 + i5) * D:(par * 5 + i5 + 1) * D],
                                ksrc(widx)).then_inc(klw[par], 16)
                    e.dma_start(vfb[:, (par * 5 + i5) * D:(par * 5 + i5 + 1) * D],
                                vsrc(widx)).then_inc(klw[par], 16)
                for j, widx in enumerate((32 + t, 31 + t)):
                    e.dma_start(vnear[:, (par * 2 + j) * D:(par * 2 + j + 1) * D],
                                vsrc(widx)).then_inc(vlw[par], 16)
                e.dma_start(qtok[:, par * D:(par + 1) * D],
                            qt_i[t * 128:(t + 1) * 128, :]).then_inc(vlw[par], 16)
        @b.tensor
        def _(e):
            e.wait_ge(vss, 3)
            for t in range(NT):
                par = t % 2
                e.wait_ge(vlw[par], 48 * (t // 2 + 1))
                for h in range(H):
                    n = t * 16 + h
                    hp, hh = h // 2, h % 2
                    psN = (psN0, psN1)[n % 2]
                    psU = (psU0, psU1)[n % 2]
                    psZ = (psZ0, psZ1)[n % 2]
                    if n >= 2:
                        e.wait_ge(sa, n - 1)
                    qmv = qTsb[hh * 64:(hh + 1) * 64, hp * T + t * 128:hp * T + (t + 1) * 128]
                    e.matmul(psN[:, 0:128],
                             kTsb[hh * 64:(hh + 1) * 64, hp * T + t * 128:hp * T + (t + 1) * 128],
                             qmv, start=True, stop=True)
                    if t == 0:
                        lhs_prev = kprevT[hh * 64:(hh + 1) * 64, hp * 128:(hp + 1) * 128]
                    else:
                        lhs_prev = kTsb[hh * 64:(hh + 1) * 64,
                                        hp * T + (t - 1) * 128:hp * T + t * 128]
                    e.matmul(psN[:, 128:256], lhs_prev, qmv,
                             start=True, stop=True).then_inc(mm, 1)
                    e.wait_ge(sv, VB(t) + 4 * h + 2)
                    if n >= 2:
                        e.wait_ge(sv, VB(t) + 4 * h - 8 + 3) if False else None
                        tp2, hp2 = (n - 2) // 16, (n - 2) % 16
                        e.wait_ge(sv, VB(tp2) + 4 * hp2 + 3)
                    wpar = n % 2
                    vtt = vnear[:, (par * 2 + 0) * D + h * DH:(par * 2 + 0) * D + (h + 1) * DH]
                    vtp_ = vnear[:, (par * 2 + 1) * D + h * DH:(par * 2 + 1) * D + (h + 1) * DH]
                    wm0 = wmb[:, wpar * 256:wpar * 256 + 128]
                    wm1 = wmb[:, wpar * 256 + 128:wpar * 256 + 256]
                    e.matmul(psU[:], wm0, vtt, start=True, stop=False)
                    e.matmul(psU[:], wm1, vtp_, start=False, stop=False)
                    e.matmul(psU[:], masks[:, 256:384], vtt, start=False, stop=False)
                    e.matmul(psU[:], masks[:, 384:512], vtp_, start=False, stop=False)
                    e.matmul(psU[:], onesr[:], vsb[0:1, h * DH:(h + 1) * DH],
                             start=False, stop=True).then_inc(mm, 1)
                    if n >= 2:
                        tp2, hp2 = (n - 2) // 16, (n - 2) % 16
                        e.wait_ge(sv, VB(tp2) + 4 * hp2 + 4)
                    e.matmul(psZ[:], wm0, onescol[:], start=True, stop=False)
                    e.matmul(psZ[:], wm1, onescol[:],
                             start=False, stop=True).then_inc(mm, 1)
        @b.scalar
        def _(e):
            for t in range(NT):
                for h in range(H):
                    n = t * 16 + h
                    psN = (psN0, psN1)[n % 2]
                    e.wait_ge(mm, 3 * n + 1)
                    if n >= 2:
                        tp2, hp2 = (n - 2) // 16, (n - 2) % 16
                        e.wait_ge(sv, VB(tp2) + 4 * hp2 + 2)
                    e.activation(wtile[:, (n % 2) * 256:(n % 2 + 1) * 256], psN[:],
                                 AF.Exp, scale=ISC).then_inc(sa, 1)
                e.wait_ge(sv, VB(t) + 64 + 10)
                if t >= 2:
                    e.wait_ge(sv, VB(t - 2) + 64 + 11)
                e.activation(efar[:], sfar[:], AF.Exp, scale=ISC).then_inc(sfx, 1)
        @b.vector
        def _(e):
            for t in range(NT):
                par = t % 2
                nv = VB(t)
                def stepw(inst):
                    nonlocal nv
                    inst.then_inc(sv, 1)
                    nv += 1
                for h in range(H):
                    n = t * 16 + h
                    wpar = n % 2
                    e.wait_ge(sa, n + 1)
                    if n >= 2:
                        e.wait_ge(mm, 3 * (n - 1))
                    stepw(e.tensor_tensor(wmb[:, wpar * 256:wpar * 256 + 128],
                                          wtile[:, wpar * 256:wpar * 256 + 128],
                                          masks[:, 0:128], ALU.mult))
                    stepw(e.tensor_tensor(wmb[:, wpar * 256 + 128:wpar * 256 + 256],
                                          wtile[:, wpar * 256 + 128:wpar * 256 + 256],
                                          masks[:, 128:256], ALU.mult))
                    e.wait_ge(mm, 3 * n + 2)
                    stepw(e.tensor_copy(uall[:, par * D + h * DH:par * D + (h + 1) * DH],
                                        (psU0, psU1)[n % 2][:]))
                    e.wait_ge(mm, 3 * n + 3)
                    stepw(e.tensor_copy(zall[:, par * 16 + h:par * 16 + h + 1],
                                        (psZ0, psZ1)[n % 2][:]))
                e.wait_ge(klw[par], 160 * (t // 2 + 1))
                e.wait_ge(vlw[par], 48 * (t // 2 + 1))
                for i5 in range(5):
                    kslice = kfb[:, (par * 5 + i5) * D:(par * 5 + i5 + 1) * D]
                    e.wait_ge(sv, nv)
                    stepw(e.tensor_tensor(pfull[:], qtok[:, par * D:(par + 1) * D], kslice,
                                          ALU.mult))
                    e.wait_ge(sv, nv)
                    stepw(e.tensor_reduce(
                        bass.AP(sfar[:].tensor, sfar[:].offset + i5,
                                [[80, 128], [5, 16]]),
                        pfull[:].rearrange("p (hh d) -> p hh d", d=DH),
                        AX.X, ALU.add))
                e.wait_ge(sfx, t + 1)
                stepw(e.tensor_single_scalar(wfar[:], efar[:], 1.0, ALU.subtract))
                for i5 in range(5):
                    vslice = vfb[:, (par * 5 + i5) * D:(par * 5 + i5 + 1) * D]
                    wb = bass.AP(wfar[:].tensor, wfar[:].offset + i5,
                                 [[80, 128], [5, 16], [0, DH]])
                    e.wait_ge(sv, nv)
                    stepw(e.tensor_tensor(ftmp[:].rearrange("p (hh d) -> p hh d", d=DH),
                                          vslice.rearrange("p (hh d) -> p hh d", d=DH),
                                          wb, ALU.mult))
                    e.wait_ge(sv, nv)
                    stepw(e.tensor_tensor(uall[:, par * D:(par + 1) * D],
                                          uall[:, par * D:(par + 1) * D],
                                          ftmp[:], ALU.add))
                e.wait_ge(sv, nv)
                stepw(e.tensor_reduce(zf16[:], wfar[:].rearrange("p (hh d5) -> p hh d5", d5=5),
                                      AX.X, ALU.add))
                e.wait_ge(sv, nv)
                stepw(e.scalar_tensor_tensor(zz16[:], zf16[:], float(SEQ - 9),
                                             zall[:, par * 16:(par + 1) * 16],
                                             ALU.add, ALU.add))
                e.wait_ge(sv, nv)
                stepw(e.reciprocal(rz16[:], zz16[:]))
                rzb = rz16[:].unsqueeze(2).broadcast_to((128, 16, DH))
                e.wait_ge(sv, nv)
                stepw(e.tensor_tensor(attn[:, t * D:(t + 1) * D]
                                      .rearrange("p (hh d) -> p hh d", d=DH),
                                      uall[:, par * D:(par + 1) * D]
                                      .rearrange("p (hh d) -> p hh d", d=DH),
                                      rzb, ALU.mult))
                assert nv == VB(t) + VE2, (nv, VB(t), VE2)

    SVH = VH0 + NT * VE2

    with nc.Block() as b:
        @b.tensor
        def _(e):
            e.wait_ge(sv, SVH)
            for t in range(NT):
                for kc in range(NC):
                    na = t * NC + kc
                    if na >= 2:
                        e.wait_ge(sva, na - 1)
                    if kc < 2 and t >= 1:
                        e.wait_ge(mo, 2 * t)
                    ps = (psN0.bitcast(BF16), psN1.bitcast(BF16))[kc % 2]
                    e.transpose(ps[:, 0:128], attn[:, t * D + kc * 128:t * D + (kc + 1) * 128],
                                identb[:]).then_inc(atp, 1)
                e.wait_ge(sva, (t + 1) * NC)
                for half in range(2):
                    pso = (psO0, psO1)[half]
                    if t >= 1:
                        e.wait_ge(svy, 2 * (t - 1) + half + 1)
                    for kc in range(NC):
                        e.matmul(pso[:], attnT[:, kc * 128:(kc + 1) * 128],
                                 woTsb[:, kc * D + half * 512:kc * D + (half + 1) * 512],
                                 start=(kc == 0), stop=False)
                    e.matmul(pso[:], onesr[:],
                             borow[0:1, half * 512:(half + 1) * 512],
                             start=False, stop=True).then_inc(mo, 1)
        @b.vector
        def _(e):
            for t in range(NT):
                for kc in range(NC):
                    na = t * NC + kc
                    e.wait_ge(atp, na + 1)
                    ps = (psN0.bitcast(BF16), psN1.bitcast(BF16))[kc % 2]
                    e.tensor_copy(attnT[:, kc * 128:(kc + 1) * 128], ps[:, 0:128]).then_inc(sva, 1)
                for half in range(2):
                    e.wait_ge(mo, 2 * t + half + 1)
                    if t >= 1:
                        e.wait_ge(syo[half], 16 * t)
                    pso = (psO0, psO1)[half]
                    e.tensor_copy(ybuf[:, half * 512:(half + 1) * 512],
                                  pso[:]).then_inc(svy, 1)
        @b.gpsimd
        def _(e):
            for t in range(NT):
                for half in range(2):
                    e.wait_ge(svy, 2 * t + half + 1)
                    e.dma_start(y1_o[t * 128:(t + 1) * 128, half * 512:(half + 1) * 512],
                                ybuf[:, half * 512:(half + 1) * 512]).then_inc(syo[half], 16)
            for i in range(2):
                e.wait_ge(syo[i], 16 * 16)
    return


def build_launchE():
    nc = bass.Bass("TRN2", target_bir_lowering=False)
    io = {}
    io["qTf_i"] = nc.dram_tensor("qTf_i", [D, T], BF16, kind="ExternalInput").ap()
    io["kTf_i"] = nc.dram_tensor("kTf_i", [D, T], BF16, kind="ExternalInput").ap()
    io["qt_i"] = nc.dram_tensor("qt_i", [T, D], BF16, kind="ExternalInput").ap()
    io["kwin_i"] = nc.dram_tensor("kwin_i", [3 * T, D], BF16, kind="ExternalInput").ap()
    io["vwin_i"] = nc.dram_tensor("vwin_i", [3 * T, D], BF16, kind="ExternalInput").ap()
    io["vfull_i"] = nc.dram_tensor("vfull_i", [SEQ, D], BF16, kind="ExternalInput").ap()
    io["Wo_i"] = nc.dram_tensor("Wo_i", [D, D], F32, kind="ExternalInput").ap()
    io["bo_i"] = nc.dram_tensor("bo_i", [1, D], F32, kind="ExternalInput").ap()
    io["mtt_i"] = nc.dram_tensor("mtt_i", [128, 128], BF16, kind="ExternalInput").ap()
    io["mtp_i"] = nc.dram_tensor("mtp_i", [128, 128], BF16, kind="ExternalInput").ap()
    io["ntt_i"] = nc.dram_tensor("ntt_i", [128, 128], BF16, kind="ExternalInput").ap()
    io["ntp_i"] = nc.dram_tensor("ntp_i", [128, 128], BF16, kind="ExternalInput").ap()
    io["idb"] = nc.dram_tensor("idb", [128, 128], BF16, kind="ExternalInput").ap()
    io["y1_o"] = nc.dram_tensor("y1_o", [T, D], F32, kind="ExternalOutput").ap()
    st = ExitStack()
    emit_segEe(nc, "Ee_", io, st)
    st.close()
    return nc





def emit_segFf(nc, pre, io, st, me_src=None):
    x1_i = io["x1_i"]
    y1_i = io["y1_i"]
    xn1T_i = io["xn1T_i"]
    Wr_i = io["Wr_i"]
    Wi_i = io.get("Wi_i")
    Wo_i = io.get("Wo_i")
    sel_i = io["sel_i"]
    idb = io["idb"]
    out_o = io["out_o"]

    x1T = _sb(st, nc, pre + "x1T", [128, NC * T], BF16)
    wrsb = _sb(st, nc, pre + "wrsb", [128, NC * NEXP], BF16)
    selsb = _sb(st, nc, pre + "selsb", [NEXP, NEXP * 128], BF16)
    identb = _sb(st, nc, pre + "identb", [128, 128], BF16)
    gT = _sb(st, nc, pre + "gT", [NEXP, T], BF16)
    rcol = _sb(st, nc, pre + "rcol", [128, NEXP], F32)
    rexp = _sb(st, nc, pre + "rexp", [128, NEXP], F32)
    rmaxc = _sb(st, nc, pre + "rmaxc", [128, 1], F32)
    nmc = _sb(st, nc, pre + "nmc", [128, 1], F32)
    rsum = _sb(st, nc, pre + "rsum", [128, 1], F32)
    rrec = _sb(st, nc, pre + "rrec", [128, 1], F32)
    m1c = _sb(st, nc, pre + "m1c", [128, 1], F32)
    m2c = _sb(st, nc, pre + "m2c", [128, 1], F32)
    eq1 = _sb(st, nc, pre + "eq1", [128, NEXP], F32)
    eq2 = _sb(st, nc, pre + "eq2", [128, NEXP], F32)
    cs1 = _sb(st, nc, pre + "cs1", [128, NEXP], F32)
    cs2 = _sb(st, nc, pre + "cs2", [128, NEXP], F32)
    cs3 = _sb(st, nc, pre + "cs3", [128, NEXP], F32)
    excl = _sb(st, nc, pre + "excl", [128, NEXP], F32)
    izt = _sb(st, nc, pre + "izt", [128, NEXP], F32)
    first = _sb(st, nc, pre + "first", [128, NEXP], F32)
    second = _sb(st, nc, pre + "second", [128, NEXP], F32)
    maskd = _sb(st, nc, pre + "maskd", [128, NEXP], F32)
    selt = _sb(st, nc, pre + "selt", [128, NEXP], F32)
    gma = _sb(st, nc, pre + "gma", [128, NEXP], F32)
    gmb = _sb(st, nc, pre + "gmb", [128, NEXP], BF16)
    wisb = _sb(st, nc, pre + "wisb", [128, NC * D], BF16)
    wesb = _sb(st, nc, pre + "wesb", [128, NC * D], BF16)
    wiT = _sb(st, nc, pre + "wiT", [128, NC * D], BF16)
    mesb = _sb(st, nc, pre + "mesb", [128, NC * D], BF16)
    bcf = _sb(st, nc, pre + "bcf", [128, 128], F32)
    scb = _sb(st, nc, pre + "scb", [128, NC * 128], BF16)
    ymoe = _sb(st, nc, pre + "ymoe", [128, NT * D], F32)
    xres = _sb(st, nc, pre + "xres", [128, 2 * D], F32)
    yres = _sb(st, nc, pre + "yres", [128, 2 * D], F32)
    tadd = _sb(st, nc, pre + "tadd", [128, D], F32)
    obuf = _sb(st, nc, pre + "obuf", [128, 2 * D], F32)

    psR = _ps(st, nc, pre + "psR", [128, NEXP], F32)
    psT0 = _ps(st, nc, pre + "psT0", [128, 128], BF16)
    psT1 = _ps(st, nc, pre + "psT1", [128, 128], BF16)
    psM0 = _ps(st, nc, pre + "psM0", [128, 512], F32)
    psM1 = _ps(st, nc, pre + "psM1", [128, 512], F32)
    psB = _ps(st, nc, pre + "psB", [128, 128], F32) if me_src is None else None
    psB5 = _ps(st, nc, pre + "psB5", [128, 512], F32) if me_src is not None else None
    psY0 = _ps(st, nc, pre + "psY0", [128, 512], F32)
    psY1 = _ps(st, nc, pre + "psY1", [128, 512], F32)

    il = nc.alloc_semaphore(pre + "il")
    wl = nc.alloc_semaphore(pre + "wl")
    wle = nc.alloc_semaphore(pre + "wle")
    rl = [nc.alloc_semaphore(f"{pre}rl{i}") for i in range(2)]
    sa = nc.alloc_semaphore(pre + "sa")
    sr = nc.alloc_semaphore(pre + "sr")
    mr = nc.alloc_semaphore(pre + "mr")
    gt = nc.alloc_semaphore(pre + "gt")
    wit = nc.alloc_semaphore(pre + "wit")
    sv = nc.alloc_semaphore(pre + "sv")
    mmm = nc.alloc_semaphore(pre + "mmm")
    mby = nc.alloc_semaphore(pre + "mby")
    sxo = [nc.alloc_semaphore(f"{pre}sxo{i}") for i in range(2)]

    WL_ALL = 16 * (NC + NC + NEXP)

    with nc.Block() as b:
        @b.gpsimd
        def _(e):
            e.dma_start(identb[:], idb[:]).then_inc(il, 16)
            for kc in range(NC):
                e.dma_start(x1T[:, kc * T:(kc + 1) * T],
                            xn1T_i[kc * 128:(kc + 1) * 128, :]).then_inc(wl, 16)
                e.dma_start(wrsb[:, kc * NEXP:(kc + 1) * NEXP],
                            Wr_i[kc * 128:(kc + 1) * 128, :]).then_inc(wl, 16)
            for ee in range(NEXP):
                e.dma_start(selsb[:, ee * 128:(ee + 1) * 128], sel_i[ee]).then_inc(wl, 16)

    pos_gb = []
    pos_end = []
    pos_nm = []
    with nc.Block() as b:
        @b.vector
        def _(e):
            nv = 0
            for tt in range(NT):
                def stepw(inst):
                    nonlocal nv
                    inst.then_inc(sr, 1)
                    nv += 1
                e.wait_ge(mr, tt + 1)
                stepw(e.tensor_copy(rcol[:], psR[:]))
                e.wait_ge(sr, nv)
                stepw(e.tensor_reduce(rmaxc[:], rcol[:], AX.X, ALU.max))
                e.wait_ge(sr, nv)
                stepw(e.tensor_scalar_mul(nmc[:], rmaxc[:], -1.0))
                pos_nm.append(nv)
                e.wait_ge(sa, tt + 1)
                stepw(e.reciprocal(rrec[:], rsum[:]))
                stepw(e.tensor_reduce(m1c[:], rexp[:], AX.X, ALU.max))
                e.wait_ge(sr, nv)
                stepw(e.tensor_scalar(eq1[:], rexp[:], m1c[:], None, ALU.is_ge))
                e.wait_ge(sr, nv)
                stepw(e.tensor_copy(cs1[:, 0:1], eq1[:, 0:1]))
                stepw(e.tensor_tensor(cs1[:, 1:8], eq1[:, 1:8], eq1[:, 0:7], ALU.add))
                e.wait_ge(sr, nv)
                stepw(e.tensor_copy(cs2[:, 0:2], cs1[:, 0:2]))
                stepw(e.tensor_tensor(cs2[:, 2:8], cs1[:, 2:8], cs1[:, 0:6], ALU.add))
                e.wait_ge(sr, nv)
                stepw(e.tensor_copy(cs3[:, 0:4], cs2[:, 0:4]))
                stepw(e.tensor_tensor(cs3[:, 4:8], cs2[:, 4:8], cs2[:, 0:4], ALU.add))
                e.wait_ge(sr, nv)
                stepw(e.tensor_sub(excl[:], cs3[:], eq1[:]))
                e.wait_ge(sr, nv)
                stepw(e.tensor_single_scalar(izt[:], excl[:], 0.5, ALU.is_lt))
                e.wait_ge(sr, nv)
                stepw(e.tensor_mul(first[:], eq1[:], izt[:]))
                e.wait_ge(sr, nv)
                stepw(e.scalar_tensor_tensor(maskd[:], first[:], -1e30, rexp[:],
                                             ALU.mult, ALU.add))
                e.wait_ge(sr, nv)
                stepw(e.tensor_reduce(m2c[:], maskd[:], AX.X, ALU.max))
                e.wait_ge(sr, nv)
                stepw(e.tensor_scalar(eq2[:], maskd[:], m2c[:], None, ALU.is_ge))
                e.wait_ge(sr, nv)
                stepw(e.tensor_copy(cs1[:, 0:1], eq2[:, 0:1]))
                stepw(e.tensor_tensor(cs1[:, 1:8], eq2[:, 1:8], eq2[:, 0:7], ALU.add))
                e.wait_ge(sr, nv)
                stepw(e.tensor_copy(cs2[:, 0:2], cs1[:, 0:2]))
                stepw(e.tensor_tensor(cs2[:, 2:8], cs1[:, 2:8], cs1[:, 0:6], ALU.add))
                e.wait_ge(sr, nv)
                stepw(e.tensor_copy(cs3[:, 0:4], cs2[:, 0:4]))
                stepw(e.tensor_tensor(cs3[:, 4:8], cs2[:, 4:8], cs2[:, 0:4], ALU.add))
                e.wait_ge(sr, nv)
                stepw(e.tensor_sub(excl[:], cs3[:], eq2[:]))
                e.wait_ge(sr, nv)
                stepw(e.tensor_single_scalar(izt[:], excl[:], 0.5, ALU.is_lt))
                e.wait_ge(sr, nv)
                stepw(e.tensor_mul(second[:], eq2[:], izt[:]))
                e.wait_ge(sr, nv)
                stepw(e.tensor_add(selt[:], first[:], second[:]))
                e.wait_ge(sr, nv)
                stepw(e.tensor_mul(gma[:], selt[:], rexp[:]))
                e.wait_ge(sr, nv)
                stepw(e.tensor_scalar_mul(gma[:], gma[:], rrec[:]))
                e.wait_ge(sr, nv)
                stepw(e.tensor_copy(gmb[:], gma[:]))
                pos_gb.append(nv)
                e.wait_ge(gt, tt + 1)
                stepw(e.tensor_copy(gT[0:NEXP, tt * 128:(tt + 1) * 128], psT0[0:NEXP, :]))
                pos_end.append(nv)
        @b.scalar
        def _(e):
            for tt in range(NT):
                e.wait_ge(sr, pos_nm[tt])
                e.activation(rexp[:], rcol[:], AF.Exp, bias=nmc[:],
                             accum_out=rsum[:]).then_inc(sa, 1)
        @b.tensor
        def _(e):
            e.wait_ge(wl, WL_ALL)
            e.wait_ge(il, 16)
            for tt in range(NT):
                if tt >= 1:
                    e.wait_ge(sr, pos_end[tt - 1] - 1)
                for kc in range(NC):
                    mi = e.matmul(psR[:], x1T[:, kc * T + tt * 128:kc * T + (tt + 1) * 128],
                                  wrsb[:, kc * NEXP:(kc + 1) * NEXP],
                                  start=(kc == 0), stop=(kc == NC - 1))
                    if kc == NC - 1:
                        mi.then_inc(mr, 1)
                e.wait_ge(sr, pos_gb[tt])
                if tt >= 1:
                    e.wait_ge(sr, pos_end[tt - 1])
                e.transpose(psT0[0:NEXP, :], gmb[:], identb[:]).then_inc(gt, 1)
    RV = 31

    if me_src is not None:
        scb5 = _sb(st, nc, pre + "scb5", [128, NC * 512], BF16)
        bcf5 = _sb(st, nc, pre + "bcf5", [128, 512], F32)
        with nc.Block() as b:
            @b.gpsimd
            def _(e):
                for ee in range(NEXP):
                    if ee >= 1:
                        e.wait_ge(mby, ee * 36)
                    for fi in range(NC):
                        e.dma_start(mesb[:, fi * D:(fi + 1) * D],
                                    me_src[ee * D + fi * 128:ee * D + (fi + 1) * 128, :]
                                    ).then_inc(wle, 16)
            @b.vector
            def _(e):
                for ee in range(NEXP):
                    EB = ee * 68
                    for tc in range(4):
                        b0 = EB + tc * 17
                        e.wait_ge(mby, ee * 36 + tc * 9 + 1)
                        e.tensor_copy(bcf5[:], psB5[:]).then_inc(sv, 1)
                        for kc in range(NC):
                            e.wait_ge(sv, b0 + 1 + kc)
                            e.tensor_tensor(scb5[:, kc * 512:(kc + 1) * 512],
                                            x1T[:, kc * T + tc * 512:kc * T + (tc + 1) * 512],
                                            bcf5[:], ALU.mult).then_inc(sv, 1)
                        for tsub in range(4):
                            tt = tc * 4 + tsub
                            for half in range(2):
                                e.wait_ge(mby, ee * 36 + tc * 9 + 1 + tsub * 2 + half + 1)
                                ps = (psY0, psY1)[half]
                                dst = ymoe[:, tt * D + half * 512:tt * D + (half + 1) * 512]
                                if ee == 0:
                                    e.tensor_copy(dst, ps[:]).then_inc(sv, 1)
                                else:
                                    e.tensor_tensor(dst, dst, ps[:], ALU.add).then_inc(sv, 1)
            @b.tensor
            def _(e):
                e.wait_ge(sr, pos_end[NT - 1])
                for ee in range(NEXP):
                    EB = ee * 68
                    e.wait_ge(wle, 16 * NC * (ee + 1))
                    for tc in range(4):
                        b0 = EB + tc * 17
                        if ee + tc > 0:
                            prev = (EB + (tc - 1) * 17 + 1) if tc >= 1 else (EB - 68 + 3 * 17 + 1)
                            e.wait_ge(sv, prev)
                        e.matmul(psB5[:], selsb[:, ee * 128:(ee + 1) * 128],
                                 gT[0:NEXP, tc * 512:(tc + 1) * 512],
                                 start=True, stop=True).then_inc(mby, 1)
                        e.wait_ge(sv, b0 + 9)
                        for tsub in range(4):
                            for half in range(2):
                                ps = (psY0, psY1)[half]
                                if tsub >= 1:
                                    e.wait_ge(sv, b0 + 9 + (tsub - 1) * 2 + half + 1)
                                elif ee + tc > 0:
                                    pv = b0 - 17 + 9 + 3 * 2 + half + 1 if tc >= 1 else EB - 68 + 3 * 17 + 9 + 3 * 2 + half + 1
                                    e.wait_ge(sv, pv)
                                for kc in range(NC):
                                    mi = e.matmul(ps[:], scb5[:, kc * 512 + tsub * 128:kc * 512 + tsub * 128 + 128],
                                                  mesb[:, kc * D + half * 512:kc * D + (half + 1) * 512],
                                                  start=(kc == 0), stop=(kc == NC - 1))
                                    if kc == NC - 1:
                                        mi.then_inc(mby, 1)
    else:
        with nc.Block() as b:
            @b.gpsimd
            def _(e):
                for ee in range(NEXP):
                    if me_src is not None:
                        if ee >= 1:
                            e.wait_ge(mby, ee * NT * 3)
                        for fi in range(NC):
                            e.dma_start(mesb[:, fi * D:(fi + 1) * D],
                                        me_src[ee * D + fi * 128:ee * D + (fi + 1) * 128, :]
                                        ).then_inc(wle, 16)
                        continue
                    if ee >= 1:
                        e.wait_ge(wit, ee * 64)
                        e.wait_ge(mmm, ee * 16)
                    for kc in range(NC):
                        e.dma_start(wisb[:, kc * D:(kc + 1) * D],
                                    Wi_i[ee, kc * 128:(kc + 1) * 128, :]).then_inc(wle, 16)
                        e.dma_start(wesb[:, kc * D:(kc + 1) * D],
                                    Wo_i[ee, kc * 128:(kc + 1) * 128, :]).then_inc(wle, 16)
            @b.vector
            def _(e):
                nv = 0
                for ee in range(NEXP):
                    if me_src is None:
                        for n in range(64):
                            e.wait_ge(wit, ee * 64 + n + 1)
                            ps = (psT0, psT1)[n % 2]
                            oc, ic = n // NC, n % NC
                            e.tensor_copy(wiT[:, ic * D + oc * 128:ic * D + (oc + 1) * 128],
                                          ps[:]).then_inc(sv, 1)
                            nv += 1
                        for j in range(16):
                            fi, half = j // 2, j % 2
                            e.wait_ge(mmm, ee * 16 + j + 1)
                            ps = (psM0, psM1)[j % 2]
                            e.tensor_copy(mesb[:, fi * D + half * 512:fi * D + (half + 1) * 512],
                                          ps[:]).then_inc(sv, 1)
                            nv += 1
                    for tt in range(NT):
                        e.wait_ge(mby, (ee * NT + tt) * 3 + 1)
                        e.tensor_copy(bcf[:], psB[:]).then_inc(sv, 1)
                        nv += 1
                        for kc in range(NC):
                            e.wait_ge(sv, nv)
                            e.tensor_tensor(scb[:, kc * 128:(kc + 1) * 128],
                                            x1T[:, kc * T + tt * 128:kc * T + (tt + 1) * 128],
                                            bcf[:], ALU.mult).then_inc(sv, 1)
                            nv += 1
                        for half in range(2):
                            e.wait_ge(mby, (ee * NT + tt) * 3 + 1 + half + 1)
                            ps = (psY0, psY1)[half]
                            dst = ymoe[:, tt * D + half * 512:tt * D + (half + 1) * 512]
                            if ee == 0:
                                e.tensor_copy(dst, ps[:]).then_inc(sv, 1)
                            else:
                                e.tensor_tensor(dst, dst, ps[:], ALU.add).then_inc(sv, 1)
                            nv += 1
            @b.tensor
            def _(e):
                VPE = (64 + 16 + NT * 11) if me_src is None else (NT * 11)
                MEOFF = 80 if me_src is None else 0
                e.wait_ge(sr, pos_end[NT - 1])
                for ee in range(NEXP):
                    if me_src is None:
                        e.wait_ge(wle, 16 * 2 * NC * (ee + 1))
                        for n in range(64):
                            oc, ic = n // NC, n % NC
                            if n >= 2:
                                e.wait_ge(sv, ee * VPE + n - 1)
                            elif ee >= 1:
                                e.wait_ge(sv, ee * VPE)
                            ps = (psT0, psT1)[n % 2]
                            e.transpose(ps[:], wisb[:, oc * D + ic * 128:oc * D + (ic + 1) * 128],
                                        identb[:]).then_inc(wit, 1)
                        e.wait_ge(sv, ee * VPE + 64)
                        for j in range(16):
                            fi, half = j // 2, j % 2
                            if j >= 2:
                                e.wait_ge(sv, ee * VPE + 64 + j - 1)
                            ps = (psM0, psM1)[j % 2]
                            for hc in range(NC):
                                mi = e.matmul(ps[:], wiT[:, hc * D + fi * 128:hc * D + (fi + 1) * 128],
                                              wesb[:, hc * D + half * 512:hc * D + (half + 1) * 512],
                                              start=(hc == 0), stop=(hc == NC - 1))
                                if hc == NC - 1:
                                    mi.then_inc(mmm, 1)
                    else:
                        e.wait_ge(wle, 16 * NC * (ee + 1))
                    for tt in range(NT):
                        base = ee * VPE + MEOFF + tt * 11
                        if ee * NT + tt >= 1:
                            e.wait_ge(sv, base - 10)
                        e.matmul(psB[:], selsb[:, ee * 128:(ee + 1) * 128],
                                 gT[0:NEXP, tt * 128:(tt + 1) * 128],
                                 start=True, stop=True).then_inc(mby, 1)
                        e.wait_ge(sv, base + 9)
                        for half in range(2):
                            ps = (psY0, psY1)[half]
                            if ee * NT + tt >= 1:
                                e.wait_ge(sv, base - 2 + half + 1)
                            for kc in range(NC):
                                mi = e.matmul(ps[:], scb[:, kc * 128:(kc + 1) * 128],
                                              mesb[:, kc * D + half * 512:kc * D + (half + 1) * 512],
                                              start=(kc == 0), stop=(kc == NC - 1))
                                if kc == NC - 1:
                                    mi.then_inc(mby, 1)


    SVM = NEXP * ((64 + 16 + NT * 11) if me_src is None else 68)

    with nc.Block() as b:
        @b.gpsimd
        def _(e):
            for tt in range(NT + 1):
                par = tt % 2
                if tt < NT:
                    if tt >= 2:
                        e.wait_ge(sv, SVM + (tt - 1) * 2)
                    e.dma_start(xres[:, par * D:(par + 1) * D],
                                x1_i[tt * 128:(tt + 1) * 128, :]).then_inc(rl[par], 16)
                    e.dma_start(yres[:, par * D:(par + 1) * D],
                                y1_i[tt * 128:(tt + 1) * 128, :]).then_inc(rl[par], 16)
                if tt >= 1:
                    st = tt - 1
                    e.wait_ge(sv, SVM + st * 2 + 2)
                    e.dma_start(out_o[st * 128:(st + 1) * 128, :],
                                obuf[:, (st % 2) * D:(st % 2) * D + D]).then_inc(sxo[st % 2], 16)
            for i in range(2):
                e.wait_ge(sxo[i], 16 * 8)
        @b.vector
        def _(e):
            for tt in range(NT):
                par = tt % 2
                e.wait_ge(rl[par], 32 * (tt // 2 + 1))
                if tt >= 2:
                    e.wait_ge(sxo[par], 16 * (tt // 2))
                e.wait_ge(sv, SVM + tt * 2)
                e.tensor_add(tadd[:], ymoe[:, tt * D:(tt + 1) * D],
                             xres[:, par * D:(par + 1) * D]).then_inc(sv, 1)
                e.wait_ge(sv, SVM + tt * 2 + 1)
                e.tensor_add(obuf[:, par * D:(par + 1) * D], tadd[:],
                             yres[:, par * D:(par + 1) * D]).then_inc(sv, 1)
    return


def build_launchF():
    nc = bass.Bass("TRN2", target_bir_lowering=False)
    io = {}
    io["x1_i"] = nc.dram_tensor("x1_i", [T, D], F32, kind="ExternalInput").ap()
    io["y1_i"] = nc.dram_tensor("y1_i", [T, D], F32, kind="ExternalInput").ap()
    io["xn1T_i"] = nc.dram_tensor("xn1T_i", [D, T], BF16, kind="ExternalInput").ap()
    io["Wr_i"] = nc.dram_tensor("Wr_i", [D, NEXP], F32, kind="ExternalInput").ap()
    io["Wi_i"] = nc.dram_tensor("Wi_i", [NEXP, D, D], F32, kind="ExternalInput").ap()
    io["Wo_i"] = nc.dram_tensor("Wo_i", [NEXP, D, D], F32, kind="ExternalInput").ap()
    io["sel_i"] = nc.dram_tensor("sel_i", [NEXP, NEXP, 128], BF16, kind="ExternalInput").ap()
    io["idb"] = nc.dram_tensor("idb", [128, 128], BF16, kind="ExternalInput").ap()
    io["out_o"] = nc.dram_tensor("out_o", [T, D], F32, kind="ExternalOutput").ap()
    st = ExitStack()
    emit_segFf(nc, "Ff_", io, st)
    st.close()
    return nc


def emit_segEF(nc, pre, io, st, me_src=None, ag3=None, ag2=None):
    qTf_i = io["qTf_i"]
    kTf_i = io["kTf_i"]
    qt_i = io["qt_i"]
    kwin_i = io["kwin_i"]
    vwin_i = io["vwin_i"]
    vfull_i = io["vfull_i"]
    Wo_i = io["Wo_i"]
    bo_i = io["bo_i"]
    mtt_i = io["mtt_i"]
    mtp_i = io["mtp_i"]
    ntt_i = io["ntt_i"]
    ntp_i = io["ntp_i"]
    idb = io["idb"]
    x1_i = io["x1_i"]
    y1_io = io["y1_io"]
    xn1T_i = io["xn1T_i"]
    Wr_i = io["Wr_i"]
    out_o = io["out_o"]

    OFFT = [2, 4, 8, 16, 32]

    identb = _sb(st, nc, pre + "identb", [128, 128], BF16)
    masks = _sb(st, nc, pre + "masks", [128, 6 * 128], BF16)
    onesr = _sb(st, nc, pre + "onesr", [1, 128], BF16)
    onescol = _sb(st, nc, pre + "onescol", [128, 1], BF16)
    borow = _sb(st, nc, pre + "borow", [1, D], BF16)
    vsrow = _sb(st, nc, pre + "vsrow", [1, D], F32)
    vsb = _sb(st, nc, pre + "vsb", [1, D], BF16)
    kprevT = _sb(st, nc, pre + "kprevT", [128, NC * 128], BF16)
    qtok = _sb(st, nc, pre + "qtok", [128, 2 * D], BF16)
    attn = _sb(st, nc, pre + "attn", [128, 2 * D], BF16)
    woTsb = _sb(st, nc, pre + "woTsb", [128, NC * D], BF16)
    wtile = _sb(st, nc, pre + "wtile", [128, 2 * 256], BF16)
    zcol = _sb(st, nc, pre + "zcol", [128, 2], F32)
    uall = _sb(st, nc, pre + "uall", [128, 2 * D], F32)
    zall = _sb(st, nc, pre + "zall", [128, 2 * 16], F32)
    pfull = _sb(st, nc, pre + "pfull", [128, D], F32)
    ftmp = _sb(st, nc, pre + "ftmp", [128, D], F32)
    sfar = _sb(st, nc, pre + "sfar", [128, 80], F32)
    efar = _sb(st, nc, pre + "efar", [128, 80], F32)
    wfar = _sb(st, nc, pre + "wfar", [128, 80], F32)
    zf16 = _sb(st, nc, pre + "zf16", [128, 16], F32)
    zz16 = _sb(st, nc, pre + "zz16", [128, 16], F32)
    rz16 = _sb(st, nc, pre + "rz16", [128, 16], F32)
    attnT = _sb(st, nc, pre + "attnT", [128, NC * 128], BF16)
    ybuf = _sb(st, nc, pre + "ybuf", [128, 2 * 512], F32)
    gall = _sb(st, nc, pre + "gall", [128, NT * NEXP], F32)
    wrsb = _sb(st, nc, pre + "wrsb", [128, NC * NEXP], BF16)
    RW = NT * NEXP
    rcol = _sb(st, nc, pre + "rcol", [128, RW], F32)
    rsh = _sb(st, nc, pre + "rsh", [128, RW], F32)
    rexp = _sb(st, nc, pre + "rexp", [128, RW], F32)
    rmaxc = _sb(st, nc, pre + "rmaxc", [128, NT], F32)
    rsum = _sb(st, nc, pre + "rsum", [128, NT], F32)
    rrec = _sb(st, nc, pre + "rrec", [128, NT], F32)
    m1c = _sb(st, nc, pre + "m1c", [128, NT], F32)
    m2c = _sb(st, nc, pre + "m2c", [128, NT], F32)
    eq1 = _sb(st, nc, pre + "eq1", [128, RW], F32)
    eq2 = _sb(st, nc, pre + "eq2", [128, RW], F32)
    cs1 = _sb(st, nc, pre + "cs1", [128, RW], F32)
    cs2 = _sb(st, nc, pre + "cs2", [128, RW], F32)
    cs3 = _sb(st, nc, pre + "cs3", [128, RW], F32)
    excl = _sb(st, nc, pre + "excl", [128, RW], F32)
    izt = _sb(st, nc, pre + "izt", [128, RW], F32)
    first = _sb(st, nc, pre + "first", [128, RW], F32)
    second = _sb(st, nc, pre + "second", [128, RW], F32)
    maskd = _sb(st, nc, pre + "maskd", [128, RW], F32)
    selt = _sb(st, nc, pre + "selt", [128, RW], F32)
    gma = _sb(st, nc, pre + "gma", [128, RW], F32)
    pst = ExitStack()
    wosb = _sb(pst, nc, pre + "wosb", [128, NC * D], BF16)
    vf = _sb(pst, nc, pre + "vf", [128, 2 * D], BF16)
    kw31 = _sb(pst, nc, pre + "kw31", [128, D], BF16)
    x1Tr = _sb(pst, nc, pre + "x1Tr", [128, 2 * NC * 128], BF16)

    psN0 = _ps(st, nc, pre + "psN0", [128, 256], F32)
    psN1 = _ps(st, nc, pre + "psN1", [128, 256], F32)
    psU0 = _ps(st, nc, pre + "psU0", [128, DH + 1], F32)
    psU1 = _ps(st, nc, pre + "psU1", [128, DH + 1], F32)
    psY0 = _ps(st, nc, pre + "psY0", [128, 512], F32)
    psY1 = _ps(st, nc, pre + "psY1", [128, 512], F32)
    psO0 = _ps(st, nc, pre + "psO0", [128, 512], F32)
    psO1 = _ps(st, nc, pre + "psO1", [128, 512], F32)

    il = nc.alloc_semaphore(pre + "il")
    wl = nc.alloc_semaphore(pre + "wl")
    wrl = nc.alloc_semaphore(pre + "wrl")
    wcs = nc.alloc_semaphore(pre + "wcs")
    ol = nc.alloc_semaphore(pre + "ol")
    vfl = [nc.alloc_semaphore(f"{pre}vfl{i}") for i in range(2)]
    klw = [nc.alloc_semaphore(f"{pre}klw{i}") for i in range(2)]
    vlw = [nc.alloc_semaphore(f"{pre}vlw{i}") for i in range(2)]
    ktl = [nc.alloc_semaphore(f"{pre}ktl{i}") for i in range(3)]
    xrl = [nc.alloc_semaphore(f"{pre}xrl{i}") for i in range(2)]
    xsl = [nc.alloc_semaphore(f"{pre}xsl{i}") for i in range(2)]
    sa = nc.alloc_semaphore(pre + "sa")
    sar = nc.alloc_semaphore(pre + "sar")
    sfx = nc.alloc_semaphore(pre + "sfx")
    sv = nc.alloc_semaphore(pre + "sv")
    sr = nc.alloc_semaphore(pre + "sr")
    svf = nc.alloc_semaphore(pre + "svf")
    mm = nc.alloc_semaphore(pre + "mm")
    mr = nc.alloc_semaphore(pre + "mr")
    mby = nc.alloc_semaphore(pre + "mby")
    mo = nc.alloc_semaphore(pre + "mo")
    wot = nc.alloc_semaphore(pre + "wot")
    mvs = nc.alloc_semaphore(pre + "mvs")
    vss = nc.alloc_semaphore(pre + "vss")
    wle = nc.alloc_semaphore(pre + "wle")
    atp = nc.alloc_semaphore(pre + "atp")
    sva = nc.alloc_semaphore(pre + "sva")
    svy = nc.alloc_semaphore(pre + "svy")
    syo = [nc.alloc_semaphore(f"{pre}syo{i}") for i in range(2)]
    rlx = [nc.alloc_semaphore(f"{pre}rlx{i}") for i in range(2)]
    sxo = [nc.alloc_semaphore(f"{pre}sxo{i}") for i in range(2)]
    svt = nc.alloc_semaphore(pre + "svt")

    IL_N = 22
    WOV = 64 + 8
    VE2 = 16 * 2 + 25
    VH0 = WOV

    def VB(t):
        return VH0 + t * VE2

    SVH = VH0 + NT * VE2

    with nc.Block() as b:
        @b.gpsimd
        def _(e):
            e.dma_start(identb[:], idb[:]).then_inc(il, 16)
            e.dma_start(masks[:, 0:128], mtt_i[:]).then_inc(il, 16)
            e.dma_start(masks[:, 128:256], mtp_i[:]).then_inc(il, 16)
            e.dma_start(masks[:, 256:384], ntt_i[:]).then_inc(il, 16)
            e.dma_start(masks[:, 384:512], ntp_i[:]).then_inc(il, 16)
            e.dma_start(borow[:], bo_i[:]).then_inc(il, 16)
            for oc in range(NC):
                e.dma_start(wosb[:, oc * D:(oc + 1) * D],
                            Wo_i[oc * 128:(oc + 1) * 128, :]).then_inc(il, 16)
            for kc in range(NC):
                e.dma_start(wrsb[:, kc * NEXP:(kc + 1) * NEXP],
                            Wr_i[kc * 128:(kc + 1) * 128, :]).then_inc(wrl, 16)
        @b.sync
        def _(e):
            if ag2 is not None:
                e.wait_ge(ag2[0], ag2[1])
                pid = e.partition_id()
                woff = (pid % 4) * (T * D)
                ktp_i, vtp_i = io["ktp_i"], io["vtp_i"]
                e.dma_start(kwin_i, bass.AP(ktp_i.tensor, woff, [[D, 3 * T], [1, D]])
                            ).then_inc(wcs, 16)
                e.dma_start(vwin_i, bass.AP(vtp_i.tensor, woff, [[D, 3 * T], [1, D]])
                            ).then_inc(wcs, 16)
                e.wait_ge(wcs, 32)
            e.dma_start(kw31[:], kwin_i[31 * 128:32 * 128, :]).then_inc(wl, 16)
        @b.vector
        def _(e):
            e.memset(onesr[:], 1.0)
            e.memset(onescol[:], 1.0).then_inc(ol, 1)
            e.wait_ge(il, 16 * 14)
            e.tensor_scalar(masks[:, 512:640], masks[:, 0:128], 1.0, 3200.0,
                            ALU.subtract, ALU.mult).then_inc(ol, 1)
            e.wait_ge(ol, 2)
            e.tensor_scalar(masks[:, 640:768], masks[:, 128:256], 1.0, 3200.0,
                            ALU.subtract, ALU.mult).then_inc(ol, 1)
        @b.tensor
        def _(e):
            e.wait_ge(il, 16 * 14)
            psT0 = psN0.bitcast(BF16)[:, 0:128]
            n = 0
            for oc in range(NC):
                for ic in range(NC):
                    e.transpose(psT0, wosb[:, oc * D + ic * 128:oc * D + (ic + 1) * 128],
                                identb[:]).then_inc(wot, 1)
                    e.wait_ge(sv, n + 1)
                    n += 1
            e.wait_ge(wl, 16)
            for kc in range(NC):
                e.transpose(psT0, kw31[:, kc * 128:(kc + 1) * 128],
                            identb[:]).then_inc(wot, 1)
                e.wait_ge(sv, 64 + kc + 1)
        @b.vector
        def _(e):
            psT0 = psN0.bitcast(BF16)[:, 0:128]
            n = 0
            for oc in range(NC):
                for ic in range(NC):
                    e.wait_ge(wot, n + 1)
                    e.tensor_copy(woTsb[:, ic * D + oc * 128:ic * D + (oc + 1) * 128],
                                  psT0).then_inc(sv, 1)
                    n += 1
            for kc in range(NC):
                e.wait_ge(wot, 64 + kc + 1)
                e.tensor_copy(kprevT[:, kc * 128:(kc + 1) * 128], psT0).then_inc(sv, 1)

    with nc.Block() as b:
        @b.gpsimd
        def _(e):
            if ag2 is not None:
                e.wait_ge(ag2[0], ag2[1])
            for i in range(SEQ // 128):
                if i >= 2:
                    e.wait_ge(mvs, i - 1)
                e.dma_start(vf[:, (i % 2) * D:(i % 2 + 1) * D],
                            vfull_i[i * 128:(i + 1) * 128, :]).then_inc(vfl[i % 2], 16)
        @b.tensor
        def _(e):
            e.wait_ge(ol, 1)
            for i in range(SEQ // 128):
                e.wait_ge(vfl[i % 2], 16 * (i // 2 + 1))
                first_ = (i == 0)
                last = (i == SEQ // 128 - 1)
                e.matmul(psY0[0:1, :], onescol[:], vf[:, (i % 2) * D:(i % 2) * D + 512],
                         start=first_, stop=last)
                e.matmul(psY1[0:1, :], onescol[:], vf[:, (i % 2) * D + 512:(i % 2 + 1) * D],
                         start=first_, stop=last).then_inc(mvs, 1)
        @b.vector
        def _(e):
            e.wait_ge(mvs, SEQ // 128)
            e.tensor_copy(vsrow[0:1, 0:512], psY0[0:1, :]).then_inc(vss, 1)
            e.tensor_copy(vsrow[0:1, 512:1024], psY1[0:1, :]).then_inc(vss, 1)
            e.wait_ge(vss, 2)
            e.tensor_copy(vsb[:], vsrow[:]).then_inc(vss, 1)

    pos_rsh = [0]
    with nc.Block() as b:
        def g3(tile, off=0, w=NEXP):
            base = tile[:]
            return bass.AP(base.tensor, base.offset + off,
                           [[128, 128], [NEXP, NT], [1, w]])

        def bc16(tile):
            return tile[:].unsqueeze(2).broadcast_to((128, NT, NEXP))

        @b.vector
        def _(e):
            nvr = 0
            def stepw(inst):
                nonlocal nvr
                inst.then_inc(sr, 1)
                nvr += 1
            e.wait_ge(mr, NT)
            stepw(e.tensor_copy(rcol[:], psO0[:, 0:128]))
            e.wait_ge(sr, nvr)
            stepw(e.tensor_reduce(rmaxc[:], g3(rcol), AX.X, ALU.max))
            e.wait_ge(sr, nvr)
            stepw(e.tensor_tensor(g3(rsh), g3(rcol), bc16(rmaxc), ALU.subtract))
            pos_rsh[0] = nvr
            e.wait_ge(sar, 1)
            stepw(e.tensor_reduce(rsum[:], g3(rexp), AX.X, ALU.add))
            e.wait_ge(sr, nvr)
            stepw(e.reciprocal(rrec[:], rsum[:]))
            stepw(e.tensor_reduce(m1c[:], g3(rexp), AX.X, ALU.max))
            e.wait_ge(sr, nvr)
            stepw(e.tensor_tensor(g3(eq1), g3(rexp), bc16(m1c), ALU.is_ge))
            e.wait_ge(sr, nvr)
            stepw(e.tensor_copy(g3(cs1, 0, 1), g3(eq1, 0, 1)))
            stepw(e.tensor_tensor(g3(cs1, 1, 7), g3(eq1, 1, 7), g3(eq1, 0, 7), ALU.add))
            e.wait_ge(sr, nvr)
            stepw(e.tensor_copy(g3(cs2, 0, 2), g3(cs1, 0, 2)))
            stepw(e.tensor_tensor(g3(cs2, 2, 6), g3(cs1, 2, 6), g3(cs1, 0, 6), ALU.add))
            e.wait_ge(sr, nvr)
            stepw(e.tensor_copy(g3(cs3, 0, 4), g3(cs2, 0, 4)))
            stepw(e.tensor_tensor(g3(cs3, 4, 4), g3(cs2, 4, 4), g3(cs2, 0, 4), ALU.add))
            e.wait_ge(sr, nvr)
            stepw(e.tensor_sub(excl[:], cs3[:], eq1[:]))
            e.wait_ge(sr, nvr)
            stepw(e.tensor_single_scalar(izt[:], excl[:], 0.5, ALU.is_lt))
            e.wait_ge(sr, nvr)
            stepw(e.tensor_mul(first[:], eq1[:], izt[:]))
            e.wait_ge(sr, nvr)
            stepw(e.scalar_tensor_tensor(maskd[:], first[:], -1e30, rexp[:],
                                         ALU.mult, ALU.add))
            e.wait_ge(sr, nvr)
            stepw(e.tensor_reduce(m2c[:], g3(maskd), AX.X, ALU.max))
            e.wait_ge(sr, nvr)
            stepw(e.tensor_tensor(g3(eq2), g3(maskd), bc16(m2c), ALU.is_ge))
            e.wait_ge(sr, nvr)
            stepw(e.tensor_copy(g3(cs1, 0, 1), g3(eq2, 0, 1)))
            stepw(e.tensor_tensor(g3(cs1, 1, 7), g3(eq2, 1, 7), g3(eq2, 0, 7), ALU.add))
            e.wait_ge(sr, nvr)
            stepw(e.tensor_copy(g3(cs2, 0, 2), g3(cs1, 0, 2)))
            stepw(e.tensor_tensor(g3(cs2, 2, 6), g3(cs1, 2, 6), g3(cs1, 0, 6), ALU.add))
            e.wait_ge(sr, nvr)
            stepw(e.tensor_copy(g3(cs3, 0, 4), g3(cs2, 0, 4)))
            stepw(e.tensor_tensor(g3(cs3, 4, 4), g3(cs2, 4, 4), g3(cs2, 0, 4), ALU.add))
            e.wait_ge(sr, nvr)
            stepw(e.tensor_sub(excl[:], cs3[:], eq2[:]))
            e.wait_ge(sr, nvr)
            stepw(e.tensor_single_scalar(izt[:], excl[:], 0.5, ALU.is_lt))
            e.wait_ge(sr, nvr)
            stepw(e.tensor_mul(second[:], eq2[:], izt[:]))
            e.wait_ge(sr, nvr)
            stepw(e.tensor_add(selt[:], first[:], second[:]))
            e.wait_ge(sr, nvr)
            stepw(e.tensor_mul(gma[:], selt[:], rexp[:]))
            e.wait_ge(sr, nvr)
            stepw(e.tensor_tensor(g3(gall), g3(gma), bc16(rrec), ALU.mult))
        @b.scalar
        def _(e):
            e.wait_ge(sr, pos_rsh[0])
            e.activation(rexp[:], rsh[:], AF.Exp).then_inc(sar, 1)
        @b.tensor
        def _(e):
            e.wait_ge(wrl, 16 * NC)
            for tt in range(NT):
                e.wait_ge(xrl[tt % 2], 128 * (tt // 2 + 1))
                for kc in range(NC):
                    mi = e.matmul(psO0[:, tt * NEXP:(tt + 1) * NEXP],
                                  x1Tr[:, (tt % 2) * 1024 + kc * 128:(tt % 2) * 1024 + (kc + 1) * 128],
                                  wrsb[:, kc * NEXP:(kc + 1) * NEXP],
                                  start=(kc == 0), stop=(kc == NC - 1))
                    if kc == NC - 1:
                        mi.then_inc(mr, 1)
        @b.gpsimd
        def _(e):
            for tt in range(NT):
                if tt >= 2:
                    e.wait_ge(mr, tt - 1)
                for kc in range(NC):
                    e.dma_start(x1Tr[:, (tt % 2) * 1024 + kc * 128:(tt % 2) * 1024 + (kc + 1) * 128],
                                xn1T_i[kc * 128:(kc + 1) * 128, tt * 128:(tt + 1) * 128]
                                ).then_inc(xrl[tt % 2], 16)

    pst.close()
    kfb = _sb(st, nc, pre + "kfb", [128, 2 * 5 * D], BF16)
    vfb = _sb(st, nc, pre + "vfb", [128, 2 * 5 * D], BF16)
    vnear = _sb(st, nc, pre + "vnear", [128, 2 * 2 * D], BF16)
    qts = _sb(st, nc, pre + "qts", [128, 2 * NC * 128], BF16)
    kts = _sb(st, nc, pre + "kts", [128, 3 * NC * 128], BF16)
    x1s = _sb(st, nc, pre + "x1s", [128, 2 * NC * 512], BF16)
    mesb = _sb(st, nc, pre + "mesb", [128, NC * D], BF16)
    ymoe = _sb(st, nc, pre + "ymoe", [128, NT * D], BF16)
    ymt = _sb(st, nc, pre + "ymt", [128, 2 * 512], BF16)

    def emit_outproj(e, j):
        e.wait_ge(sv, VB(j + 1))
        e.wait_ge(sa, min(16 * (j + 2), 16 * NT))
        if j == 0:
            e.wait_ge(sr, 1)
        for kc in range(NC):
            na = j * NC + kc
            if na >= 2:
                e.wait_ge(sva, na - 1)
            ps = (psN0, psN1)[kc % 2].bitcast(BF16)
            e.transpose(ps[:, 0:128],
                        attn[:, (j % 2) * D + kc * 128:(j % 2) * D + (kc + 1) * 128],
                        identb[:]).then_inc(atp, 1)
        e.wait_ge(sva, (j + 1) * NC)
        for half in range(2):
            pso = (psO0, psO1)[half]
            if j >= 1:
                e.wait_ge(svy, 2 * (j - 1) + half + 1)
            for kc in range(NC):
                e.matmul(pso[:], attnT[:, kc * 128:(kc + 1) * 128],
                         woTsb[:, kc * D + half * 512:kc * D + (half + 1) * 512],
                         start=(kc == 0), stop=False)
            e.matmul(pso[:], onesr[:],
                     borow[0:1, half * 512:(half + 1) * 512],
                     start=False, stop=True).then_inc(mo, 1)

    def emit_outproj_evac(e, j):
        for kc in range(NC):
            na = j * NC + kc
            e.wait_ge(atp, na + 1)
            ps = (psN0, psN1)[kc % 2].bitcast(BF16)
            e.tensor_copy(attnT[:, kc * 128:(kc + 1) * 128], ps[:, 0:128]).then_inc(sva, 1)
        for half in range(2):
            e.wait_ge(mo, 2 * j + half + 1)
            if j >= 1:
                e.wait_ge(syo[half], 16 * j)
            pso = (psO0, psO1)[half]
            e.tensor_copy(ybuf[:, half * 512:(half + 1) * 512], pso[:]).then_inc(svy, 1)

    with nc.Block() as b:
        @b.gpsimd
        def _(e):
            for t in range(NT):
                par = t % 2
                if t == 0:
                    e.wait_ge(mvs, SEQ // 128)
                    e.wait_ge(wot, 72)
                    e.wait_ge(mr, NT)
                    if ag2 is not None:
                        e.wait_ge(wcs, 32)
                if t >= 2:
                    e.wait_ge(sv, VB(t - 1))
                ksrc = lambda widx: kwin_i[widx * 128:(widx + 1) * 128, :]
                vsrc = lambda widx: vwin_i[widx * 128:(widx + 1) * 128, :]
                for i5, off in enumerate(OFFT):
                    widx = 32 + t - off
                    e.dma_start(kfb[:, (par * 5 + i5) * D:(par * 5 + i5 + 1) * D],
                                ksrc(widx)).then_inc(klw[par], 16)
                    e.dma_start(vfb[:, (par * 5 + i5) * D:(par * 5 + i5 + 1) * D],
                                vsrc(widx)).then_inc(klw[par], 16)
                for j, widx in enumerate((32 + t, 31 + t)):
                    e.dma_start(vnear[:, (par * 2 + j) * D:(par * 2 + j + 1) * D],
                                vsrc(widx)).then_inc(vlw[par], 16)
                e.dma_start(qtok[:, par * D:(par + 1) * D],
                            qt_i[t * 128:(t + 1) * 128, :]).then_inc(vlw[par], 16)
                for pr in range(NP):
                    e.dma_start(qts[:, par * 1024 + pr * 128:par * 1024 + (pr + 1) * 128],
                                qTf_i[pr * 128:(pr + 1) * 128, t * 128:(t + 1) * 128]
                                ).then_inc(vlw[par], 16)
                for pr in range(NP):
                    e.dma_start(kts[:, (t % 3) * 1024 + pr * 128:(t % 3) * 1024 + (pr + 1) * 128],
                                kTf_i[pr * 128:(pr + 1) * 128, t * 128:(t + 1) * 128]
                                ).then_inc(ktl[t % 3], 16)
        @b.sync
        def _(e):
            if ag3 is not None:
                e.wait_ge(ag3[0], ag3[1])
            e.wait_ge(mvs, SEQ // 128)
            e.wait_ge(wot, 72)
            e.wait_ge(mr, NT)
            for fi in range(NC):
                e.dma_start(mesb[:, fi * D:(fi + 1) * D],
                            me_src[fi * 128:(fi + 1) * 128, :]).then_inc(wle, 16)
            for c in (0, 1):
                tc = c % 4
                for kc in range(NC):
                    e.dma_start(x1s[:, (c % 2) * 4096 + kc * 512:(c % 2) * 4096 + (kc + 1) * 512],
                                xn1T_i[kc * 128:(kc + 1) * 128, tc * 512:(tc + 1) * 512]
                                ).then_inc(xsl[c % 2], 16)
            for t in range(NT):
                if t >= 3:
                    jj = t - 3
                    e.wait_ge(svy, 2 * jj + 2)
                    for half in range(2):
                        e.dma_start(y1_io[jj * 128:(jj + 1) * 128, half * 512:(half + 1) * 512],
                                    ybuf[:, half * 512:(half + 1) * 512]).then_inc(syo[half], 16)
                if t % 2 == 1:
                    ee_next = (t + 1) // 2
                    if ee_next <= NEXP - 1:
                        e.wait_ge(mby, 32 * ee_next)
                        for fi in range(NC):
                            e.dma_start(mesb[:, fi * D:(fi + 1) * D],
                                        me_src[ee_next * D + fi * 128:ee_next * D + (fi + 1) * 128, :]
                                        ).then_inc(wle, 16)
                for j2 in (0, 1):
                    c = 2 * t + 2 + j2
                    if c < 32:
                        e.wait_ge(mby, (c - 1) * 8)
                        tc = c % 4
                        for kc in range(NC):
                            e.dma_start(x1s[:, (c % 2) * 4096 + kc * 512:(c % 2) * 4096 + (kc + 1) * 512],
                                        xn1T_i[kc * 128:(kc + 1) * 128, tc * 512:(tc + 1) * 512]
                                        ).then_inc(xsl[c % 2], 16)
            for jj in (NT - 3, NT - 2, NT - 1):
                e.wait_ge(svy, 2 * jj + 2)
                for half in range(2):
                    e.dma_start(y1_io[jj * 128:(jj + 1) * 128, half * 512:(half + 1) * 512],
                                ybuf[:, half * 512:(half + 1) * 512]).then_inc(syo[half], 16)
        pos_qk = {}
        pos_num = {}
        pos_z = {}

        @b.tensor
        def _(e):
            e.wait_ge(vss, 3)
            mmc = 0

            def minc(inst, book, n):
                nonlocal mmc
                inst.then_inc(mm, 1)
                mmc += 1
                book[n] = mmc

            def emit_qk(n):
                t, h = n // 16, n % 16
                par = t % 2
                hp, hh = h // 2, h % 2
                psN = (psN0, psN1)[n % 2]
                if n >= 2:
                    e.wait_ge(sa, n - 1)
                qmv = qts[hh * 64:(hh + 1) * 64, par * 1024 + hp * 128:par * 1024 + (hp + 1) * 128]
                e.matmul(psN[:, 0:128],
                         kts[hh * 64:(hh + 1) * 64,
                             (t % 3) * 1024 + hp * 128:(t % 3) * 1024 + (hp + 1) * 128],
                         qmv, start=True, stop=False)
                e.matmul(psN[:, 0:128], identb[:], masks[:, 512:640],
                         start=False, stop=True)
                if t == 0:
                    lhs_prev = kprevT[hh * 64:(hh + 1) * 64, hp * 128:(hp + 1) * 128]
                else:
                    lhs_prev = kts[hh * 64:(hh + 1) * 64,
                                   ((t - 1) % 3) * 1024 + hp * 128:((t - 1) % 3) * 1024 + (hp + 1) * 128]
                e.matmul(psN[:, 128:256], lhs_prev, qmv, start=True, stop=False)
                minc(e.matmul(psN[:, 128:256], identb[:], masks[:, 640:768],
                              start=False, stop=True), pos_qk, n)

            def emit_numz(n):
                t, h = n // 16, n % 16
                par = t % 2
                psU = (psU0, psU1)[n % 2]
                e.wait_ge(sa, n + 1)
                if n >= 2:
                    tp2, hp2 = (n - 2) // 16, (n - 2) % 16
                    e.wait_ge(sv, VB(tp2) + 2 * hp2 + 2)
                wpar = n % 2
                vtt = vnear[:, (par * 2 + 0) * D + h * DH:(par * 2 + 0) * D + (h + 1) * DH]
                vtp_ = vnear[:, (par * 2 + 1) * D + h * DH:(par * 2 + 1) * D + (h + 1) * DH]
                wm0 = wtile[:, wpar * 256:wpar * 256 + 128]
                wm1 = wtile[:, wpar * 256 + 128:wpar * 256 + 256]
                e.matmul(psU[:, 0:DH], wm0, vtt, start=True, stop=False)
                e.matmul(psU[:, 0:DH], wm1, vtp_, start=False, stop=False)
                e.matmul(psU[:, 0:DH], masks[:, 256:384], vtt, start=False, stop=False)
                e.matmul(psU[:, 0:DH], masks[:, 384:512], vtp_, start=False, stop=False)
                minc(e.matmul(psU[:, 0:DH], onesr[:], vsb[0:1, h * DH:(h + 1) * DH],
                              start=False, stop=True), pos_num, n)

            for t in range(NT):
                par = t % 2
                if t >= 2:
                    emit_outproj(e, t - 2)
                e.wait_ge(vlw[par], 176 * (t // 2 + 1))
                e.wait_ge(ktl[t % 3], 128 * (t // 3 + 1))
                for h in range(H):
                    n = t * 16 + h
                    if t >= 2 and h == 0:
                        e.wait_ge(sva, NC * (t - 1))
                    if n == 0:
                        e.wait_ge(ol, 3)
                    emit_qk(n)
                    if h >= 1:
                        emit_numz(n - 1)
                emit_numz(t * 16 + 15)
                for j2 in (0, 1):
                    c = 2 * t + j2
                    ee, tc = c // 4, c % 4
                    e.wait_ge(wle, 128 * (ee + 1))
                    e.wait_ge(xsl[c % 2], 128 * (c // 2 + 1))
                    if c == 0:
                        e.wait_ge(vss, 3)
                    for tsub in range(4):
                        g = c * 4 + tsub
                        for half in range(2):
                            ps = (psY0, psY1)[half]
                            if g >= 1:
                                e.wait_ge(svf, (g - 1) * 4 + 2 * half + 1)
                            for kc in range(NC):
                                mi = e.matmul(ps[:],
                                              x1s[:, (c % 2) * 4096 + kc * 512 + tsub * 128:
                                                  (c % 2) * 4096 + kc * 512 + tsub * 128 + 128],
                                              mesb[:, kc * D + half * 512:kc * D + (half + 1) * 512],
                                              start=(kc == 0), stop=(kc == NC - 1))
                                if kc == NC - 1:
                                    mi.then_inc(mby, 1)
            emit_outproj(e, NT - 2)
            emit_outproj(e, NT - 1)
        @b.scalar
        def _(e):
            PH = 2 * H
            for t in range(NT):
                for h in range(H):
                    n = t * 16 + h
                    psN = (psN0, psN1)[n % 2]
                    e.wait_ge(mm, pos_qk[n])
                    if n >= 2:
                        tp2, hp2 = (n - 2) // 16, (n - 2) % 16
                        e.wait_ge(sv, VB(tp2) + 2 * hp2 + 2)
                    e.activation(wtile[:, (n % 2) * 256:(n % 2 + 1) * 256], psN[:, 0:256],
                                 AF.Exp, scale=ISC,
                                 accum_out=zcol[:, n % 2:n % 2 + 1]).then_inc(sa, 1)
                e.wait_ge(sv, VB(t) + PH + 10)
                if t >= 2:
                    e.wait_ge(sv, VB(t - 2) + PH + 11)
                e.activation(efar[:], sfar[:], AF.Exp, scale=ISC).then_inc(sfx, 1)
        @b.vector
        def _(e):
            for t in range(NT):
                par = t % 2
                if t >= 2:
                    emit_outproj_evac(e, t - 2)
                nv = VB(t)
                def stepw(inst):
                    nonlocal nv
                    inst.then_inc(sv, 1)
                    nv += 1
                for h in range(H):
                    n = t * 16 + h
                    e.wait_ge(mm, pos_num[n])
                    stepw(e.tensor_copy(uall[:, par * D + h * DH:par * D + (h + 1) * DH],
                                        (psU0, psU1)[n % 2][:, 0:DH]))
                    e.wait_ge(sa, n + 1)
                    stepw(e.tensor_copy(zall[:, par * 16 + h:par * 16 + h + 1],
                                        zcol[:, n % 2:n % 2 + 1]))
                e.wait_ge(klw[par], 160 * (t // 2 + 1))
                e.wait_ge(vlw[par], 176 * (t // 2 + 1))
                for i5 in range(5):
                    kslice = kfb[:, (par * 5 + i5) * D:(par * 5 + i5 + 1) * D]
                    e.wait_ge(sv, nv)
                    stepw(e.tensor_tensor(pfull[:], qtok[:, par * D:(par + 1) * D], kslice,
                                          ALU.mult))
                    e.wait_ge(sv, nv)
                    stepw(e.tensor_reduce(
                        bass.AP(sfar[:].tensor, sfar[:].offset + i5,
                                [[80, 128], [5, 16]]),
                        pfull[:].rearrange("p (hh d) -> p hh d", d=DH),
                        AX.X, ALU.add))
                e.wait_ge(sfx, t + 1)
                stepw(e.tensor_single_scalar(wfar[:], efar[:], 1.0, ALU.subtract))
                for i5 in range(5):
                    vslice = vfb[:, (par * 5 + i5) * D:(par * 5 + i5 + 1) * D]
                    wb = bass.AP(wfar[:].tensor, wfar[:].offset + i5,
                                 [[80, 128], [5, 16], [0, DH]])
                    e.wait_ge(sv, nv)
                    stepw(e.tensor_tensor(ftmp[:].rearrange("p (hh d) -> p hh d", d=DH),
                                          vslice.rearrange("p (hh d) -> p hh d", d=DH),
                                          wb, ALU.mult))
                    e.wait_ge(sv, nv)
                    stepw(e.tensor_tensor(uall[:, par * D:(par + 1) * D],
                                          uall[:, par * D:(par + 1) * D],
                                          ftmp[:], ALU.add))
                e.wait_ge(sv, nv)
                stepw(e.tensor_reduce(zf16[:], wfar[:].rearrange("p (hh d5) -> p hh d5", d5=5),
                                      AX.X, ALU.add))
                e.wait_ge(sv, nv)
                stepw(e.scalar_tensor_tensor(zz16[:], zf16[:], float(SEQ - 9),
                                             zall[:, par * 16:(par + 1) * 16],
                                             ALU.add, ALU.add))
                e.wait_ge(sv, nv)
                stepw(e.reciprocal(rz16[:], zz16[:]))
                rzb = rz16[:].unsqueeze(2).broadcast_to((128, 16, DH))
                e.wait_ge(sv, nv)
                if t >= 2:
                    e.wait_ge(atp, NC * (t - 1))
                stepw(e.tensor_tensor(attn[:, par * D:(par + 1) * D]
                                      .rearrange("p (hh d) -> p hh d", d=DH),
                                      uall[:, par * D:(par + 1) * D]
                                      .rearrange("p (hh d) -> p hh d", d=DH),
                                      rzb, ALU.mult))
                assert nv == VB(t) + VE2, (nv, VB(t), VE2)
                for j2 in (0, 1):
                    c = 2 * t + j2
                    ee, tc = c // 4, c % 4
                    for tsub in range(4):
                        g = c * 4 + tsub
                        tt = tc * 4 + tsub
                        for half in range(2):
                            e.wait_ge(mby, c * 8 + tsub * 2 + half + 1)
                            if g >= 1:
                                e.wait_ge(svf, (g - 1) * 4 + 2 * half + 2)
                            gcol = gall[:, tt * NEXP + ee:tt * NEXP + ee + 1]
                            e.tensor_scalar(ymt[:, half * 512:(half + 1) * 512],
                                            (psY0, psY1)[half][:], gcol, None,
                                            ALU.mult).then_inc(svf, 1)
                            e.wait_ge(svf, g * 4 + 2 * half + 1)
                            dst = ymoe[:, tt * D + half * 512:tt * D + (half + 1) * 512]
                            if ee == 0:
                                e.tensor_copy(dst, ymt[:, half * 512:(half + 1) * 512]
                                              ).then_inc(svf, 1)
                            else:
                                e.tensor_tensor(dst, dst, ymt[:, half * 512:(half + 1) * 512],
                                                ALU.add).then_inc(svf, 1)
            emit_outproj_evac(e, NT - 2)
            emit_outproj_evac(e, NT - 1)

    kbc = kfb.bitcast(F32)
    vbc = vfb.bitcast(F32)
    with nc.Block() as b:
        @b.gpsimd
        def _(e):
            e.wait_ge(sv, SVH)
            e.wait_ge(svf, 32 * 16)
            e.wait_ge(syo[0], 16 * NT)
            e.wait_ge(syo[1], 16 * NT)
            for tt in range(NT + 1):
                par = tt % 2
                if tt < NT:
                    if tt >= 2:
                        e.wait_ge(svt, 3 * (tt - 2) + 1)
                    e.dma_start(kbc[:, par * D:(par + 1) * D],
                                x1_i[tt * 128:(tt + 1) * 128, :]).then_inc(rlx[par], 16)
                    e.dma_start(kbc[:, (2 + par) * D:(3 + par) * D],
                                y1_io[tt * 128:(tt + 1) * 128, :]).then_inc(rlx[par], 16)
                if tt >= 1:
                    jj = tt - 1
                    e.wait_ge(svt, 3 * jj + 3)
                    e.dma_start(out_o[jj * 128:(jj + 1) * 128, :],
                                vbc[:, (jj % 2) * D:(jj % 2 + 1) * D]).then_inc(sxo[jj % 2], 16)
            for i in range(2):
                e.wait_ge(sxo[i], 16 * 8)
        @b.vector
        def _(e):
            for tt in range(NT):
                par = tt % 2
                e.wait_ge(rlx[par], 32 * (tt // 2 + 1))
                if tt >= 2:
                    e.wait_ge(sxo[par], 16 * (tt // 2))
                if tt >= 1:
                    e.wait_ge(svt, 3 * tt)
                e.tensor_tensor(kbc[:, 4 * D:5 * D], kbc[:, par * D:(par + 1) * D],
                                kbc[:, (2 + par) * D:(3 + par) * D], ALU.add).then_inc(svt, 1)
                e.tensor_copy(vbc[:, 2 * D:3 * D], ymoe[:, tt * D:(tt + 1) * D]).then_inc(svt, 1)
                e.wait_ge(svt, 3 * tt + 2)
                e.tensor_tensor(vbc[:, par * D:(par + 1) * D], kbc[:, 4 * D:5 * D],
                                vbc[:, 2 * D:3 * D], ALU.add).then_inc(svt, 1)
    return


def build_launchEF():
    nc = bass.Bass("TRN2", target_bir_lowering=False)
    io = {}
    io["qTf_i"] = nc.dram_tensor("qTf_i", [D, T], BF16, kind="ExternalInput").ap()
    io["kTf_i"] = nc.dram_tensor("kTf_i", [D, T], BF16, kind="ExternalInput").ap()
    io["qt_i"] = nc.dram_tensor("qt_i", [T, D], BF16, kind="ExternalInput").ap()
    io["kwin_i"] = nc.dram_tensor("kwin_i", [3 * T, D], BF16, kind="ExternalInput").ap()
    io["vwin_i"] = nc.dram_tensor("vwin_i", [3 * T, D], BF16, kind="ExternalInput").ap()
    io["vfull_i"] = nc.dram_tensor("vfull_i", [SEQ, D], BF16, kind="ExternalInput").ap()
    io["Wo_i"] = nc.dram_tensor("Wo_i", [D, D], F32, kind="ExternalInput").ap()
    io["bo_i"] = nc.dram_tensor("bo_i", [1, D], F32, kind="ExternalInput").ap()
    io["mtt_i"] = nc.dram_tensor("mtt_i", [128, 128], BF16, kind="ExternalInput").ap()
    io["mtp_i"] = nc.dram_tensor("mtp_i", [128, 128], BF16, kind="ExternalInput").ap()
    io["ntt_i"] = nc.dram_tensor("ntt_i", [128, 128], BF16, kind="ExternalInput").ap()
    io["ntp_i"] = nc.dram_tensor("ntp_i", [128, 128], BF16, kind="ExternalInput").ap()
    io["idb"] = nc.dram_tensor("idb", [128, 128], BF16, kind="ExternalInput").ap()
    io["x1_i"] = nc.dram_tensor("x1_i", [T, D], F32, kind="ExternalInput").ap()
    io["xn1T_i"] = nc.dram_tensor("xn1T_i", [D, T], BF16, kind="ExternalInput").ap()
    io["Wr_i"] = nc.dram_tensor("Wr_i", [D, NEXP], F32, kind="ExternalInput").ap()
    meg = nc.dram_tensor("meg_i", [NEXP * D, D], BF16, kind="ExternalInput").ap()
    io["y1_io"] = nc.dram_tensor("y1_b", [T, D], F32, kind="Internal").ap()
    io["out_o"] = nc.dram_tensor("out_o", [T, D], F32, kind="ExternalOutput").ap()
    st = ExitStack()
    emit_segEF(nc, "EF_", io, st, me_src=meg)
    st.close()
    return nc


def _selectors():
    sel = np.zeros((NEXP, NEXP, 128), np.float32)
    for ee in range(NEXP):
        sel[ee, ee, :] = 1.0
    return sel.astype(BFDT)


def _launchF(inputs, resC, resD, resE):
    ident_bf = np.eye(128, dtype=np.float32).astype(BFDT)
    ncF = build_launchF()
    in_maps = []
    for c in range(8):
        in_maps.append({
            "x1_i": np.asarray(resC.results[c]["x1_o"]),
            "y1_i": np.asarray(resE.results[c]["y1_o"]),
            "xn1T_i": np.asarray(resD.results[c]["xn1T_o"]),
            "Wr_i": np.asarray(inputs["L1_ffn_Wr"], np.float32),
            "Wi_i": np.asarray(inputs["L1_ffn_Wi"], np.float32),
            "Wo_i": np.asarray(inputs["L1_ffn_Wo"], np.float32),
            "sel_i": _selectors(),
            "idb": ident_bf,
        })
    return run_bass_kernel_spmd(ncF, in_maps, core_ids=list(range(8)))



_A_CACHE = {}


def _launchA(inputs):
    x = np.asarray(inputs["L0_attn_x"], np.float32)
    xs = _shard_rows(x)
    cos128, sin128s = _rope_tables()
    ident_bf = np.eye(128, dtype=np.float32).astype(BFDT)
    ncA = build_launchA()
    in_maps = []
    for c in range(8):
        r = c % 4
        in_maps.append({
            "xin": xs[c],
            "wqkv": np.asarray(inputs["L0_attn_w_qkv"], np.float32),
            "gam": np.asarray(inputs["L0_attn_ln_gamma"], np.float32).reshape(1, D),
            "bet": np.asarray(inputs["L0_attn_ln_beta"], np.float32).reshape(1, D),
            "cosd": np.ascontiguousarray(cos128[:, r * T:(r + 1) * T]),
            "sind": np.ascontiguousarray(sin128s[:, r * T:(r + 1) * T]),
            "idb": ident_bf,
        })
    return run_bass_kernel_spmd(ncA, in_maps, core_ids=list(range(8)))





def emit_segMe(nc, pre, io, st):
    Wi_i, Wo_i, idb, me_o = io["WiM_i"], io["WoM_i"], io["idb"], io["me_o"]
    wisb = _sb(st, nc, pre + "wisb", [128, NC * D], BF16)
    wesb = _sb(st, nc, pre + "wesb", [128, NC * D], BF16)
    wiT = _sb(st, nc, pre + "wiT", [128, NC * D], BF16)
    mesb = _sb(st, nc, pre + "mesb", [128, NC * D], BF16)
    identb = _sb(st, nc, pre + "identb", [128, 128], BF16)
    psT0 = _ps(st, nc, pre + "psT0", [128, 128], BF16)
    psT1 = _ps(st, nc, pre + "psT1", [128, 128], BF16)
    psM0 = _ps(st, nc, pre + "psM0", [128, 512], F32)
    psM1 = _ps(st, nc, pre + "psM1", [128, 512], F32)
    il = nc.alloc_semaphore(pre + "il")
    wle = nc.alloc_semaphore(pre + "wle")
    wit = nc.alloc_semaphore(pre + "wit")
    mmm = nc.alloc_semaphore(pre + "mmm")
    sv = nc.alloc_semaphore(pre + "sv")
    smo = nc.alloc_semaphore(pre + "smo")
    with nc.Block() as b:
        @b.gpsimd
        def _(e):
            e.dma_start(identb[:], idb[:]).then_inc(il, 16)
            pid = e.partition_id()
            off = pid * (D * D)
            for kc in range(NC):
                swi = bass.AP(Wi_i.tensor, off + kc * 128 * D, [[D, 128], [1, D]])
                swo = bass.AP(Wo_i.tensor, off + kc * 128 * D, [[D, 128], [1, D]])
                e.dma_start(wisb[:, kc * D:(kc + 1) * D], swi).then_inc(wle, 16)
                e.dma_start(wesb[:, kc * D:(kc + 1) * D], swo).then_inc(wle, 16)
            e.wait_ge(sv, 80)
            for fi in range(NC):
                e.dma_start(me_o[fi * 128:(fi + 1) * 128, :],
                            mesb[:, fi * D:(fi + 1) * D]).then_inc(smo, 16)
            e.wait_ge(smo, 16 * NC)
        @b.tensor
        def _(e):
            e.wait_ge(il, 16)
            e.wait_ge(wle, 16 * 2 * NC)
            for n in range(64):
                oc, ic = n // NC, n % NC
                if n >= 2:
                    e.wait_ge(sv, n - 1)
                ps = (psT0, psT1)[n % 2]
                e.transpose(ps[:], wisb[:, oc * D + ic * 128:oc * D + (ic + 1) * 128],
                            identb[:]).then_inc(wit, 1)
            e.wait_ge(sv, 64)
            for j in range(16):
                fi, half = j // 2, j % 2
                if j >= 2:
                    e.wait_ge(sv, 64 + j - 1)
                ps = (psM0, psM1)[j % 2]
                for hc in range(NC):
                    mi = e.matmul(ps[:], wiT[:, hc * D + fi * 128:hc * D + (fi + 1) * 128],
                                  wesb[:, hc * D + half * 512:hc * D + (half + 1) * 512],
                                  start=(hc == 0), stop=(hc == NC - 1))
                    if hc == NC - 1:
                        mi.then_inc(mmm, 1)
        @b.vector
        def _(e):
            for n in range(64):
                e.wait_ge(wit, n + 1)
                ps = (psT0, psT1)[n % 2]
                oc, ic = n // NC, n % NC
                e.tensor_copy(wiT[:, ic * D + oc * 128:ic * D + (oc + 1) * 128],
                              ps[:]).then_inc(sv, 1)
            for j in range(16):
                fi, half = j // 2, j % 2
                e.wait_ge(mmm, j + 1)
                ps = (psM0, psM1)[j % 2]
                e.tensor_copy(mesb[:, fi * D + half * 512:fi * D + (half + 1) * 512],
                              ps[:]).then_inc(sv, 1)
    return



def build_fused():
    nc = bass.Bass("TRN2", target_bir_lowering=False)
    EI = lambda n, s, d: nc.dram_tensor(n, s, d, kind="ExternalInput").ap()
    IN = lambda n, s, d: nc.dram_tensor(n, s, d, kind="Internal").ap()
    xin = EI("xin", [T, D], F32)
    wqkv = EI("wqkv", [D, 3 * D], F32)
    gam = EI("gam", [1, D], F32)
    bet = EI("bet", [1, D], F32)
    cosd = EI("cosd", [128, T], F32)
    sind = EI("sind", [128, T], F32)
    idb = EI("idb", [128, 128], BF16)
    wout = EI("wout", [D, D], F32)
    wtiles = EI("wtiles", [7, 128, FFI], F32)
    Wq_i = EI("Wq_i", [D, D], F32)
    Wk_i = EI("Wk_i", [D, D], F32)
    Wv_i = EI("Wv_i", [D, D], F32)
    bq_i = EI("bq_i", [1, D], F32)
    bk_i = EI("bk_i", [1, D], F32)
    bv_i = EI("bv_i", [1, D], F32)
    WoL1 = EI("WoL1", [D, D], F32)
    bo_i = EI("bo_i", [1, D], F32)
    mtt_i = EI("mtt_i", [128, 128], BF16)
    mtp_i = EI("mtp_i", [128, 128], BF16)
    ntt_i = EI("ntt_i", [128, 128], BF16)
    ntp_i = EI("ntp_i", [128, 128], BF16)
    Wr_i = EI("Wr_i", [D, NEXP], F32)
    WiM = EI("WiM", [NEXP, D, D], F32)
    WoM = EI("WoM", [NEXP, D, D], F32)
    sel_i = EI("sel_i", [NEXP, NEXP, 128], BF16)
    out_o = nc.dram_tensor("out_o", [T, D], F32, kind="ExternalOutput").ap()

    qT_r = IN("qT_r", [D, T], BF16)
    kT_r = IN("kT_r", [D, T], BF16)
    v_t = IN("v_t", [T, D], BF16)
    xn_t = IN("xn_t", [T, D], F32)
    kTg = IN("kTg", [4 * D, T], BF16)
    vg = IN("vg", [SEQ, D], BF16)
    yat = IN("yat", [T, D], F32)
    wq_c = IN("wq_c", [7, FFI, 128], BF16)
    wjobs = IN("wjobs", [56, FFI, 128], BF16)
    h_t = IN("h_t", [T, FFI], F32)
    x1 = IN("x1", [T, D], F32)
    qt = IN("qt", [T, D], BF16)
    kt = IN("kt", [T, D], BF16)
    vt = IN("vt", [T, D], BF16)
    qTf = IN("qTf", [D, T], BF16)
    kTf = IN("kTf", [D, T], BF16)
    xn1T = IN("xn1T", [D, T], BF16)
    ktp = IN("ktp", [2 * T + SEQ, D], BF16)
    vtp = IN("vtp", [2 * T + SEQ, D], BF16)
    y1 = IN("y1", [T, D], F32)
    me = IN("me", [D, D], BF16)
    meg = IN("meg", [NEXP * D, D], BF16)

    ALU_ = ALU

    def seg(fn, pre, io, **kw):
        snap = nc.snapshot_sems()
        st = ExitStack()
        fn(nc, pre, io, st, **kw)
        st.close()
        nc.all_engine_barrier()
        nc.clear_and_free_semaphores(nc.allocated_since(snap))

    def coll_issue(pre, pairs, groups):
        cc = nc.alloc_semaphore(pre + "cc")
        with nc.Block() as b:
            @b.gpsimd
            def _(e):
                for src_ap, dst_ap in pairs:
                    e.collective_compute("AllGather", ALU_.bypass,
                                         replica_groups=groups,
                                         ins=[src_ap.opt()], outs=[dst_ap.opt()]).then_inc(cc, 1)
        return cc, len(pairs)

    def coll_wait(tok):
        cc, n = tok
        with nc.Block() as b:
            @b.gpsimd
            def _(e):
                e.wait_ge(cc, n)
        nc.all_engine_barrier()

    grp4 = [[0, 1, 2, 3], [4, 5, 6, 7]]
    grp8 = [list(range(8))]

    zst = ExitStack()
    zt = _sb(zst, nc, "zpad", [128, D], BF16)
    zs = nc.alloc_semaphore("zps")
    zd = nc.alloc_semaphore("zpd")
    with nc.Block() as b:
        @b.vector
        def _(e):
            e.memset(zt[:], 0.0).then_inc(zs, 1)
        @b.gpsimd
        def _(e):
            e.wait_ge(zs, 1)
            for i in range(2 * T // 128):
                e.dma_start(ktp[i * 128:(i + 1) * 128, :], zt[:]).then_inc(zd, 16)
                e.dma_start(vtp[i * 128:(i + 1) * 128, :], zt[:]).then_inc(zd, 16)
            e.wait_ge(zd, 16 * 2 * (2 * T // 128))
    zst.close()
    nc.all_engine_barrier()
    nc.clear_and_free_semaphores([zs, zd])

    ag1k_cc = nc.alloc_semaphore("ag1kcc")

    def _ag1k_hook(sro):
        with nc.Block() as b:
            @b.gpsimd
            def _(e):
                for i in range(2):
                    e.wait_ge(sro[i], 16 * 32)
                e.collective_compute("AllGather", ALU_.bypass, replica_groups=grp4,
                                     ins=[kT_r.opt()], outs=[kTg.opt()]
                                     ).then_inc(ag1k_cc, 1)

    seg(emit_segA, "a_", {"xin": xin, "wqkv": wqkv, "gam": gam, "bet": bet,
                          "cosd": cosd, "sind": sind, "idb": idb,
                          "qT_o": qT_r, "kT_o": kT_r, "v_o": v_t, "xn_o": xn_t},
        ag1k_hook=_ag1k_hook)
    ag1v = coll_issue("ag1v", [(v_t, vg)], grp4)
    seg(emit_segC0, "c0_", {"wtiles": wtiles, "idb": idb, "wq_o": wq_c})
    agw = coll_issue("agw", [(wq_c, wjobs)], grp8)
    seg(emit_segMe, "me_", {"WiM_i": WiM, "WoM_i": WoM, "idb": idb, "me_o": me})
    ag3 = coll_issue("ag3", [(me, meg)], grp8)
    seg(emit_segB, "b_", {"qT_i": qT_r, "kT_i": kTg, "v_i": vg, "wout": wout,
                          "yat_o": yat}, gathered=True,
        ag1k=(ag1k_cc, 1), ag1v=ag1v)
    coll_wait(agw)
    seg(emit_segCa, "ca_", {"xn_i": xn_t, "wjobs": wjobs, "idb": idb, "h_o": h_t},
        from_jobs=True)
    seg(emit_segCb, "cb_", {"h_i": h_t, "wjobs": wjobs, "xin": xin, "yat_i": yat,
                            "idb": idb, "x1_o": x1}, from_jobs=True)
    snapD = nc.snapshot_sems()
    stD = ExitStack()
    ag2box = []

    def _ag2_hook():
        ag2box.append(coll_issue("ag2", [(kt, ktp[2 * T:2 * T + SEQ, :]),
                                         (vt, vtp[2 * T:2 * T + SEQ, :])], grp4))

    emit_segDd(nc, "d_", {"x1_i": x1, "Wq_i": Wq_i, "Wk_i": Wk_i, "Wv_i": Wv_i,
                          "bq_i": bq_i, "bk_i": bk_i, "bv_i": bv_i, "idb": idb,
                          "qt_o": qt, "kt_o": kt, "vt_o": vt, "qTf_o": qTf,
                          "kTf_o": kTf, "xn1T_o": xn1T}, stD, ag2_hook=_ag2_hook)
    stD.close()
    ag2 = ag2box[0]
    nc.all_engine_barrier()
    nc.clear_and_free_semaphores(nc.allocated_since(snapD)[:-1])
    kwin_l = IN("kwin_l", [3 * T, D], BF16)
    vwin_l = IN("vwin_l", [3 * T, D], BF16)
    seg(emit_segEF, "e_", {"qTf_i": qTf, "kTf_i": kTf, "qt_i": qt,
                           "kwin_i": kwin_l, "vwin_i": vwin_l,
                           "vfull_i": vtp[2 * T:2 * T + SEQ, :],
                           "ktp_i": ktp, "vtp_i": vtp,
                           "Wo_i": WoL1, "bo_i": bo_i, "mtt_i": mtt_i,
                           "mtp_i": mtp_i, "ntt_i": ntt_i, "ntp_i": ntp_i,
                           "idb": idb, "x1_i": x1, "y1_io": y1,
                           "xn1T_i": xn1T, "Wr_i": Wr_i, "out_o": out_o},
        me_src=meg, ag3=ag3, ag2=ag2)
    return nc


def _c0_jobs(inputs):
    wg = np.asarray(inputs["L0_ffn_w_gate"], np.float32)
    wu = np.asarray(inputs["L0_ffn_w_up"], np.float32)
    wd = np.asarray(inputs["L0_ffn_w_down"], np.float32)
    jobs = []
    for w in (wg, wu):
        for t in range(NFI):
            tile = np.zeros((128, FFI), np.float32)
            tile[:, 0:D] = w[t * 128:(t + 1) * 128, :]
            jobs.append(tile)
    for t in range(NC):
        jobs.append(np.ascontiguousarray(wd[t * 128:(t + 1) * 128, :]))
    while len(jobs) < 56:
        jobs.append(np.zeros((128, FFI), np.float32))
    return jobs


def kernel_fused(inputs):
    x = np.asarray(inputs["L0_attn_x"], np.float32)
    xs = _shard_rows(x)
    cos128, sin128s = _rope_tables()
    ident_bf = np.eye(128, dtype=np.float32).astype(BFDT)
    mtt, mtp = _l1_masks()
    jobs = _c0_jobs(inputs)
    nc = build_fused()
    in_maps = []
    for c in range(8):
        r = c % 4
        in_maps.append({
            "xin": xs[c],
            "wqkv": np.asarray(inputs["L0_attn_w_qkv"], np.float32),
            "gam": np.asarray(inputs["L0_attn_ln_gamma"], np.float32).reshape(1, D),
            "bet": np.asarray(inputs["L0_attn_ln_beta"], np.float32).reshape(1, D),
            "cosd": np.ascontiguousarray(cos128[:, r * T:(r + 1) * T]),
            "sind": np.ascontiguousarray(sin128s[:, r * T:(r + 1) * T]),
            "idb": ident_bf,
            "wout": np.asarray(inputs["L0_attn_w_out"], np.float32),
            "wtiles": np.stack(jobs[c * 7:(c + 1) * 7]),
            "Wq_i": np.asarray(inputs["L1_attn_Wq"], np.float32),
            "Wk_i": np.asarray(inputs["L1_attn_Wk"], np.float32),
            "Wv_i": np.asarray(inputs["L1_attn_Wv"], np.float32),
            "bq_i": np.asarray(inputs["L1_attn_bq"], np.float32).reshape(1, D),
            "bk_i": np.asarray(inputs["L1_attn_bk"], np.float32).reshape(1, D),
            "bv_i": np.asarray(inputs["L1_attn_bv"], np.float32).reshape(1, D),
            "WoL1": np.asarray(inputs["L1_attn_Wo"], np.float32),
            "bo_i": np.asarray(inputs["L1_attn_bo"], np.float32).reshape(1, D),
            "mtt_i": mtt.astype(BFDT),
            "mtp_i": mtp.astype(BFDT),
            "ntt_i": (-mtt).astype(BFDT),
            "ntp_i": (-mtp).astype(BFDT),
            "Wr_i": np.asarray(inputs["L1_ffn_Wr"], np.float32),
            "WiM": np.asarray(inputs["L1_ffn_Wi"], np.float32),
            "WoM": np.asarray(inputs["L1_ffn_Wo"], np.float32),
            "sel_i": _selectors(),
        })
    res = run_bass_kernel_spmd(nc, in_maps, core_ids=list(range(8)))
    out = np.empty((2, SEQ, D), np.float32)
    for c in range(8):
        b, r = c // 4, c % 4
        out[b, r * T:(r + 1) * T, :] = np.asarray(res.results[c]["out_o"], np.float32)
    return out


def _l1_masks():
    Dfull = {1, 2, 4, 8, 16, 32, 64, 128}
    mtt = np.zeros((128, 128), np.float32)
    mtp = np.zeros((128, 128), np.float32)
    for rk in range(128):
        for rq in range(128):
            d = rq - rk
            if d == 0 or (d > 0 and d in Dfull):
                mtt[rk, rq] = 1.0
            if (rq + 128 - rk) in Dfull:
                mtp[rk, rq] = 1.0
    return mtt, mtp


def _launchE(resC, resD, inputs):
    ident_bf = np.eye(128, dtype=np.float32).astype(BFDT)
    mtt, mtp = _l1_masks()
    kts = [np.asarray(resD.results[c]["kt_o"]) for c in range(8)]
    vts = [np.asarray(resD.results[c]["vt_o"]) for c in range(8)]
    ncE = build_launchE()
    in_maps = []
    for c in range(8):
        b, r = c // 4, c % 4
        grp = list(range(b * 4, b * 4 + 4))
        k_full = np.concatenate([kts[g] for g in grp], axis=0)
        v_full = np.concatenate([vts[g] for g in grp], axis=0)
        kwin = np.zeros((3 * T, D), BFDT)
        vwin = np.zeros((3 * T, D), BFDT)
        lo = r * T - 2 * T
        g0, g1 = max(0, lo), min(SEQ, lo + 3 * T)
        kwin[g0 - lo:g1 - lo] = k_full[g0:g1]
        vwin[g0 - lo:g1 - lo] = v_full[g0:g1]
        in_maps.append({
            "qTf_i": np.asarray(resD.results[c]["qTf_o"]),
            "kTf_i": np.asarray(resD.results[c]["kTf_o"]),
            "qt_i": np.asarray(resD.results[c]["qt_o"]),
            "kwin_i": kwin,
            "vwin_i": vwin,
            "vfull_i": np.ascontiguousarray(v_full),
            "Wo_i": np.asarray(inputs["L1_attn_Wo"], np.float32),
            "bo_i": np.asarray(inputs["L1_attn_bo"], np.float32).reshape(1, D),
            "mtt_i": mtt.astype(BFDT),
            "mtp_i": mtp.astype(BFDT),
            "ntt_i": (-mtt).astype(BFDT),
            "ntp_i": (-mtp).astype(BFDT),
            "idb": ident_bf,
        })
    return run_bass_kernel_spmd(ncE, in_maps, core_ids=list(range(8)))


FUSED = True


def kernel(**inputs):
    if FUSED:
        return kernel_fused(inputs)
    resA = _launchA(inputs)
    resB = _launchB(resA, inputs)
    wgT, wuT, wdT = _launchC0(inputs)
    resC, _ = _launchC(inputs, resA, resB, wgT, wuT, wdT)
    resD = _launchD(resC, inputs)
    resE = _launchE(resC, resD, inputs)
    resF = _launchF(inputs, resC, resD, resE)
    out = np.empty((2, SEQ, D), np.float32)
    for c in range(8):
        b, r = c // 4, c % 4
        out[b, r * T:(r + 1) * T, :] = np.asarray(resF.results[c]["out_o"], np.float32)
    return out
